# version 14
# baseline (speedup 1.0000x reference)
from contextlib import ExitStack

import numpy as np
import ml_dtypes

import concourse.bass as bass
import concourse.mybir as mybir
from concourse.bass_utils import run_bass_kernel_spmd

F32 = mybir.dt.float32
BF16 = mybir.dt.bfloat16
U8 = mybir.dt.uint8
U32 = mybir.dt.uint32
AF = mybir.ActivationFunctionType
ALU = mybir.AluOpType
AX = mybir.AxisListType

D = 4096
S = 2048
NH = 16
HD = 256
KD = D // 128
ACUT = 10
EPS = 1e-5
NEGV = -1e9
SCL = 1.0 / 16.0

BFNP = np.dtype(ml_dtypes.bfloat16)


def _split512(n):
    out = []
    while n > 0:
        c = min(n, 512)
        out.append(c)
        n -= c
    return out


class Sched:
    def __init__(self):
        self.mode = "count"
        self.cnt = {}
        self.ms = {}
        self.eng = None
        self.engname = None
        self.sems = None
        self.waited = None
        self.fence = {}

    def begin_count(self):
        self.mode = "count"
        self.cnt = {}
        self.ms = {}
        self.fence = {}

    def begin_emit(self, engname, eng, sems):
        self.mode = "emit"
        self.engname = engname
        self.eng = eng
        self.sems = sems
        self.waited = {}
        self.fence = {}

    def set_fence(self, keys):
        for en in ("sync", "gpsimd", "tensor", "vector", "scalar"):
            self.fence[en] = list(keys)

    def op(self, engname, fn, waits=(), inc=None, ms=None):
        pend = self.fence.pop(engname, None)
        if pend:
            waits = list(waits) + pend
        if self.mode == "count":
            for k in waits:
                assert k in self.ms, f"wait on undefined milestone {k}"
            if inc is not None:
                s, n = inc
                self.cnt[s] = self.cnt.get(s, 0) + n
                if ms is not None:
                    for k in (ms if isinstance(ms, list) else [ms]):
                        assert k not in self.ms, f"dup milestone {k}"
                        self.ms[k] = (s, self.cnt[s])
            else:
                assert ms is None, ms
            return
        if engname != self.engname:
            return
        for k in waits:
            s, v = self.ms[k]
            if v > self.waited.get(s, 0):
                self.eng.wait_ge(self.sems[s], v)
                self.waited[s] = v
        instr = fn(self.eng)
        if inc is not None:
            s, n = inc
            instr.then_inc(self.sems[s], n)

    def final_wait(self, engname, semname):
        if self.mode == "count" or engname != self.engname:
            return
        v = self.cnt.get(semname, 0)
        if v > self.waited.get(semname, 0):
            self.eng.wait_ge(self.sems[semname], v)
            self.waited[semname] = v


def build_nc():
    nc = bass.Bass("TRN2", target_bir_lowering=False)

    NQR = [ACUT, NH - ACUT]
    NKVR = [ACUT, 16]
    QB0R = [0, ACUT]
    NMX = max(NQR)

    xT_d = nc.dram_tensor("xT", [D, S], F32, kind="ExternalInput").ap()
    xres_d = nc.dram_tensor("xres", [NMX * 128, D], F32, kind="ExternalInput").ap()
    wq_d = nc.dram_tensor("wq", [D, D], F32, kind="ExternalInput").ap()
    wk_d = nc.dram_tensor("wk", [D, D], F32, kind="ExternalInput").ap()
    wv_d = nc.dram_tensor("wv", [D, D], F32, kind="ExternalInput").ap()
    wo_d = nc.dram_tensor("wo", [D, D], F32, kind="ExternalInput").ap()
    fw_d = nc.dram_tensor("fw", [D, D], U8, kind="ExternalInput").ap()
    fs_d = nc.dram_tensor("fs", [1, 1], F32, kind="ExternalInput").ap()
    role_d = nc.dram_tensor("rolein", [1, 1], U32, kind="ExternalInput").ap()
    tabq_d = nc.dram_tensor("tabq", [128, NMX * 128], BF16, kind="ExternalInput").ap()
    tabk_d = nc.dram_tensor("tabk", [128, S], BF16, kind="ExternalInput").ap()
    maska_d = nc.dram_tensor("maska", [128, 128], BF16, kind="ExternalInput").ap()
    ident_d = nc.dram_tensor("ident", [128, 128], BF16, kind="ExternalInput").ap()
    out_d = nc.dram_tensor("out", [NMX * 128, D], F32, kind="ExternalOutput").ap()

    qspill = nc.dram_tensor("qspill", [NH, HD, NMX * 128], BF16, kind="Internal").ap()
    kspill = nc.dram_tensor("kspill", [NH, HD, S], BF16, kind="Internal").ap()
    vspill = nc.dram_tensor("vspill", [NH, 128, 16 * HD], BF16, kind="Internal").ap()
    z_hbm = nc.dram_tensor("z_hbm", [NMX, 128, D], F32, kind="Internal").ap()
    x2_hbm = nc.dram_tensor("x2_hbm", [NMX, 128, D], F32, kind="Internal").ap()
    z2_hbm = nc.dram_tensor("z2_hbm", [NMX, 128, D], F32, kind="Internal").ap()

    es = ExitStack()
    A = es.enter_context(nc.sbuf_tensor("arenaA", [128, 65536], BF16))
    Bu = es.enter_context(nc.sbuf_tensor("arenaB", [128, 64512], U8))
    Bf = Bu.bitcast(F32)
    Bb = Bu.bitcast(BF16)
    identb = es.enter_context(nc.sbuf_tensor("identb", [128, 128], BF16))
    maska = es.enter_context(nc.sbuf_tensor("maska_s", [128, 128], BF16))
    tabqc = es.enter_context(nc.sbuf_tensor("tabqc", [64, NMX * 128], BF16))
    tabqs = es.enter_context(nc.sbuf_tensor("tabqs", [64, NMX * 128], BF16))
    tabkc = es.enter_context(nc.sbuf_tensor("tabkc", [64, S], BF16))
    tabks = es.enter_context(nc.sbuf_tensor("tabks", [64, S], BF16))
    stats = es.enter_context(nc.sbuf_tensor("stats", [128, 576], F32))
    ones1 = es.enter_context(nc.sbuf_tensor("ones1", [1, 128], F32))
    rolesb = es.enter_context(nc.sbuf_tensor("rolesb", [1, 1], U32))

    ST_BN = 0
    ST_MV = 480
    ST_RSTD = 500
    ST_NMR = 512
    ST_RS = 524
    ST_NB = 536
    ST_FS = 548
    ST_FSB = 549
    ST_FS2 = 550
    ST_NEGFS = 551
    ST_EPS = 552
    ST_RSP = 560

    PL = es.enter_context(nc.psum_tensor("PL", [128, 2048], F32))
    pb = [es.enter_context(nc.psum_tensor(f"pb{i}", [128, 512], F32)) for i in range(4)]
    trP = [p.bitcast(BF16) for p in pb[:2]]
    avP = [pb[2], pb[3]]

    def bank(i):
        if i < 4:
            return PL[:, i * 512:(i + 1) * 512]
        return pb[i - 4][:, :]

    E0, E1 = Sched(), Sched()

    def walk(role, E):
        NQ = NQR[role]
        NKV = NKVR[role]
        QB0 = QB0R[role]
        NM = NQ
        QW = NQ * 128
        KWA = NKV * 128
        T2 = NH * NQ

        QK = {"sync": 12, "gpsimd": 8}
        qc = {"sync": 0, "gpsimd": 0}

        def dma(q, out, in_, waits=(), ms=None):
            j = qc[q]
            qc[q] = j + 1
            sem = f"{q[0]}{j % QK[q]}"
            w = list(waits)
            if j >= QK[q]:
                w.append(("_dma", q, j - QK[q]))
            mss = [("_dma", q, j)] + ([ms] if ms is not None else [])
            E.op(q, lambda e, o=out, i=in_: e.dma_start(out=o, in_=i),
                 w, inc=(sem, 16), ms=mss)

        def op(eng, fn, waits=(), ms=None):
            semname = {"tensor": "t", "vector": "v", "scalar": "s", "gpsimd": "p"}[eng]
            E.op(eng, fn, waits, inc=(semname, 1) if ms is not None else None, ms=ms)

        def dma_tail(q):
            j1 = qc[q]
            return [("_dma", q, j) for j in range(max(0, j1 - QK[q]), j1)]

        bankfree = [None] * 8

        def bwait(i):
            return [bankfree[i]] if bankfree[i] is not None else []

        dma("sync", identb[:, :], ident_d, ms=("d_ident",))
        dma("sync", maska[:, :], maska_d, ms=("d_maska",))
        dma("sync", tabqc[:, :], tabq_d[0:64, :], ms=("tabq",))
        dma("sync", tabqs[:, :], tabq_d[64:128, :], ms=("tabqs",))
        dma("sync", tabkc[:, :], tabk_d[0:64, :], ms=("tabk",))
        dma("sync", tabks[:, :], tabk_d[64:128, :], ms=("tabks",))
        dma("sync", stats[0:1, ST_FS:ST_FS + 1], fs_d, ms=("d_fs",))
        op("vector", lambda e: e.memset(ones1[:, :], 1.0), ms=("ones",))
        op("vector", lambda e: e.memset(stats[:, ST_EPS:ST_EPS + 1], float(EPS)), ms=("epsv",))
        op("tensor", lambda e: e.matmul(pb[0][:, 0:1], ones1[0:1, :],
                                        stats[0:1, ST_FS:ST_FS + 1], start=True, stop=True),
           waits=[("ones",), ("d_fs",)], ms=("fsmm",))
        op("vector", lambda e: e.tensor_copy(stats[:, ST_FSB:ST_FSB + 1], pb[0][:, 0:1]),
           waits=[("fsmm",)], ms=("fsb",))
        op("vector", lambda e: e.tensor_scalar_mul(stats[:, ST_FS2:ST_FS2 + 1],
                                                   stats[:, ST_FSB:ST_FSB + 1], 2.0),
           waits=[("fsb",)], ms=("fs2",))
        op("vector", lambda e: e.tensor_scalar_mul(stats[:, ST_NEGFS:ST_NEGFS + 1],
                                                   stats[:, ST_FSB:ST_FSB + 1], -1.0),
           waits=[("fsb",)], ms=("negfs",))
        bankfree[4] = ("fsb",)

        XH = KWA // 2
        XSTb = 0
        WQSTb = XSTb + 3 * XH * 4
        WKSTb = WQSTb + 2 * 512 * 4
        WVSTb = WKSTb + 2 * 512 * 4
        WQBFb = WVSTb + 2 * 1024 * 4
        WKBFb = WQBFb + 4 * 512 * 2
        WVBFb = WKBFb + 4 * 512 * 2
        QSTGb = WVBFb + 4 * 1024 * 2
        KSTGb = QSTGb + 2 * QW * 2
        VSTGb = KSTGb + 2 * KWA * 2
        assert VSTGb + 4096 * 2 <= 64512, VSTGb
        XST = XSTb // 4
        WQST, WKST, WVST = WQSTb // 4, WKSTb // 4, WVSTb // 4
        WQBF, WKBF, WVBF = WQBFb // 2, WKBFb // 2, WVBFb // 2
        QSTG, KSTG, VSTG = QSTGb // 2, KSTGb // 2, VSTGb // 2

        qsplit = _split512(QW)
        ksplit = _split512(KWA)
        nqc, nkc = len(qsplit), len(ksplit)
        vpasses = []
        c0 = 0
        while c0 < NKV:
            vpasses.append((c0, min(c0 + 8, NKV)))
            c0 += 8
        NVP = len(vpasses)

        for kk in range(KD):
            for hf in range(2):
                idx = 2 * kk + hf
                s = idx % 3
                if idx >= 3:
                    pk, ph = divmod(idx - 3, 2)
                    w = [("xcast", pk, ph)]
                else:
                    w = []
                dma("gpsimd", Bf[:, XST + s * XH:XST + (s + 1) * XH],
                    xT_d[kk * 128:(kk + 1) * 128, hf * XH:(hf + 1) * XH],
                    waits=w, ms=("xdma", kk, hf))
                op("scalar",
                   lambda e, kk=kk, hf=hf, s=s: e.copy(
                       A[:, kk * KWA + hf * XH:kk * KWA + (hf + 1) * XH],
                       Bf[:, XST + s * XH:XST + (s + 1) * XH]),
                   waits=[("xdma", kk, hf)], ms=("xcast", kk, hf))

        for p in range(8):
            h0 = 2 * p
            for hh in range(2):
                h = h0 + hh
                for k2 in range(16):
                    b = k2 % 2
                    if k2 >= 2:
                        w = [("qcast", p, hh, k2 - 2)]
                    elif hh == 1:
                        w = [("qcast", p, 0, 14 + k2)]
                    elif p > 0:
                        w = [("qcast", p - 1, 1, 14 + k2)]
                    else:
                        w = []
                    dma("sync", Bf[:, WQST + b * 512:WQST + (b + 1) * 512].rearrange("p (c m) -> p c m", m=256),
                        wq_d[k2 * 256:(k2 + 1) * 256, h * HD:(h + 1) * HD].rearrange("(c p) m -> p c m", p=128),
                        waits=w, ms=("qwd", p, hh, k2))
                    b4 = k2 % 4
                    wc = [("qwd", p, hh, k2)]
                    if k2 >= 4:
                        wc.append(("qmm", p, hh, k2 - 4))
                    elif hh == 1:
                        wc.append(("qmm", p, 0, 12 + k2))
                    elif p > 0:
                        wc.append(("qmm", p - 1, 1, 12 + k2))
                    op("vector",
                       lambda e, b=b, b4=b4: e.tensor_copy(Bb[:, WQBF + b4 * 512:WQBF + (b4 + 1) * 512],
                                                           Bf[:, WQST + b * 512:WQST + (b + 1) * 512]),
                       waits=wc, ms=("qcast", p, hh, k2))
                    for sub in range(2):
                        kk = 2 * k2 + sub
                        coff = 0
                        for n, csz in enumerate(qsplit):
                            for m in range(2):
                                wmm = [("qcast", p, hh, k2), ("xcast", kk, 0), ("xcast", kk, 1)]
                                if kk == 0:
                                    wmm += bwait(m * nqc + n)
                                op("tensor",
                                   lambda e, b4=b4, sub=sub, m=m, n=n, kk=kk, coff=coff, csz=csz: e.matmul(
                                       bank(m * nqc + n)[:, 0:csz],
                                       Bb[:, WQBF + b4 * 512 + sub * 256 + m * 128:
                                          WQBF + b4 * 512 + sub * 256 + (m + 1) * 128],
                                       A[:, kk * KWA + QB0 * 128 + coff:kk * KWA + QB0 * 128 + coff + csz],
                                       start=(kk == 0), stop=(kk == KD - 1)),
                                   waits=wmm if (kk == 0 or (m == 0 and n == 0)) else (),
                                   ms=("qmm", p, hh, k2) if (m == 1 and n == nqc - 1 and sub == 1) else None)
                            coff += csz
                for m in range(2):
                    coff = 0
                    for n, csz in enumerate(qsplit):
                        w = [("qmm", p, hh, 15)]
                        if hh == 1:
                            w.append(("qspill", p, 0))
                        elif p > 0:
                            w.append(("qspill", p - 1, 1))
                        mskey = ("qevb", p, hh, m * nqc + n)
                        op("vector",
                           lambda e, m=m, coff=coff, csz=csz, n=n: e.tensor_scalar_mul(
                               Bb[:, QSTG + m * QW + coff:QSTG + m * QW + coff + csz],
                               bank(m * nqc + n)[:, 0:csz], SCL),
                           waits=w, ms=mskey)
                        bankfree[m * nqc + n] = mskey
                        coff += csz
                dma("gpsimd",
                    qspill[h][:, 0:QW].rearrange("(c p) t -> p c t", p=128),
                    Bb[:, QSTG:QSTG + 2 * QW].rearrange("p (c t) -> p c t", t=QW),
                    waits=[("qevb", p, hh, m * nqc + n) for m in range(2) for n in range(nqc)],
                    ms=("qspill", p, hh))

            for hh in range(2):
                h = h0 + hh
                for k2 in range(16):
                    b = k2 % 2
                    if k2 >= 2:
                        w = [("kcast", p, hh, k2 - 2)]
                    elif hh == 1:
                        w = [("kcast", p, 0, 14 + k2)]
                    elif p > 0:
                        w = [("kcast", p - 1, 1, 14 + k2)]
                    else:
                        w = []
                    dma("sync", Bf[:, WKST + b * 512:WKST + (b + 1) * 512].rearrange("p (c m) -> p c m", m=256),
                        wk_d[k2 * 256:(k2 + 1) * 256, h * HD:(h + 1) * HD].rearrange("(c p) m -> p c m", p=128),
                        waits=w, ms=("kwd", p, hh, k2))
                    b4 = k2 % 4
                    wc = [("kwd", p, hh, k2)]
                    if k2 >= 4:
                        wc.append(("kmm", p, hh, k2 - 4))
                    elif hh == 1:
                        wc.append(("kmm", p, 0, 12 + k2))
                    elif p > 0:
                        wc.append(("kmm", p - 1, 1, 12 + k2))
                    op("vector",
                       lambda e, b=b, b4=b4: e.tensor_copy(Bb[:, WKBF + b4 * 512:WKBF + (b4 + 1) * 512],
                                                           Bf[:, WKST + b * 512:WKST + (b + 1) * 512]),
                       waits=wc, ms=("kcast", p, hh, k2))
                    for sub in range(2):
                        kk = 2 * k2 + sub
                        coff = 0
                        for n, csz in enumerate(ksplit):
                            for m in range(2):
                                wmm = [("kcast", p, hh, k2), ("xcast", kk, 0), ("xcast", kk, 1)]
                                if kk == 0:
                                    wmm += bwait(m * nkc + n)
                                op("tensor",
                                   lambda e, b4=b4, sub=sub, m=m, n=n, kk=kk, coff=coff, csz=csz: e.matmul(
                                       bank(m * nkc + n)[:, 0:csz],
                                       Bb[:, WKBF + b4 * 512 + sub * 256 + m * 128:
                                          WKBF + b4 * 512 + sub * 256 + (m + 1) * 128],
                                       A[:, kk * KWA + coff:kk * KWA + coff + csz],
                                       start=(kk == 0), stop=(kk == KD - 1)),
                                   waits=wmm if (kk == 0 or (m == 0 and n == 0)) else (),
                                   ms=("kmm", p, hh, k2) if (m == 1 and n == nkc - 1 and sub == 1) else None)
                            coff += csz
                for m in range(2):
                    coff = 0
                    for n, csz in enumerate(ksplit):
                        w = [("kmm", p, hh, 15)]
                        if hh == 1:
                            w.append(("kspill", p, 0))
                        elif p > 0:
                            w.append(("kspill", p - 1, 1))
                        mskey = ("kevb", p, hh, m * nkc + n)
                        op("vector",
                           lambda e, m=m, coff=coff, csz=csz, n=n: e.tensor_copy(
                               Bb[:, KSTG + m * KWA + coff:KSTG + m * KWA + coff + csz],
                               bank(m * nkc + n)[:, 0:csz]),
                           waits=w, ms=mskey)
                        bankfree[m * nkc + n] = mskey
                        coff += csz
                dma("gpsimd",
                    kspill[h][:, 0:KWA].rearrange("(c p) t -> p c t", p=128),
                    Bb[:, KSTG:KSTG + 2 * KWA].rearrange("p (c t) -> p c t", t=KWA),
                    waits=[("kevb", p, hh, m * nkc + n) for m in range(2) for n in range(nkc)],
                    ms=("kspill", p, hh))

            for ps, (ca, cb) in enumerate(vpasses):
                npc = cb - ca
                for k2 in range(16):
                    b = k2 % 2
                    if k2 >= 2:
                        w = [("vcast", p, ps, k2 - 2)]
                    elif ps > 0:
                        w = [("vcast", p, ps - 1, 14 + k2)]
                    elif p > 0:
                        w = [("vcast", p - 1, NVP - 1, 14 + k2)]
                    else:
                        w = []
                    dma("sync", Bf[:, WVST + b * 1024:WVST + (b + 1) * 1024].rearrange("p (c m) -> p c m", m=512),
                        wv_d[k2 * 256:(k2 + 1) * 256, h0 * HD:h0 * HD + 512].rearrange("(c p) m -> p c m", p=128),
                        waits=w, ms=("vwd", p, ps, k2))
                    b4 = k2 % 4
                    wc = [("vwd", p, ps, k2)]
                    if k2 >= 4:
                        wc.append(("vmm", p, ps, k2 - 4))
                    elif ps > 0:
                        wc.append(("vmm", p, ps - 1, 12 + k2))
                    elif p > 0:
                        wc.append(("vmm", p - 1, NVP - 1, 12 + k2))
                    op("vector",
                       lambda e, b=b, b4=b4: e.tensor_copy(Bb[:, WVBF + b4 * 1024:WVBF + (b4 + 1) * 1024],
                                                           Bf[:, WVST + b * 1024:WVST + (b + 1) * 1024]),
                       waits=wc, ms=("vcast", p, ps, k2))
                    for sub in range(2):
                        kk = 2 * k2 + sub
                        for bidx in range(npc):
                            c = ca + bidx
                            wmm = [("vcast", p, ps, k2), ("xcast", kk, 0), ("xcast", kk, 1)]
                            if kk == 0:
                                wmm += bwait(bidx)
                            op("tensor",
                               lambda e, b4=b4, sub=sub, c=c, bidx=bidx, kk=kk: e.matmul(
                                   bank(bidx),
                                   A[:, kk * KWA + c * 128:kk * KWA + (c + 1) * 128],
                                   Bb[:, WVBF + b4 * 1024 + sub * 512:WVBF + b4 * 1024 + (sub + 1) * 512],
                                   start=(kk == 0), stop=(kk == KD - 1)),
                               waits=wmm if (kk == 0 or bidx == 0) else (),
                               ms=("vmm", p, ps, k2) if (bidx == npc - 1 and sub == 1) else None)
                for bidx in range(npc):
                    w = [("vmm", p, ps, 15)]
                    if ps > 0:
                        w += [("vspill", p, ps - 1, 0), ("vspill", p, ps - 1, 1)]
                    elif p > 0:
                        w += [("vspill", p - 1, NVP - 1, 0), ("vspill", p - 1, NVP - 1, 1)]
                    mskey = ("vevb", p, ps, bidx)
                    op("vector",
                       lambda e, bidx=bidx: e.tensor_copy(
                           Bb[:, VSTG + bidx * 512:VSTG + (bidx + 1) * 512], bank(bidx)),
                       waits=w, ms=mskey)
                    bankfree[bidx] = mskey
                for hh in range(2):
                    h = h0 + hh
                    dma("gpsimd",
                        vspill[h].rearrange("p (c d) -> p c d", d=HD)[:, ca:cb, :],
                        Bb[:, VSTG:VSTG + npc * 512].rearrange("p (c two) -> p c two", two=512)[:, :, hh * HD:(hh + 1) * HD],
                        waits=[("vevb", p, ps, i) for i in range(npc)],
                        ms=("vspill", p, ps, hh))

        p1done = ("vspill", 7, NVP - 1, 1)
        p1all = [("vspill", 7, NVP - 1, 0), ("vspill", 7, NVP - 1, 1)]
        npc_last = vpasses[-1][1] - vpasses[-1][0]
        E.set_fence([("vmm", 7, NVP - 1, 15), ("vevb", 7, NVP - 1, npc_last - 1),
                     ("xcast", 31, 0), ("xcast", 31, 1)] + dma_tail("sync") + dma_tail("gpsimd"))

        SETSZ = 2 * QW + 2 * KWA + NKV * 256
        SETA0 = 32 * QW
        ANW = (QB0 + NQ) * 128
        ANB = SETA0 + 2 * SETSZ
        assert ANB + 6 * ANW <= 65536, (ANB, ANW)
        SET3b = 0
        SWQb = SET3b + SETSZ * 2
        SWKb = SWQb + QW * 2
        SCQb = SWKb + KWA * 2
        SCKb = SCQb + QW * 2
        AUBb = SCKb + KWA * 2
        SVBb = AUBb + 3 * ANW * 2
        assert SVBb + 3 * 16 * 4 <= 64512, SVBb
        SET3 = SET3b // 2
        SWQ, SWK, SCQ, SCK = SWQb // 2, SWKb // 2, SCQb // 2, SCKb // 2
        AUB0 = AUBb // 2
        SVB0 = SVBb // 4

        grp_hist = []

        def qkvset(h):
            si = h % 3
            if si < 2:
                return A, SETA0 + si * SETSZ
            return Bb, SET3

        def emit_head_loads(h):
            AR, qb = qkvset(h)
            kb = qb + 2 * QW
            vb = kb + 2 * KWA
            wl = [("qspill", h // 2, h % 2), ("kspill", h // 2, h % 2)] + \
                 [("vspill", h // 2, psx, h % 2) for psx in range(NVP)] + p1all
            if h >= 3:
                wl.append(("avmm", (h - 3) * NQ + NQ - 1))
            dma("gpsimd", AR[:, qb:qb + 2 * QW].rearrange("p (c t) -> p c t", t=QW),
                qspill[h][:, 0:QW].rearrange("(c p) t -> p c t", p=128),
                waits=wl, ms=("ldq", h))
            dma("gpsimd", AR[:, kb:kb + 2 * KWA].rearrange("p (c t) -> p c t", t=KWA),
                kspill[h][:, 0:KWA].rearrange("(c p) t -> p c t", p=128),
                waits=wl, ms=("ldk", h))
            dma("gpsimd", AR[:, vb:vb + NKV * 256],
                vspill[h][:, 0:NKV * 256], waits=wl, ms=("ldv", h))
            wr = [("ldq", h)] + ([("ropeq", h - 1)] if h >= 1 else [])
            dma("gpsimd", Bb[0:32, SWQ:SWQ + QW], AR[32:64, qb:qb + QW], waits=wr, ms=("swq1", h))
            dma("gpsimd", Bb[32:64, SWQ:SWQ + QW], AR[0:32, qb:qb + QW], waits=wr, ms=("swq2", h))
            wr = [("ldk", h)] + ([("ropek", h - 1)] if h >= 1 else [])
            dma("gpsimd", Bb[0:32, SWK:SWK + KWA], AR[32:64, kb:kb + KWA], waits=wr, ms=("swk1", h))
            dma("gpsimd", Bb[32:64, SWK:SWK + KWA], AR[0:32, kb:kb + KWA], waits=wr, ms=("swk2", h))
            wv_ = [("swq1", h), ("swq2", h), ("tabq",), ("tabqs",)] + ([("ropeq", h - 1)] if h >= 1 else [])
            op("gpsimd", lambda e, AR=AR, qb=qb: e.tensor_mul(Bb[0:64, SCQ:SCQ + QW],
                                                              AR[0:64, qb:qb + QW], tabqc[:, 0:QW]),
               waits=wv_, ms=("rq1", h))
            op("gpsimd", lambda e: e.tensor_mul(Bb[0:64, SWQ:SWQ + QW],
                                                Bb[0:64, SWQ:SWQ + QW], tabqs[:, 0:QW]),
               waits=[("rq1", h)] + wv_, ms=("rq2", h))
            op("gpsimd", lambda e, AR=AR, qb=qb: e.tensor_add(AR[0:64, qb:qb + QW],
                                                              Bb[0:64, SCQ:SCQ + QW],
                                                              Bb[0:64, SWQ:SWQ + QW]),
               waits=[("rq1", h), ("rq2", h)], ms=("ropeq", h))
            wv_ = [("swk1", h), ("swk2", h), ("tabk",), ("tabks",)] + ([("ropek", h - 1)] if h >= 1 else [])
            op("gpsimd", lambda e, AR=AR, kb=kb: e.tensor_mul(Bb[0:64, SCK:SCK + KWA],
                                                              AR[0:64, kb:kb + KWA], tabkc[:, 0:KWA]),
               waits=wv_, ms=("rk1", h))
            op("gpsimd", lambda e: e.tensor_mul(Bb[0:64, SWK:SWK + KWA],
                                                Bb[0:64, SWK:SWK + KWA], tabks[:, 0:KWA]),
               waits=[("rk1", h)] + wv_, ms=("rk2", h))
            op("gpsimd", lambda e, AR=AR, kb=kb: e.tensor_add(AR[0:64, kb:kb + KWA],
                                                              Bb[0:64, SCK:SCK + KWA],
                                                              Bb[0:64, SWK:SWK + KWA]),
               waits=[("rk1", h), ("rk2", h)], ms=("ropek", h))

        pl_bank_last = {i: bankfree[i] for i in range(4)}
        tk_base = {}
        pl_flip = [0]

        def pl_banks(t):
            jl = t % NQ
            W = (QB0 + jl + 1) * 128
            b0 = tk_base[t]
            return list(range(b0 // 512, (b0 + W + 511) // 512))

        def emit_lg(t):
            h, jl = divmod(t, NQ)
            klen = QB0 + jl + 1
            W = klen * 128
            if role == 0 and klen <= 8:
                b0 = pl_flip[0] * 1024
                pl_flip[0] ^= 1
            else:
                b0 = 0
            tk_base[t] = b0
            AR, qb = qkvset(h)
            kb = qb + 2 * QW
            wlog = [("ropeq", h), ("ropek", h), ("d_maska",), ("d_ident",)]
            if t == 0:
                wlog.append(p1done)
            for bi in pl_banks(t):
                if pl_bank_last[bi] is not None:
                    wlog.append(pl_bank_last[bi])
            first = True
            coff = 0
            chunks = _split512(W)
            for ci, csz in enumerate(chunks):
                isdiag = (ci == len(chunks) - 1)
                for m in range(2):
                    stop = (m == 1 and not isdiag)
                    op("tensor",
                       lambda e, m=m, coff=coff, csz=csz, qb=qb, kb=kb, jl=jl,
                       stop=stop, AR=AR, b0=b0: e.matmul(
                           PL[:, b0 + coff:b0 + coff + csz],
                           AR[:, qb + m * QW + jl * 128:qb + m * QW + (jl + 1) * 128],
                           AR[:, kb + m * KWA + coff:kb + m * KWA + coff + csz],
                           start=(m == 0), stop=stop),
                       waits=wlog if first else (),
                       ms=("lgh", t) if (stop and ci == len(chunks) - 2) else None)
                    first = False
                if isdiag:
                    op("tensor",
                       lambda e, W=W, b0=b0: e.matmul(
                           PL[:, b0 + W - 128:b0 + W], identb[:, :], maska[:, :],
                           start=False, stop=True),
                       ms=("lgp", t))
                coff += csz

        def emit_mx(t):
            h, jl = divmod(t, NQ)
            W = (QB0 + jl + 1) * 128
            b0 = tk_base[t]
            SVX = SVB0 + 16 * (t % 3)
            nch = len(_split512(W))
            if nch == 1:
                wmx = [("lgp", t)]
                if t >= 3:
                    wmx.append(("exp", t - 3))
                op("vector", lambda e, W=W, SVX=SVX, b0=b0: e.reduce_max(
                       Bf[:, SVX:SVX + 1], PL[:, b0:b0 + W], axis=AX.X, negate=True),
                   waits=wmx, ms=("mx", t))
                return
            W0 = (nch - 1) * 512
            wmx = [("lgh", t)]
            if t >= 3:
                wmx.append(("mx", t - 3))
            op("vector", lambda e, W0=W0, SVX=SVX, b0=b0: e.reduce_max(
                   Bf[:, SVX + 1:SVX + 2], PL[:, b0:b0 + W0], axis=AX.X, negate=True),
               waits=wmx, ms=("mxa", t))
            op("vector", lambda e, W0=W0, W=W, SVX=SVX, b0=b0: e.reduce_max(
                   Bf[:, SVX + 2:SVX + 3], PL[:, b0 + W0:b0 + W], axis=AX.X, negate=True),
               waits=[("lgp", t)], ms=("mxb", t))
            wcmb = [("mxa", t), ("mxb", t)]
            if t >= 3:
                wcmb.append(("exp", t - 3))
            op("vector", lambda e, SVX=SVX: e.tensor_tensor(
                   Bf[:, SVX:SVX + 1], Bf[:, SVX + 1:SVX + 2],
                   Bf[:, SVX + 2:SVX + 3], ALU.min),
               waits=wcmb, ms=("mx", t))

        def emit_smx(t):
            h, jl = divmod(t, NQ)
            gp = t % 3
            W = (QB0 + jl + 1) * 128
            b0 = tk_base[t]
            SVX = SVB0 + 16 * gp
            AUB = AUB0 + ANW * gp
            AN = ANB + ANW * gp
            wexp = [("mx", t)]
            if t >= 3:
                wexp += [("nrm", t - 3), ("rcp", t - 3)]
            op("scalar", lambda e, W=W, SVX=SVX, AUB=AUB, b0=b0: e.activation(
                   Bb[:, AUB:AUB + W], PL[:, b0:b0 + W], AF.Exp,
                   bias=Bf[:, SVX:SVX + 1], scale=1.0,
                   accum_out=Bf[:, SVX + 5:SVX + 6]),
               waits=wexp, ms=("exp", t))
            for bi in pl_banks(t):
                pl_bank_last[bi] = ("exp", t)
            op("vector", lambda e, SVX=SVX: e.reciprocal(
                   Bf[:, SVX + 6:SVX + 7], Bf[:, SVX + 5:SVX + 6]),
               waits=[("exp", t)], ms=("rcp", t))
            wn = [("rcp", t)]
            if t >= 3:
                wn.append(("trl", t - 3))
            op("vector", lambda e, W=W, AUB=AUB, AN=AN: e.tensor_scalar_mul(
                   A[:, AN:AN + W], Bb[:, AUB:AUB + W], Bf[:, SVX + 6:SVX + 7]),
               waits=wn, ms=("nrm", t))

        def emit_tr(t):
            h, jl = divmod(t, NQ)
            klen = QB0 + jl + 1
            gp = t % 3
            AN = ANB + ANW * gp
            AT = ANB + 3 * ANW + ANW * gp
            ngrp = (klen + 3) // 4
            for g in range(ngrp):
                kks = list(range(4 * g, min(4 * g + 4, klen)))
                tb = len(grp_hist) % 2
                wtr = [("nrm", t)]
                if len(grp_hist) >= 2:
                    wtr.append(grp_hist[-2])
                elif bankfree[4 + tb] is not None:
                    wtr.append(bankfree[4 + tb])
                for i, kk in enumerate(kks):
                    op("tensor",
                       lambda e, tb=tb, kk=kk, i=i, AN=AN: e.transpose(
                           trP[tb][:, i * 128:(i + 1) * 128],
                           A[:, AN + kk * 128:AN + (kk + 1) * 128], identb[:, :]),
                       waits=wtr if i == 0 else (),
                       ms=("trl", t) if kk == klen - 1 else ("trx", t, kk))
                wge = [("trl", t) if kks[-1] == klen - 1 else ("trx", t, kks[-1])]
                if t >= 3:
                    wge.append(("avmm", t - 3))
                gkey = ("grpev", t, g)
                op("vector",
                   lambda e, tb=tb, g=g, n=len(kks), AT=AT: e.tensor_copy(
                       A[:, AT + g * 512:AT + g * 512 + n * 128],
                       trP[tb][:, 0:n * 128]),
                   waits=wge, ms=gkey)
                grp_hist.append(gkey)

        def emit_av(t):
            h, jl = divmod(t, NQ)
            klen = QB0 + jl + 1
            gp = t % 3
            AR, qb = qkvset(h)
            vb = qb + 2 * QW + 2 * KWA
            AT = ANB + 3 * ANW + ANW * gp
            for kk in range(klen):
                wav = [("grpev", t, kk // 4)]
                if kk == 0:
                    wav.append(("avev", t - 1) if t > 0 else p1done)
                    wav.append(("ldv", h))
                    if t == 0:
                        wav += bwait(6) + bwait(7)
                for dh in range(2):
                    op("tensor",
                       lambda e, dh=dh, kk=kk, klen=klen, vb=vb, AT=AT, AR=AR: e.matmul(
                           avP[dh][:, 0:128],
                           AR[:, vb + kk * 256 + dh * 128:vb + kk * 256 + (dh + 1) * 128],
                           A[:, AT + kk * 128:AT + (kk + 1) * 128],
                           start=(kk == 0), stop=(kk == klen - 1)),
                       waits=wav if (dh == 0 or kk == 0) else (),
                       ms=("avmm", t) if (kk == klen - 1 and dh == 1) else None)

        def emit_avev(t):
            h, jl = divmod(t, NQ)
            for dh in range(2):
                op("scalar",
                   lambda e, dh=dh, h=h, jl=jl: e.copy(
                       A[:, (2 * h + dh) * QW + jl * 128:(2 * h + dh) * QW + (jl + 1) * 128],
                       avP[dh][:, 0:128]),
                   waits=[("avmm", t)], ms=("avev", t) if dh == 1 else None)

        emit_head_loads(0)
        emit_head_loads(1)
        emit_head_loads(2)
        for u in range(T2 + 2):
            if u < T2:
                h, jl = divmod(u, NQ)
                if jl == 3 and 3 <= h + 2 < NH:
                    emit_head_loads(h + 2)
                emit_lg(u)
                emit_mx(u)
                emit_smx(u)
            if u - 2 >= 0:
                emit_av(u - 2)
            if 0 <= u - 1 < T2:
                emit_tr(u - 1)
            if u - 2 >= 0:
                emit_avev(u - 2)

        p2done = ("avev", T2 - 1)
        bankfree = [None] * 8
        E.set_fence([("avmm", T2 - 1), p2done, grp_hist[-1],
                     ("ropeq", NH - 1), ("ropek", NH - 1)]
                    + dma_tail("sync") + dma_tail("gpsimd"))

        WOSTb4 = 0
        WOBFb4 = WOSTb4 + 2 * 1024 * 4
        ZSTb4 = WOBFb4 + 16384 * 2
        XRSb4 = ZSTb4 + 4 * 512 * 4
        assert XRSb4 + 4 * 512 * 4 <= 64512
        WOST4, WOBF4 = WOSTb4 // 4, WOBFb4 // 2
        ZST4, XRS4 = ZSTb4 // 4, XRSb4 // 4
        groupsM = [list(range(0, min(NM, 8)))] + ([list(range(8, NM))] if NM > 8 else [])
        NG = len(groupsM)
        zlast = {}
        xlast = {}

        def wo_half_free(nn, half):
            if nn == 0:
                return None
            return ("wommH0", nn - 1, NG - 1) if half == 0 else ("womm", nn - 1, NG - 1)

        for nn in range(8):
            for k2 in range(16):
                b = k2 % 2
                half = k2 // 8
                if k2 >= 2:
                    w = [("wocast", nn, k2 - 2)]
                elif nn > 0:
                    w = [("wocast", nn - 1, 14 + k2)]
                else:
                    w = [p2done]
                dma("sync", Bf[:, WOST4 + b * 1024:WOST4 + (b + 1) * 1024].rearrange("p (c m) -> p c m", m=512),
                    wo_d[k2 * 256:(k2 + 1) * 256, nn * 512:(nn + 1) * 512].rearrange("(c p) m -> p c m", p=128),
                    waits=w, ms=("wod", nn, k2))
                wc = [("wod", nn, k2)]
                whf = wo_half_free(nn, half)
                if whf is not None:
                    wc.append(whf)
                if nn == 0 and k2 < 2:
                    wc.append(p2done)
                op("scalar",
                   lambda e, b=b, k2=k2: e.copy(Bb[:, WOBF4 + k2 * 1024:WOBF4 + (k2 + 1) * 1024],
                                                Bf[:, WOST4 + b * 1024:WOST4 + (b + 1) * 1024]),
                   waits=wc, ms=("wocast", nn, k2))
            for gi, ms_ in enumerate(groupsM):
                for kk in range(KD):
                    for ii, m in enumerate(ms_):
                        wmm = [("wocast", nn, kk // 2)]
                        if kk == 0:
                            wmm += bwait(ii) + [p2done]
                        if kk == KD - 1 and ii == len(ms_) - 1:
                            mskey = ("womm", nn, gi)
                        elif kk == 15 and ii == len(ms_) - 1:
                            mskey = ("wommH0", nn, gi)
                        else:
                            mskey = None
                        op("tensor",
                           lambda e, m=m, ii=ii, kk=kk: e.matmul(
                               bank(ii),
                               A[:, kk * QW + m * 128:kk * QW + (m + 1) * 128],
                               Bb[:, WOBF4 + kk * 512:WOBF4 + (kk + 1) * 512],
                               start=(kk == 0), stop=(kk == KD - 1)),
                           waits=wmm if (ii == 0 or kk == 0) else (),
                           ms=mskey)
                for ii, m in enumerate(ms_):
                    zb = m % 4
                    wx = [p2done]
                    if ("x", zb) in xlast:
                        wx.append(xlast[("x", zb)])
                    dma("sync", Bf[:, XRS4 + zb * 512:XRS4 + (zb + 1) * 512],
                        xres_d[m * 128:(m + 1) * 128, nn * 512:(nn + 1) * 512],
                        waits=wx, ms=("xrsd", nn, m))
                    w = [("womm", nn, gi), ("xrsd", nn, m)]
                    if ("z", zb) in zlast:
                        w.append(zlast[("z", zb)])
                    mskey = ("zev", nn, m)
                    op("vector",
                       lambda e, ii=ii, zb=zb: e.tensor_add(
                           Bf[:, ZST4 + zb * 512:ZST4 + (zb + 1) * 512], bank(ii),
                           Bf[:, XRS4 + zb * 512:XRS4 + (zb + 1) * 512]),
                       waits=w, ms=mskey)
                    bankfree[ii] = mskey
                    xlast[("x", zb)] = mskey
                    op("vector",
                       lambda e, m=m, nn=nn, zb=zb: e.bn_stats(
                           stats[:, ST_BN + (m * 8 + nn) * 6:ST_BN + (m * 8 + nn + 1) * 6],
                           Bf[:, ZST4 + zb * 512:ZST4 + (zb + 1) * 512]),
                       waits=[("zev", nn, m)], ms=("bnst", m, nn))
                    dma("gpsimd", z_hbm[m][:, nn * 512:(nn + 1) * 512],
                        Bf[:, ZST4 + zb * 512:ZST4 + (zb + 1) * 512],
                        waits=[("zev", nn, m), ("bnst", m, nn)], ms=("zdma", nn, m))
                    zlast[("z", zb)] = ("zdma", nn, m)

        zall_list = [("zdma", 7, i) for i in range(NM)]
        E.set_fence([("womm", 7, NG - 1), ("bnst", groupsM[-1][-1], 7),
                     ("wocast", 7, 15)] + dma_tail("sync") + dma_tail("gpsimd"))

        TSLb5 = 0
        X2Fb5 = TSLb5 + 4 * 512 * 4
        X2Bb5 = X2Fb5 + 4 * 512 * 4
        TSL5, X2F5 = TSLb5 // 4, X2Fb5 // 4
        X2B5 = X2Bb5 // 2
        def p5_stats(m):
            op("vector",
               lambda e, m=m: e.bn_aggr(stats[:, ST_MV + m * 2:ST_MV + (m + 1) * 2],
                                        stats[:, ST_BN + m * 48:ST_BN + (m + 1) * 48]),
               waits=[("bnst", m, n2) for n2 in range(8)], ms=("mv", m))
            op("scalar",
               lambda e, m=m: e.activation(stats[:, ST_RSTD + m:ST_RSTD + m + 1],
                                           stats[:, ST_MV + m * 2 + 1:ST_MV + m * 2 + 2],
                                           AF.Ln, bias=stats[:, ST_EPS:ST_EPS + 1], scale=1.0),
               waits=[("mv", m), ("epsv",)], ms=("lnv", m))
            op("scalar",
               lambda e, m=m: e.activation(stats[:, ST_RSTD + m:ST_RSTD + m + 1],
                                           stats[:, ST_RSTD + m:ST_RSTD + m + 1],
                                           AF.Exp, bias=0.0, scale=-0.5),
               waits=[("lnv", m)], ms=("rstd", m))
            op("vector",
               lambda e, m=m: e.tensor_mul(stats[:, ST_NMR + m:ST_NMR + m + 1],
                                           stats[:, ST_MV + m * 2:ST_MV + m * 2 + 1],
                                           stats[:, ST_RSTD + m:ST_RSTD + m + 1]),
               waits=[("rstd", m)], ms=("nmr", m))

        def p5_front(m, nn):
            bx = nn % 4
            w = list(zall_list) + [("zdma", 7, m)]
            if nn >= 4:
                w.append(("x2n", m, nn - 4))
            elif m > 0:
                w.append(("x2n", m - 1, nn + 4))
            dma("sync", Bf[:, TSL5 + bx * 512:TSL5 + (bx + 1) * 512],
                z_hbm[m][:, nn * 512:(nn + 1) * 512], waits=w, ms=("zrd", m, nn))
            tb = nn % 4
            wn = [("zrd", m, nn), ("nmr", m)]
            if nn >= 4:
                wn.append(("x2dma", m, nn - 4))
            elif m > 0:
                wn.append(("x2dma", m - 1, nn + 4))
            op("vector",
               lambda e, m=m, bx=bx, tb=tb: e.tensor_scalar(
                   Bf[:, X2F5 + tb * 512:X2F5 + (tb + 1) * 512],
                   Bf[:, TSL5 + bx * 512:TSL5 + (bx + 1) * 512],
                   stats[:, ST_RSTD + m:ST_RSTD + m + 1],
                   stats[:, ST_NMR + m:ST_NMR + m + 1],
                   ALU.mult, ALU.subtract),
               waits=wn, ms=("x2n", m, nn))
            dma("gpsimd", x2_hbm[m][:, nn * 512:(nn + 1) * 512],
                Bf[:, X2F5 + tb * 512:X2F5 + (tb + 1) * 512],
                waits=[("x2n", m, nn)], ms=("x2dma", m, nn))
            wxc = [("x2n", m, nn)]
            if nn >= 4:
                wxc.append(("x2t", m, (nn - 4) * 4 + 3))
            elif m > 0:
                wxc.append(("x2t", m - 1, (nn + 4) * 4 + 3))
            op("scalar",
               lambda e, tb=tb: e.copy(
                   Bb[:, X2B5 + tb * 512:X2B5 + (tb + 1) * 512],
                   Bf[:, X2F5 + tb * 512:X2F5 + (tb + 1) * 512]),
               waits=wxc, ms=("x2c", m, nn))

        def p5_back(m, nn):
            tb = nn % 4
            wrsp = [("x2c", m, nn)]
            if m > 0 and nn == 0:
                wrsp.append(("rs", m - 1))
            op("vector",
               lambda e, nn=nn, tb=tb: e.reduce_sum(
                   stats[:, ST_RSP + nn:ST_RSP + nn + 1],
                   Bb[:, X2B5 + tb * 512:X2B5 + (tb + 1) * 512], axis=AX.X),
               waits=wrsp, ms=("rsp", m, nn))
            pbk = len(grp_hist) % 2
            wtr = [("x2c", m, nn), p2done]
            if len(grp_hist) >= 2:
                wtr.append(grp_hist[-2])
            for t4 in range(4):
                kkc = nn * 4 + t4
                op("tensor",
                   lambda e, pbk=pbk, tb=tb, t4=t4: e.transpose(
                       trP[pbk][:, t4 * 128:(t4 + 1) * 128],
                       Bb[:, X2B5 + tb * 512 + t4 * 128:X2B5 + tb * 512 + (t4 + 1) * 128],
                       identb[:, :]),
                   waits=wtr if t4 == 0 else (), ms=("x2t", m, kkc))
            gkey = ("x2g", m, nn)
            op("vector",
               lambda e, pbk=pbk, m=m, nn=nn: e.tensor_copy(
                   A[:, 0:32 * NM * 128].rearrange("p (c t) -> p c t", t=NM * 128)
                   [:, nn * 4:nn * 4 + 4, m * 128:(m + 1) * 128],
                   trP[pbk][:, 0:512].rearrange("p (c t) -> p c t", t=128)),
               waits=[("x2t", m, nn * 4 + 3)], ms=gkey)
            grp_hist.append(gkey)
            if nn == 7:
                op("vector",
                   lambda e, m=m: e.reduce_sum(stats[:, ST_RS + m:ST_RS + m + 1],
                                               stats[:, ST_RSP:ST_RSP + 8], axis=AX.X),
                   waits=[("rsp", m, n2) for n2 in range(8)], ms=("rs", m))

        chunksP5 = [(m, nn) for m in range(NM) for nn in range(8)]
        for j5 in range(len(chunksP5) + 1):
            if j5 < len(chunksP5):
                m5, nn5 = chunksP5[j5]
                if nn5 == 0:
                    p5_stats(m5)
                p5_front(m5, nn5)
            if j5 >= 1:
                p5_back(*chunksP5[j5 - 1])
        op("vector",
           lambda e: e.tensor_scalar_mul(stats[:, ST_NB:ST_NB + NM], stats[:, ST_RS:ST_RS + NM],
                                         stats[:, ST_NEGFS:ST_NEGFS + 1]),
           waits=[("rs", m2) for m2 in range(NM)] + [("negfs",)], ms=("nb",))

        x2tall = ("x2g", NM - 1, 7)
        bankfree = [None] * 8
        E.set_fence([("x2t", NM - 1, 31), ("nb",), ("x2c", NM - 1, 7)]
                    + dma_tail("sync") + dma_tail("gpsimd"))

        FWSTb6 = 0
        FWBFb6 = FWSTb6 + 2 * 1024
        X2RSb6 = FWBFb6 + 16384 * 2
        Z2STb6 = X2RSb6 + 4 * 512 * 4
        assert Z2STb6 + 4 * 512 * 4 <= 64512
        FWST6 = FWSTb6
        FWBF6 = FWBFb6 // 2
        X2RS6, Z2ST6 = X2RSb6 // 4, Z2STb6 // 4
        zlast = {}
        xlast = {}

        for nn in range(8):
            for k2 in range(16):
                b = k2 % 2
                half = k2 // 8
                if k2 >= 2:
                    w = [("fwcast", nn, k2 - 2)]
                elif nn > 0:
                    w = [("fwcast", nn - 1, 14 + k2)]
                else:
                    w = [("nb",), x2tall]
                dma("sync", Bu[:, FWST6 + b * 1024:FWST6 + (b + 1) * 1024].rearrange("p (c m) -> p c m", m=512),
                    fw_d[k2 * 256:(k2 + 1) * 256, nn * 512:(nn + 1) * 512].rearrange("(c p) m -> p c m", p=128),
                    waits=w, ms=("fwd", nn, k2))
                wc = [("fwd", nn, k2)]
                if nn > 0:
                    wc.append(("fmmH0", nn - 1, NG - 1) if half == 0
                              else ("fmm", nn - 1, NG - 1))
                if nn == 0 and k2 < 2:
                    wc += [("nb",), x2tall]
                op("scalar",
                   lambda e, b=b, k2=k2: e.copy(Bb[:, FWBF6 + k2 * 1024:FWBF6 + (k2 + 1) * 1024],
                                                Bu[:, FWST6 + b * 1024:FWST6 + (b + 1) * 1024]),
                   waits=wc, ms=("fwcast", nn, k2))
            for gi, ms_ in enumerate(groupsM):
                for kk in range(KD):
                    for ii, m in enumerate(ms_):
                        wmm = [("fwcast", nn, kk // 2)]
                        if kk == 0:
                            wmm += bwait(ii) + [x2tall]
                        if kk == KD - 1 and ii == len(ms_) - 1:
                            mskey = ("fmm", nn, gi)
                        elif kk == 15 and ii == len(ms_) - 1:
                            mskey = ("fmmH0", nn, gi)
                        else:
                            mskey = None
                        op("tensor",
                           lambda e, m=m, ii=ii, kk=kk: e.matmul(
                               bank(ii),
                               A[:, kk * NM * 128 + m * 128:kk * NM * 128 + (m + 1) * 128],
                               Bb[:, FWBF6 + kk * 512:FWBF6 + (kk + 1) * 512],
                               start=(kk == 0), stop=(kk == KD - 1)),
                           waits=wmm if (ii == 0 or kk == 0) else (),
                           ms=mskey)
                for ii, m in enumerate(ms_):
                    zb = m % 4
                    wx = [("nb",), x2tall, ("x2dma", m, nn)]
                    if ("x", zb) in xlast:
                        wx.append(xlast[("x", zb)])
                    dma("sync", Bf[:, X2RS6 + zb * 512:X2RS6 + (zb + 1) * 512],
                        x2_hbm[m][:, nn * 512:(nn + 1) * 512],
                        waits=wx, ms=("x2sl", nn, m))
                    op("vector",
                       lambda e, m=m, zb=zb: e.tensor_scalar_add(
                           Bf[:, X2RS6 + zb * 512:X2RS6 + (zb + 1) * 512],
                           Bf[:, X2RS6 + zb * 512:X2RS6 + (zb + 1) * 512],
                           stats[:, ST_NB + m:ST_NB + m + 1]),
                       waits=[("x2sl", nn, m), ("nb",)], ms=("xadj", nn, m))
                    w = [("fmm", nn, gi), ("xadj", nn, m), ("x2sl", nn, m), ("fs2",)]
                    if ("z", zb) in zlast:
                        w.append(zlast[("z", zb)])
                    mskey = ("z2ev", nn, m)
                    op("vector",
                       lambda e, ii=ii, zb=zb: e.scalar_tensor_tensor(
                           Bf[:, Z2ST6 + zb * 512:Z2ST6 + (zb + 1) * 512], bank(ii),
                           stats[:, ST_FS2:ST_FS2 + 1],
                           Bf[:, X2RS6 + zb * 512:X2RS6 + (zb + 1) * 512],
                           ALU.mult, ALU.add),
                       waits=w, ms=mskey)
                    bankfree[ii] = mskey
                    xlast[("x", zb)] = mskey
                    wb = [("z2ev", nn, m)]
                    if nn == 0:
                        wb.append(("mv", NM - 1))
                    op("vector",
                       lambda e, m=m, nn=nn, zb=zb: e.bn_stats(
                           stats[:, ST_BN + (m * 8 + nn) * 6:ST_BN + (m * 8 + nn + 1) * 6],
                           Bf[:, Z2ST6 + zb * 512:Z2ST6 + (zb + 1) * 512]),
                       waits=wb, ms=("bnst2", m, nn))
                    dma("gpsimd", z2_hbm[m][:, nn * 512:(nn + 1) * 512],
                        Bf[:, Z2ST6 + zb * 512:Z2ST6 + (zb + 1) * 512],
                        waits=[("bnst2", m, nn)], ms=("z2dma", nn, m))
                    zlast[("z", zb)] = ("z2dma", nn, m)

        z2all_list = [("z2dma", 7, i) for i in range(NM)]
        E.set_fence([("fmm", 7, NG - 1), ("bnst2", groupsM[-1][-1], 7),
                     ("fwcast", 7, 15)] + dma_tail("sync") + dma_tail("gpsimd"))

        T7b = 0
        O7b = T7b + 4 * 512 * 4
        T7, O7 = T7b // 4, O7b // 4
        assert O7b + 4 * 512 * 4 <= 64512
        for m in range(NM):
            op("vector",
               lambda e, m=m: e.bn_aggr(stats[:, ST_MV + m * 2:ST_MV + (m + 1) * 2],
                                        stats[:, ST_BN + m * 48:ST_BN + (m + 1) * 48]),
               waits=[("bnst2", m, n2) for n2 in range(8)] + [("rstd", NM - 1), ("nmr", NM - 1)],
               ms=("mv2", m))
            op("scalar",
               lambda e, m=m: e.activation(stats[:, ST_RSTD + m:ST_RSTD + m + 1],
                                           stats[:, ST_MV + m * 2 + 1:ST_MV + m * 2 + 2],
                                           AF.Ln, bias=stats[:, ST_EPS:ST_EPS + 1], scale=1.0),
               waits=[("mv2", m), ("epsv",)], ms=("lnv2", m))
            op("scalar",
               lambda e, m=m: e.activation(stats[:, ST_RSTD + m:ST_RSTD + m + 1],
                                           stats[:, ST_RSTD + m:ST_RSTD + m + 1],
                                           AF.Exp, bias=0.0, scale=-0.5),
               waits=[("lnv2", m)], ms=("rstd2", m))
            op("vector",
               lambda e, m=m: e.tensor_mul(stats[:, ST_NMR + m:ST_NMR + m + 1],
                                           stats[:, ST_MV + m * 2:ST_MV + m * 2 + 1],
                                           stats[:, ST_RSTD + m:ST_RSTD + m + 1]),
               waits=[("rstd2", m)], ms=("nmr2", m))
            for nn in range(8):
                bx = nn % 4
                w = list(z2all_list) + [("z2dma", 7, m)]
                if nn >= 4:
                    w.append(("t2n", m, nn - 4))
                elif m > 0:
                    w.append(("t2n", m - 1, nn + 4))
                dma("sync", Bf[:, T7 + bx * 512:T7 + (bx + 1) * 512],
                    z2_hbm[m][:, nn * 512:(nn + 1) * 512], waits=w, ms=("z2rd", m, nn))
                tb = nn % 4
                wn = [("z2rd", m, nn), ("nmr2", m)]
                if nn >= 4:
                    wn.append(("outdma", m, nn - 4))
                elif m > 0:
                    wn.append(("outdma", m - 1, nn + 4))
                op("vector",
                   lambda e, m=m, bx=bx, tb=tb: e.tensor_scalar(
                       Bf[:, O7 + tb * 512:O7 + (tb + 1) * 512],
                       Bf[:, T7 + bx * 512:T7 + (bx + 1) * 512],
                       stats[:, ST_RSTD + m:ST_RSTD + m + 1],
                       stats[:, ST_NMR + m:ST_NMR + m + 1],
                       ALU.mult, ALU.subtract),
                   waits=wn, ms=("t2n", m, nn))
                dma("gpsimd", out_d[m * 128:(m + 1) * 128, nn * 512:(nn + 1) * 512],
                    Bf[:, O7 + tb * 512:O7 + (tb + 1) * 512],
                    waits=[("t2n", m, nn)], ms=("outdma", m, nn))

        for i in range(QK["sync"]):
            E.final_wait("sync", f"s{i}")
        for i in range(QK["gpsimd"]):
            E.final_wait("gpsimd", f"g{i}")

    E0.begin_count()
    walk(0, E0)
    E1.begin_count()
    walk(1, E1)

    sems = {}
    with ExitStack() as semstack:
        for name in (["t", "v", "s", "p", "rr"] + [f"s{i}" for i in range(12)] + [f"g{i}" for i in range(8)]):
            sems[name] = semstack.enter_context(nc.semaphore(f"sem_{name}"))

        def section(engname):
            def body(eng):
                if engname == "sync":
                    eng.dma_start(out=rolesb[:, :], in_=role_d).then_inc(sems["rr"], 16)
                eng.wait_ge(sems["rr"], 16)
                with eng.register(f"role_{engname}") as r:
                    eng.reg_load(r, rolesb[0:1, 0:1])
                    for i in nc.Switch(engines=[eng], index=[eng.snap(r)], n=2):
                        EE = E0 if i == 0 else E1
                        EE.begin_emit(engname, eng, sems)
                        walk(i, EE)
            return body

        with nc.Block() as block:
            block.sync(section("sync"))
            block.gpsimd(section("gpsimd"))
            block.tensor(section("tensor"))
            block.vector(section("vector"))
            block.scalar(section("scalar"))
    es.close()
    return nc



def _rot_perm():
    p = np.arange(D)
    for h in range(NH):
        base = h * HD
        p[base:base + 64] = np.concatenate([np.arange(0, 64, 2), np.arange(1, 64, 2)]) + base
    return p


def _tables(idx, width):
    inv_freq = (1.0 / (10000.0 ** (np.arange(0, 64, 2, dtype=np.float32) / np.float32(64.0)))).astype(np.float32)
    t = idx.astype(np.float32)
    sinus = np.einsum("i,j->ij", t, inv_freq).astype(np.float32)
    sn = np.sin(sinus).astype(np.float32).T
    cs = np.cos(sinus).astype(np.float32).T
    out = np.zeros((128, width), np.float32)
    n = len(idx)
    out[0:32, 0:n] = cs
    out[32:64, 0:n] = cs
    out[64:96, 0:n] = sn
    out[96:128, 0:n] = sn
    return out.astype(BFNP)


def _in_maps(x, wq, wk, wv, wo, fw, fs):
    ident = np.eye(128, dtype=np.float32).astype(BFNP)
    mval = np.float32(NEGV / 16.0)
    tri = np.where(np.arange(128)[:, None] >= np.arange(128)[None, :],
                   np.float32(0.0), mval).astype(BFNP)
    NMXT = ACUT * 128
    maps = []
    for c in range(8):
        b, role = c // 2, c % 2
        if role == 0:
            myidx = np.arange(0, ACUT * 128)
        else:
            myidx = np.arange(ACUT * 128, S)
        xres = np.zeros((NMXT, D), np.float32)
        xres[0:len(myidx)] = x[b][myidx]
        maps.append({
            "xT": np.ascontiguousarray(x[b].T),
            "xres": xres,
            "wq": wq, "wk": wk, "wv": wv, "wo": wo, "fw": fw, "fs": fs,
            "rolein": np.array([[role]], dtype=np.uint32),
            "tabq": _tables(myidx, NMXT), "tabk": _tables(np.arange(S), S),
            "maska": tri, "ident": ident,
        })
    return maps


def kernel(L0_attn_hidden_states, L0_attn_Wq, L0_attn_Wk, L0_attn_Wv, L0_attn_Wo,
           L0_ffn_kernel, L0_ffn_scale):
    x = np.asarray(L0_attn_hidden_states)
    perm = _rot_perm()
    wq = np.ascontiguousarray(np.asarray(L0_attn_Wq)[:, perm])
    wk = np.ascontiguousarray(np.asarray(L0_attn_Wk)[:, perm])
    wv = np.ascontiguousarray(np.asarray(L0_attn_Wv))
    wo = np.ascontiguousarray(np.asarray(L0_attn_Wo))
    fw = np.ascontiguousarray(np.asarray(L0_ffn_kernel).view(np.uint8))
    fs = np.asarray(L0_ffn_scale, dtype=np.float32).reshape(1, 1)

    nc = build_nc()
    in_maps = _in_maps(x, wq, wk, wv, wo, fw, fs)
    res = run_bass_kernel_spmd(nc, in_maps, core_ids=list(range(8)))
    out = np.empty((4, S, D), np.float32)
    for c in range(8):
        b, role = c // 2, c % 2
        if role == 0:
            out[b, 0:ACUT * 128, :] = res.results[c]["out"][0:ACUT * 128]
        else:
            out[b, ACUT * 128:S, :] = res.results[c]["out"][0:S - ACUT * 128]
    return out


# version 15
# speedup vs baseline: 1.0372x; 1.0372x over previous
from contextlib import ExitStack

import numpy as np
import ml_dtypes

import concourse.bass as bass
import concourse.mybir as mybir
from concourse.bass_utils import run_bass_kernel_spmd

F32 = mybir.dt.float32
BF16 = mybir.dt.bfloat16
U8 = mybir.dt.uint8
U32 = mybir.dt.uint32
AF = mybir.ActivationFunctionType
ALU = mybir.AluOpType
AX = mybir.AxisListType

D = 4096
S = 2048
NH = 16
HD = 256
KD = D // 128
ACUT = 10
EPS = 1e-5
NEGV = -1e9
SCL = 1.0 / 16.0

BFNP = np.dtype(ml_dtypes.bfloat16)


def _split512(n):
    out = []
    while n > 0:
        c = min(n, 512)
        out.append(c)
        n -= c
    return out


class Sched:
    def __init__(self):
        self.mode = "count"
        self.cnt = {}
        self.ms = {}
        self.eng = None
        self.engname = None
        self.sems = None
        self.waited = None
        self.fence = {}

    def begin_count(self):
        self.mode = "count"
        self.cnt = {}
        self.ms = {}
        self.fence = {}

    def begin_emit(self, engname, eng, sems):
        self.mode = "emit"
        self.engname = engname
        self.eng = eng
        self.sems = sems
        self.waited = {}
        self.fence = {}

    def set_fence(self, keys):
        for en in ("sync", "gpsimd", "tensor", "vector", "scalar"):
            self.fence[en] = list(keys)

    def op(self, engname, fn, waits=(), inc=None, ms=None):
        pend = self.fence.pop(engname, None)
        if pend:
            waits = list(waits) + pend
        if self.mode == "count":
            for k in waits:
                assert k in self.ms, f"wait on undefined milestone {k}"
            if inc is not None:
                s, n = inc
                self.cnt[s] = self.cnt.get(s, 0) + n
                if ms is not None:
                    for k in (ms if isinstance(ms, list) else [ms]):
                        assert k not in self.ms, f"dup milestone {k}"
                        self.ms[k] = (s, self.cnt[s])
            else:
                assert ms is None, ms
            return
        if engname != self.engname:
            return
        for k in waits:
            s, v = self.ms[k]
            if v > self.waited.get(s, 0):
                self.eng.wait_ge(self.sems[s], v)
                self.waited[s] = v
        instr = fn(self.eng)
        if inc is not None:
            s, n = inc
            instr.then_inc(self.sems[s], n)

    def final_wait(self, engname, semname):
        if self.mode == "count" or engname != self.engname:
            return
        v = self.cnt.get(semname, 0)
        if v > self.waited.get(semname, 0):
            self.eng.wait_ge(self.sems[semname], v)
            self.waited[semname] = v


def build_nc():
    nc = bass.Bass("TRN2", target_bir_lowering=False)

    NQR = [ACUT, NH - ACUT]
    NKVR = [ACUT, 16]
    QB0R = [0, ACUT]
    NMX = max(NQR)

    xT_d = nc.dram_tensor("xT", [D, S], F32, kind="ExternalInput").ap()
    xres_d = nc.dram_tensor("xres", [NMX * 128, D], F32, kind="ExternalInput").ap()
    wq_d = nc.dram_tensor("wq", [D, D], F32, kind="ExternalInput").ap()
    wk_d = nc.dram_tensor("wk", [D, D], F32, kind="ExternalInput").ap()
    wv_d = nc.dram_tensor("wv", [D, D], F32, kind="ExternalInput").ap()
    wo_d = nc.dram_tensor("wo", [D, D], F32, kind="ExternalInput").ap()
    fw_d = nc.dram_tensor("fw", [D, D], U8, kind="ExternalInput").ap()
    fs_d = nc.dram_tensor("fs", [1, 1], F32, kind="ExternalInput").ap()
    role_d = nc.dram_tensor("rolein", [1, 1], U32, kind="ExternalInput").ap()
    tabq_d = nc.dram_tensor("tabq", [128, NMX * 128], BF16, kind="ExternalInput").ap()
    tabk_d = nc.dram_tensor("tabk", [128, S], BF16, kind="ExternalInput").ap()
    maska_d = nc.dram_tensor("maska", [128, 128], BF16, kind="ExternalInput").ap()
    ident_d = nc.dram_tensor("ident", [128, 128], BF16, kind="ExternalInput").ap()
    out_d = nc.dram_tensor("out", [NMX * 128, D], F32, kind="ExternalOutput").ap()

    qspill = nc.dram_tensor("qspill", [NH, HD, NMX * 128], BF16, kind="Internal").ap()
    kspill = nc.dram_tensor("kspill", [NH, HD, S], BF16, kind="Internal").ap()
    vspill = nc.dram_tensor("vspill", [NH, 128, 16 * HD], BF16, kind="Internal").ap()
    z_hbm = nc.dram_tensor("z_hbm", [NMX, 128, D], F32, kind="Internal").ap()
    x2_hbm = nc.dram_tensor("x2_hbm", [NMX, 128, D], F32, kind="Internal").ap()
    z2_hbm = nc.dram_tensor("z2_hbm", [NMX, 128, D], F32, kind="Internal").ap()

    es = ExitStack()
    A = es.enter_context(nc.sbuf_tensor("arenaA", [128, 65536], BF16))
    Bu = es.enter_context(nc.sbuf_tensor("arenaB", [128, 64512], U8))
    Bf = Bu.bitcast(F32)
    Bb = Bu.bitcast(BF16)
    identb = es.enter_context(nc.sbuf_tensor("identb", [128, 128], BF16))
    maska = es.enter_context(nc.sbuf_tensor("maska_s", [128, 128], BF16))
    tabqc = es.enter_context(nc.sbuf_tensor("tabqc", [64, NMX * 128], BF16))
    tabqs = es.enter_context(nc.sbuf_tensor("tabqs", [64, NMX * 128], BF16))
    tabkc = es.enter_context(nc.sbuf_tensor("tabkc", [64, S], BF16))
    tabks = es.enter_context(nc.sbuf_tensor("tabks", [64, S], BF16))
    stats = es.enter_context(nc.sbuf_tensor("stats", [128, 576], F32))
    ones1 = es.enter_context(nc.sbuf_tensor("ones1", [1, 128], F32))
    rolesb = es.enter_context(nc.sbuf_tensor("rolesb", [1, 1], U32))

    ST_BN = 0
    ST_MV = 480
    ST_RSTD = 500
    ST_NMR = 512
    ST_RS = 524
    ST_NB = 536
    ST_FS = 548
    ST_FSB = 549
    ST_FS2 = 550
    ST_NEGFS = 551
    ST_EPS = 552
    ST_RSP = 560

    PL = es.enter_context(nc.psum_tensor("PL", [128, 2048], F32))
    pb = [es.enter_context(nc.psum_tensor(f"pb{i}", [128, 512], F32)) for i in range(4)]
    trP = [p.bitcast(BF16) for p in pb[:2]]
    avP = [pb[2], pb[3]]

    def bank(i):
        if i < 4:
            return PL[:, i * 512:(i + 1) * 512]
        return pb[i - 4][:, :]

    E0, E1 = Sched(), Sched()

    def walk(role, E):
        NQ = NQR[role]
        NKV = NKVR[role]
        QB0 = QB0R[role]
        NM = NQ
        QW = NQ * 128
        KWA = NKV * 128
        T2 = NH * NQ

        QK = {"sync": 12, "gpsimd": 8}
        qc = {"sync": 0, "gpsimd": 0}

        def dma(q, out, in_, waits=(), ms=None):
            j = qc[q]
            qc[q] = j + 1
            sem = f"{q[0]}{j % QK[q]}"
            w = list(waits)
            if j >= QK[q]:
                w.append(("_dma", q, j - QK[q]))
            mss = [("_dma", q, j)] + ([ms] if ms is not None else [])
            E.op(q, lambda e, o=out, i=in_: e.dma_start(out=o, in_=i),
                 w, inc=(sem, 16), ms=mss)

        def op(eng, fn, waits=(), ms=None):
            semname = {"tensor": "t", "vector": "v", "scalar": "s", "gpsimd": "p"}[eng]
            E.op(eng, fn, waits, inc=(semname, 1) if ms is not None else None, ms=ms)

        def dma_tail(q):
            j1 = qc[q]
            return [("_dma", q, j) for j in range(max(0, j1 - QK[q]), j1)]

        bankfree = [None] * 8

        def bwait(i):
            return [bankfree[i]] if bankfree[i] is not None else []

        dma("sync", identb[:, :], ident_d, ms=("d_ident",))
        dma("sync", maska[:, :], maska_d, ms=("d_maska",))
        dma("sync", tabqc[:, :], tabq_d[0:64, :], ms=("tabq",))
        dma("sync", tabqs[:, :], tabq_d[64:128, :], ms=("tabqs",))
        dma("sync", tabkc[:, :], tabk_d[0:64, :], ms=("tabk",))
        dma("sync", tabks[:, :], tabk_d[64:128, :], ms=("tabks",))
        dma("sync", stats[0:1, ST_FS:ST_FS + 1], fs_d, ms=("d_fs",))
        op("vector", lambda e: e.memset(ones1[:, :], 1.0), ms=("ones",))
        op("vector", lambda e: e.memset(stats[:, ST_EPS:ST_EPS + 1], float(EPS)), ms=("epsv",))
        op("tensor", lambda e: e.matmul(pb[0][:, 0:1], ones1[0:1, :],
                                        stats[0:1, ST_FS:ST_FS + 1], start=True, stop=True),
           waits=[("ones",), ("d_fs",)], ms=("fsmm",))
        op("vector", lambda e: e.tensor_copy(stats[:, ST_FSB:ST_FSB + 1], pb[0][:, 0:1]),
           waits=[("fsmm",)], ms=("fsb",))
        op("vector", lambda e: e.tensor_scalar_mul(stats[:, ST_FS2:ST_FS2 + 1],
                                                   stats[:, ST_FSB:ST_FSB + 1], 2.0),
           waits=[("fsb",)], ms=("fs2",))
        op("vector", lambda e: e.tensor_scalar_mul(stats[:, ST_NEGFS:ST_NEGFS + 1],
                                                   stats[:, ST_FSB:ST_FSB + 1], -1.0),
           waits=[("fsb",)], ms=("negfs",))
        bankfree[4] = ("fsb",)

        XH = KWA // 2
        XSTb = 0
        WQSTb = XSTb + 3 * XH * 4
        WKSTb = WQSTb + 2 * 512 * 4
        WVSTb = WKSTb + 2 * 512 * 4
        WQBFb = WVSTb + 2 * 1024 * 4
        WKBFb = WQBFb + 4 * 512 * 2
        WVBFb = WKBFb + 4 * 512 * 2
        QSTGb = WVBFb + 4 * 1024 * 2
        KSTGb = QSTGb + 2 * QW * 2
        VSTGb = KSTGb + 2 * KWA * 2
        assert VSTGb + 4096 * 2 <= 64512, VSTGb
        XST = XSTb // 4
        WQST, WKST, WVST = WQSTb // 4, WKSTb // 4, WVSTb // 4
        WQBF, WKBF, WVBF = WQBFb // 2, WKBFb // 2, WVBFb // 2
        QSTG, KSTG, VSTG = QSTGb // 2, KSTGb // 2, VSTGb // 2

        qsplit = _split512(QW)
        ksplit = _split512(KWA)
        nqc, nkc = len(qsplit), len(ksplit)
        vpasses = []
        c0 = 0
        while c0 < NKV:
            vpasses.append((c0, min(c0 + 8, NKV)))
            c0 += 8
        NVP = len(vpasses)

        for kk in range(KD):
            for hf in range(2):
                idx = 2 * kk + hf
                s = idx % 3
                if idx >= 3:
                    pk, ph = divmod(idx - 3, 2)
                    w = [("xcast", pk, ph)]
                else:
                    w = []
                dma("gpsimd", Bf[:, XST + s * XH:XST + (s + 1) * XH],
                    xT_d[kk * 128:(kk + 1) * 128, hf * XH:(hf + 1) * XH],
                    waits=w, ms=("xdma", kk, hf))
                op("scalar",
                   lambda e, kk=kk, hf=hf, s=s: e.copy(
                       A[:, kk * KWA + hf * XH:kk * KWA + (hf + 1) * XH],
                       Bf[:, XST + s * XH:XST + (s + 1) * XH]),
                   waits=[("xdma", kk, hf)], ms=("xcast", kk, hf))

        for p in range(8):
            h0 = 2 * p
            for hh in range(2):
                h = h0 + hh
                for k2 in range(16):
                    b = k2 % 2
                    if k2 >= 2:
                        w = [("qcast", p, hh, k2 - 2)]
                    elif hh == 1:
                        w = [("qcast", p, 0, 14 + k2)]
                    elif p > 0:
                        w = [("qcast", p - 1, 1, 14 + k2)]
                    else:
                        w = []
                    dma("sync", Bf[:, WQST + b * 512:WQST + (b + 1) * 512].rearrange("p (c m) -> p c m", m=256),
                        wq_d[k2 * 256:(k2 + 1) * 256, h * HD:(h + 1) * HD].rearrange("(c p) m -> p c m", p=128),
                        waits=w, ms=("qwd", p, hh, k2))
                    b4 = k2 % 4
                    wc = [("qwd", p, hh, k2)]
                    if k2 >= 4:
                        wc.append(("qmm", p, hh, k2 - 4))
                    elif hh == 1:
                        wc.append(("qmm", p, 0, 12 + k2))
                    elif p > 0:
                        wc.append(("qmm", p - 1, 1, 12 + k2))
                    op("vector",
                       lambda e, b=b, b4=b4: e.tensor_copy(Bb[:, WQBF + b4 * 512:WQBF + (b4 + 1) * 512],
                                                           Bf[:, WQST + b * 512:WQST + (b + 1) * 512]),
                       waits=wc, ms=("qcast", p, hh, k2))
                    for sub in range(2):
                        kk = 2 * k2 + sub
                        coff = 0
                        for n, csz in enumerate(qsplit):
                            for m in range(2):
                                wmm = [("qcast", p, hh, k2), ("xcast", kk, 0), ("xcast", kk, 1)]
                                if kk == 0:
                                    wmm += bwait(m * nqc + n)
                                op("tensor",
                                   lambda e, b4=b4, sub=sub, m=m, n=n, kk=kk, coff=coff, csz=csz: e.matmul(
                                       bank(m * nqc + n)[:, 0:csz],
                                       Bb[:, WQBF + b4 * 512 + sub * 256 + m * 128:
                                          WQBF + b4 * 512 + sub * 256 + (m + 1) * 128],
                                       A[:, kk * KWA + QB0 * 128 + coff:kk * KWA + QB0 * 128 + coff + csz],
                                       start=(kk == 0), stop=(kk == KD - 1)),
                                   waits=wmm if (kk == 0 or (m == 0 and n == 0)) else (),
                                   ms=("qmm", p, hh, k2) if (m == 1 and n == nqc - 1 and sub == 1) else None)
                            coff += csz
                for m in range(2):
                    coff = 0
                    for n, csz in enumerate(qsplit):
                        w = [("qmm", p, hh, 15)]
                        if hh == 1:
                            w.append(("qspill", p, 0))
                        elif p > 0:
                            w.append(("qspill", p - 1, 1))
                        mskey = ("qevb", p, hh, m * nqc + n)
                        op("vector",
                           lambda e, m=m, coff=coff, csz=csz, n=n: e.tensor_scalar_mul(
                               Bb[:, QSTG + m * QW + coff:QSTG + m * QW + coff + csz],
                               bank(m * nqc + n)[:, 0:csz], SCL),
                           waits=w, ms=mskey)
                        bankfree[m * nqc + n] = mskey
                        coff += csz
                dma("gpsimd",
                    qspill[h][:, 0:QW].rearrange("(c p) t -> p c t", p=128),
                    Bb[:, QSTG:QSTG + 2 * QW].rearrange("p (c t) -> p c t", t=QW),
                    waits=[("qevb", p, hh, m * nqc + n) for m in range(2) for n in range(nqc)],
                    ms=("qspill", p, hh))

            for hh in range(2):
                h = h0 + hh
                for k2 in range(16):
                    b = k2 % 2
                    if k2 >= 2:
                        w = [("kcast", p, hh, k2 - 2)]
                    elif hh == 1:
                        w = [("kcast", p, 0, 14 + k2)]
                    elif p > 0:
                        w = [("kcast", p - 1, 1, 14 + k2)]
                    else:
                        w = []
                    dma("sync", Bf[:, WKST + b * 512:WKST + (b + 1) * 512].rearrange("p (c m) -> p c m", m=256),
                        wk_d[k2 * 256:(k2 + 1) * 256, h * HD:(h + 1) * HD].rearrange("(c p) m -> p c m", p=128),
                        waits=w, ms=("kwd", p, hh, k2))
                    b4 = k2 % 4
                    wc = [("kwd", p, hh, k2)]
                    if k2 >= 4:
                        wc.append(("kmm", p, hh, k2 - 4))
                    elif hh == 1:
                        wc.append(("kmm", p, 0, 12 + k2))
                    elif p > 0:
                        wc.append(("kmm", p - 1, 1, 12 + k2))
                    op("vector",
                       lambda e, b=b, b4=b4: e.tensor_copy(Bb[:, WKBF + b4 * 512:WKBF + (b4 + 1) * 512],
                                                           Bf[:, WKST + b * 512:WKST + (b + 1) * 512]),
                       waits=wc, ms=("kcast", p, hh, k2))
                    for sub in range(2):
                        kk = 2 * k2 + sub
                        coff = 0
                        for n, csz in enumerate(ksplit):
                            for m in range(2):
                                wmm = [("kcast", p, hh, k2), ("xcast", kk, 0), ("xcast", kk, 1)]
                                if kk == 0:
                                    wmm += bwait(m * nkc + n)
                                op("tensor",
                                   lambda e, b4=b4, sub=sub, m=m, n=n, kk=kk, coff=coff, csz=csz: e.matmul(
                                       bank(m * nkc + n)[:, 0:csz],
                                       Bb[:, WKBF + b4 * 512 + sub * 256 + m * 128:
                                          WKBF + b4 * 512 + sub * 256 + (m + 1) * 128],
                                       A[:, kk * KWA + coff:kk * KWA + coff + csz],
                                       start=(kk == 0), stop=(kk == KD - 1)),
                                   waits=wmm if (kk == 0 or (m == 0 and n == 0)) else (),
                                   ms=("kmm", p, hh, k2) if (m == 1 and n == nkc - 1 and sub == 1) else None)
                            coff += csz
                for m in range(2):
                    coff = 0
                    for n, csz in enumerate(ksplit):
                        w = [("kmm", p, hh, 15)]
                        if hh == 1:
                            w.append(("kspill", p, 0))
                        elif p > 0:
                            w.append(("kspill", p - 1, 1))
                        mskey = ("kevb", p, hh, m * nkc + n)
                        op("vector",
                           lambda e, m=m, coff=coff, csz=csz, n=n: e.tensor_copy(
                               Bb[:, KSTG + m * KWA + coff:KSTG + m * KWA + coff + csz],
                               bank(m * nkc + n)[:, 0:csz]),
                           waits=w, ms=mskey)
                        bankfree[m * nkc + n] = mskey
                        coff += csz
                dma("gpsimd",
                    kspill[h][:, 0:KWA].rearrange("(c p) t -> p c t", p=128),
                    Bb[:, KSTG:KSTG + 2 * KWA].rearrange("p (c t) -> p c t", t=KWA),
                    waits=[("kevb", p, hh, m * nkc + n) for m in range(2) for n in range(nkc)],
                    ms=("kspill", p, hh))

            for ps, (ca, cb) in enumerate(vpasses):
                npc = cb - ca
                for k2 in range(16):
                    b = k2 % 2
                    if k2 >= 2:
                        w = [("vcast", p, ps, k2 - 2)]
                    elif ps > 0:
                        w = [("vcast", p, ps - 1, 14 + k2)]
                    elif p > 0:
                        w = [("vcast", p - 1, NVP - 1, 14 + k2)]
                    else:
                        w = []
                    dma("sync", Bf[:, WVST + b * 1024:WVST + (b + 1) * 1024].rearrange("p (c m) -> p c m", m=512),
                        wv_d[k2 * 256:(k2 + 1) * 256, h0 * HD:h0 * HD + 512].rearrange("(c p) m -> p c m", p=128),
                        waits=w, ms=("vwd", p, ps, k2))
                    b4 = k2 % 4
                    wc = [("vwd", p, ps, k2)]
                    if k2 >= 4:
                        wc.append(("vmm", p, ps, k2 - 4))
                    elif ps > 0:
                        wc.append(("vmm", p, ps - 1, 12 + k2))
                    elif p > 0:
                        wc.append(("vmm", p - 1, NVP - 1, 12 + k2))
                    op("vector",
                       lambda e, b=b, b4=b4: e.tensor_copy(Bb[:, WVBF + b4 * 1024:WVBF + (b4 + 1) * 1024],
                                                           Bf[:, WVST + b * 1024:WVST + (b + 1) * 1024]),
                       waits=wc, ms=("vcast", p, ps, k2))
                    for sub in range(2):
                        kk = 2 * k2 + sub
                        for bidx in range(npc):
                            c = ca + bidx
                            wmm = [("vcast", p, ps, k2), ("xcast", kk, 0), ("xcast", kk, 1)]
                            if kk == 0:
                                wmm += bwait(bidx)
                            op("tensor",
                               lambda e, b4=b4, sub=sub, c=c, bidx=bidx, kk=kk: e.matmul(
                                   bank(bidx),
                                   A[:, kk * KWA + c * 128:kk * KWA + (c + 1) * 128],
                                   Bb[:, WVBF + b4 * 1024 + sub * 512:WVBF + b4 * 1024 + (sub + 1) * 512],
                                   start=(kk == 0), stop=(kk == KD - 1)),
                               waits=wmm if (kk == 0 or bidx == 0) else (),
                               ms=("vmm", p, ps, k2) if (bidx == npc - 1 and sub == 1) else None)
                for bidx in range(npc):
                    w = [("vmm", p, ps, 15)]
                    if ps > 0:
                        w += [("vspill", p, ps - 1, 0), ("vspill", p, ps - 1, 1)]
                    elif p > 0:
                        w += [("vspill", p - 1, NVP - 1, 0), ("vspill", p - 1, NVP - 1, 1)]
                    mskey = ("vevb", p, ps, bidx)
                    op("vector",
                       lambda e, bidx=bidx: e.tensor_copy(
                           Bb[:, VSTG + bidx * 512:VSTG + (bidx + 1) * 512], bank(bidx)),
                       waits=w, ms=mskey)
                    bankfree[bidx] = mskey
                for hh in range(2):
                    h = h0 + hh
                    dma("gpsimd",
                        vspill[h].rearrange("p (c d) -> p c d", d=HD)[:, ca:cb, :],
                        Bb[:, VSTG:VSTG + npc * 512].rearrange("p (c two) -> p c two", two=512)[:, :, hh * HD:(hh + 1) * HD],
                        waits=[("vevb", p, ps, i) for i in range(npc)],
                        ms=("vspill", p, ps, hh))

        p1done = ("vspill", 7, NVP - 1, 1)
        p1all = [("vspill", 7, NVP - 1, 0), ("vspill", 7, NVP - 1, 1)]
        npc_last = vpasses[-1][1] - vpasses[-1][0]
        E.set_fence([("vmm", 7, NVP - 1, 15), ("vevb", 7, NVP - 1, npc_last - 1),
                     ("xcast", 31, 0), ("xcast", 31, 1)] + dma_tail("sync") + dma_tail("gpsimd"))

        SETSZ = 2 * QW + 2 * KWA + NKV * 256
        SETA0 = 32 * QW
        ANW = (QB0 + NQ) * 128
        ANB = SETA0 + 2 * SETSZ
        assert ANB + 6 * ANW <= 65536, (ANB, ANW)
        SET3b = 0
        SWQb = SET3b + SETSZ * 2
        SWKb = SWQb + QW * 2
        SCQb = SWKb + KWA * 2
        SCKb = SCQb + QW * 2
        AUBb = SCKb + KWA * 2
        SVBb = AUBb + 3 * ANW * 2
        assert SVBb + 3 * 16 * 4 <= 64512, SVBb
        SET3 = SET3b // 2
        SWQ, SWK, SCQ, SCK = SWQb // 2, SWKb // 2, SCQb // 2, SCKb // 2
        AUB0 = AUBb // 2
        SVB0 = SVBb // 4

        grp_hist = []

        def qkvset(h):
            si = h % 3
            if si < 2:
                return A, SETA0 + si * SETSZ
            return Bb, SET3

        def emit_head_loads(h):
            AR, qb = qkvset(h)
            kb = qb + 2 * QW
            vb = kb + 2 * KWA
            wl = [("qspill", h // 2, h % 2), ("kspill", h // 2, h % 2)] + \
                 [("vspill", h // 2, psx, h % 2) for psx in range(NVP)] + p1all
            if h >= 3:
                wl.append(("avmm", (h - 3) * NQ + NQ - 1))
            dma("gpsimd", AR[:, qb:qb + 2 * QW].rearrange("p (c t) -> p c t", t=QW),
                qspill[h][:, 0:QW].rearrange("(c p) t -> p c t", p=128),
                waits=wl, ms=("ldq", h))
            dma("gpsimd", AR[:, kb:kb + 2 * KWA].rearrange("p (c t) -> p c t", t=KWA),
                kspill[h][:, 0:KWA].rearrange("(c p) t -> p c t", p=128),
                waits=wl, ms=("ldk", h))
            dma("gpsimd", AR[:, vb:vb + NKV * 256],
                vspill[h][:, 0:NKV * 256], waits=wl, ms=("ldv", h))
            wr = [("ldq", h)] + ([("ropeq", h - 1)] if h >= 1 else [])
            dma("gpsimd", Bb[0:32, SWQ:SWQ + QW], AR[32:64, qb:qb + QW], waits=wr, ms=("swq1", h))
            dma("gpsimd", Bb[32:64, SWQ:SWQ + QW], AR[0:32, qb:qb + QW], waits=wr, ms=("swq2", h))
            wr = [("ldk", h)] + ([("ropek", h - 1)] if h >= 1 else [])
            dma("gpsimd", Bb[0:32, SWK:SWK + KWA], AR[32:64, kb:kb + KWA], waits=wr, ms=("swk1", h))
            dma("gpsimd", Bb[32:64, SWK:SWK + KWA], AR[0:32, kb:kb + KWA], waits=wr, ms=("swk2", h))
            wv_ = [("swq1", h), ("swq2", h), ("tabq",), ("tabqs",)] + ([("ropeq", h - 1)] if h >= 1 else [])
            op("gpsimd", lambda e, AR=AR, qb=qb: e.tensor_mul(Bb[0:64, SCQ:SCQ + QW],
                                                              AR[0:64, qb:qb + QW], tabqc[:, 0:QW]),
               waits=wv_, ms=("rq1", h))
            op("gpsimd", lambda e: e.tensor_mul(Bb[0:64, SWQ:SWQ + QW],
                                                Bb[0:64, SWQ:SWQ + QW], tabqs[:, 0:QW]),
               waits=[("rq1", h)] + wv_, ms=("rq2", h))
            op("gpsimd", lambda e, AR=AR, qb=qb: e.tensor_add(AR[0:64, qb:qb + QW],
                                                              Bb[0:64, SCQ:SCQ + QW],
                                                              Bb[0:64, SWQ:SWQ + QW]),
               waits=[("rq1", h), ("rq2", h)], ms=("ropeq", h))
            wv_ = [("swk1", h), ("swk2", h), ("tabk",), ("tabks",)] + ([("ropek", h - 1)] if h >= 1 else [])
            op("gpsimd", lambda e, AR=AR, kb=kb: e.tensor_mul(Bb[0:64, SCK:SCK + KWA],
                                                              AR[0:64, kb:kb + KWA], tabkc[:, 0:KWA]),
               waits=wv_, ms=("rk1", h))
            op("gpsimd", lambda e: e.tensor_mul(Bb[0:64, SWK:SWK + KWA],
                                                Bb[0:64, SWK:SWK + KWA], tabks[:, 0:KWA]),
               waits=[("rk1", h)] + wv_, ms=("rk2", h))
            op("gpsimd", lambda e, AR=AR, kb=kb: e.tensor_add(AR[0:64, kb:kb + KWA],
                                                              Bb[0:64, SCK:SCK + KWA],
                                                              Bb[0:64, SWK:SWK + KWA]),
               waits=[("rk1", h), ("rk2", h)], ms=("ropek", h))

        pl_bank_last = {i: bankfree[i] for i in range(4)}
        tk_base = {}
        pl_flip = [0]

        def pl_banks(t):
            jl = t % NQ
            W = (QB0 + jl + 1) * 128
            b0 = tk_base[t]
            return list(range(b0 // 512, (b0 + W + 511) // 512))

        def emit_lg(t):
            h, jl = divmod(t, NQ)
            klen = QB0 + jl + 1
            W = klen * 128
            if role == 0 and klen <= 8:
                b0 = pl_flip[0] * 1024
                pl_flip[0] ^= 1
            else:
                b0 = 0
            tk_base[t] = b0
            AR, qb = qkvset(h)
            kb = qb + 2 * QW
            wlog = [("ropeq", h), ("ropek", h), ("d_maska",), ("d_ident",)]
            if t == 0:
                wlog.append(p1done)
            for bi in pl_banks(t):
                if pl_bank_last[bi] is not None:
                    wlog.append(pl_bank_last[bi])
            first = True
            coff = 0
            chunks = _split512(W)
            for ci, csz in enumerate(chunks):
                isdiag = (ci == len(chunks) - 1)
                for m in range(2):
                    stop = (m == 1 and not isdiag)
                    op("tensor",
                       lambda e, m=m, coff=coff, csz=csz, qb=qb, kb=kb, jl=jl,
                       stop=stop, AR=AR, b0=b0: e.matmul(
                           PL[:, b0 + coff:b0 + coff + csz],
                           AR[:, qb + m * QW + jl * 128:qb + m * QW + (jl + 1) * 128],
                           AR[:, kb + m * KWA + coff:kb + m * KWA + coff + csz],
                           start=(m == 0), stop=stop),
                       waits=wlog if first else (),
                       ms=("lgh", t) if (stop and ci == len(chunks) - 2) else None)
                    first = False
                if isdiag:
                    op("tensor",
                       lambda e, W=W, b0=b0: e.matmul(
                           PL[:, b0 + W - 128:b0 + W], identb[:, :], maska[:, :],
                           start=False, stop=True),
                       ms=("lgp", t))
                coff += csz

        def emit_mx(t):
            h, jl = divmod(t, NQ)
            W = (QB0 + jl + 1) * 128
            b0 = tk_base[t]
            SVX = SVB0 + 16 * (t % 3)
            nch = len(_split512(W))
            if nch == 1:
                wmx = [("lgp", t)]
                if t >= 3:
                    wmx.append(("exp", t - 3))
                op("vector", lambda e, W=W, SVX=SVX, b0=b0: e.reduce_max(
                       Bf[:, SVX:SVX + 1], PL[:, b0:b0 + W], axis=AX.X, negate=True),
                   waits=wmx, ms=("mx", t))
                return
            W0 = (nch - 1) * 512
            wmx = [("lgh", t)]
            if t >= 3:
                wmx.append(("mx", t - 3))
            op("vector", lambda e, W0=W0, SVX=SVX, b0=b0: e.reduce_max(
                   Bf[:, SVX + 1:SVX + 2], PL[:, b0:b0 + W0], axis=AX.X, negate=True),
               waits=wmx, ms=("mxa", t))
            op("vector", lambda e, W0=W0, W=W, SVX=SVX, b0=b0: e.reduce_max(
                   Bf[:, SVX + 2:SVX + 3], PL[:, b0 + W0:b0 + W], axis=AX.X, negate=True),
               waits=[("lgp", t)], ms=("mxb", t))
            wcmb = [("mxa", t), ("mxb", t)]
            if t >= 3:
                wcmb.append(("exp", t - 3))
            op("vector", lambda e, SVX=SVX: e.tensor_tensor(
                   Bf[:, SVX:SVX + 1], Bf[:, SVX + 1:SVX + 2],
                   Bf[:, SVX + 2:SVX + 3], ALU.min),
               waits=wcmb, ms=("mx", t))

        def emit_exp(t):
            h, jl = divmod(t, NQ)
            gp = t % 3
            W = (QB0 + jl + 1) * 128
            b0 = tk_base[t]
            SVX = SVB0 + 16 * gp
            AUB = AUB0 + ANW * gp
            wexp = [("mx", t)]
            if t >= 3:
                wexp += [("nrm", t - 3), ("rcp", t - 3)]
            op("scalar", lambda e, W=W, SVX=SVX, AUB=AUB, b0=b0: e.activation(
                   Bb[:, AUB:AUB + W], PL[:, b0:b0 + W], AF.Exp,
                   bias=Bf[:, SVX:SVX + 1], scale=1.0,
                   accum_out=Bf[:, SVX + 5:SVX + 6]),
               waits=wexp, ms=("exp", t))
            for bi in pl_banks(t):
                pl_bank_last[bi] = ("exp", t)

        def emit_smx_back(t):
            h, jl = divmod(t, NQ)
            gp = t % 3
            W = (QB0 + jl + 1) * 128
            SVX = SVB0 + 16 * gp
            AUB = AUB0 + ANW * gp
            AN = ANB + ANW * gp
            op("vector", lambda e, SVX=SVX: e.reciprocal(
                   Bf[:, SVX + 6:SVX + 7], Bf[:, SVX + 5:SVX + 6]),
               waits=[("exp", t)], ms=("rcp", t))
            wn = [("rcp", t)]
            if t >= 3:
                wn.append(("trl", t - 3))
            op("vector", lambda e, W=W, AUB=AUB, AN=AN, SVX=SVX: e.tensor_scalar_mul(
                   A[:, AN:AN + W], Bb[:, AUB:AUB + W], Bf[:, SVX + 6:SVX + 7]),
               waits=wn, ms=("nrm", t))

        def emit_tr(t):
            h, jl = divmod(t, NQ)
            klen = QB0 + jl + 1
            gp = t % 3
            AN = ANB + ANW * gp
            AT = ANB + 3 * ANW + ANW * gp
            ngrp = (klen + 3) // 4
            for g in range(ngrp):
                kks = list(range(4 * g, min(4 * g + 4, klen)))
                tb = len(grp_hist) % 2
                wtr = [("nrm", t)]
                if len(grp_hist) >= 2:
                    wtr.append(grp_hist[-2])
                elif bankfree[4 + tb] is not None:
                    wtr.append(bankfree[4 + tb])
                for i, kk in enumerate(kks):
                    op("tensor",
                       lambda e, tb=tb, kk=kk, i=i, AN=AN: e.transpose(
                           trP[tb][:, i * 128:(i + 1) * 128],
                           A[:, AN + kk * 128:AN + (kk + 1) * 128], identb[:, :]),
                       waits=wtr if i == 0 else (),
                       ms=("trl", t) if kk == klen - 1 else ("trx", t, kk))
                wge = [("trl", t) if kks[-1] == klen - 1 else ("trx", t, kks[-1])]
                if t >= 3:
                    wge.append(("avmm", t - 3))
                gkey = ("grpev", t, g)
                op("vector",
                   lambda e, tb=tb, g=g, n=len(kks), AT=AT: e.tensor_copy(
                       A[:, AT + g * 512:AT + g * 512 + n * 128],
                       trP[tb][:, 0:n * 128]),
                   waits=wge, ms=gkey)
                grp_hist.append(gkey)

        def emit_av(t):
            h, jl = divmod(t, NQ)
            klen = QB0 + jl + 1
            gp = t % 3
            AR, qb = qkvset(h)
            vb = qb + 2 * QW + 2 * KWA
            AT = ANB + 3 * ANW + ANW * gp
            for kk in range(klen):
                wav = [("grpev", t, kk // 4)]
                if kk == 0:
                    wav.append(("avev", t - 1) if t > 0 else p1done)
                    wav.append(("ldv", h))
                    if t == 0:
                        wav += bwait(6) + bwait(7)
                for dh in range(2):
                    op("tensor",
                       lambda e, dh=dh, kk=kk, klen=klen, vb=vb, AT=AT, AR=AR: e.matmul(
                           avP[dh][:, 0:128],
                           AR[:, vb + kk * 256 + dh * 128:vb + kk * 256 + (dh + 1) * 128],
                           A[:, AT + kk * 128:AT + (kk + 1) * 128],
                           start=(kk == 0), stop=(kk == klen - 1)),
                       waits=wav if (dh == 0 or kk == 0) else (),
                       ms=("avmm", t) if (kk == klen - 1 and dh == 1) else None)

        def emit_avev(t):
            h, jl = divmod(t, NQ)
            for dh in range(2):
                op("scalar",
                   lambda e, dh=dh, h=h, jl=jl: e.copy(
                       A[:, (2 * h + dh) * QW + jl * 128:(2 * h + dh) * QW + (jl + 1) * 128],
                       avP[dh][:, 0:128]),
                   waits=[("avmm", t)], ms=("avev", t) if dh == 1 else None)

        emit_head_loads(0)
        emit_head_loads(1)
        emit_head_loads(2)
        for u in range(T2 + 2):
            if u < T2:
                h, jl = divmod(u, NQ)
                if jl == 3 and 3 <= h + 2 < NH:
                    emit_head_loads(h + 2)
                emit_lg(u)
                emit_mx(u)
            if 0 <= u - 1 < T2:
                emit_smx_back(u - 1)
            if u < T2:
                emit_exp(u)
            if u - 2 >= 0:
                emit_av(u - 2)
            if 0 <= u - 1 < T2:
                emit_tr(u - 1)
            if u - 2 >= 0:
                emit_avev(u - 2)

        p2done = ("avev", T2 - 1)
        bankfree = [None] * 8
        E.set_fence([("avmm", T2 - 1), p2done, grp_hist[-1],
                     ("ropeq", NH - 1), ("ropek", NH - 1)]
                    + dma_tail("sync") + dma_tail("gpsimd"))

        WOSTb4 = 0
        WOBFb4 = WOSTb4 + 2 * 1024 * 4
        ZSTb4 = WOBFb4 + 16384 * 2
        XRSb4 = ZSTb4 + 4 * 512 * 4
        assert XRSb4 + 4 * 512 * 4 <= 64512
        WOST4, WOBF4 = WOSTb4 // 4, WOBFb4 // 2
        ZST4, XRS4 = ZSTb4 // 4, XRSb4 // 4
        groupsM = [list(range(0, min(NM, 8)))] + ([list(range(8, NM))] if NM > 8 else [])
        NG = len(groupsM)
        zlast = {}
        xlast = {}

        def wo_half_free(nn, half):
            if nn == 0:
                return None
            return ("wommH0", nn - 1, NG - 1) if half == 0 else ("womm", nn - 1, NG - 1)

        for nn in range(8):
            for k2 in range(16):
                b = k2 % 2
                half = k2 // 8
                if k2 >= 2:
                    w = [("wocast", nn, k2 - 2)]
                elif nn > 0:
                    w = [("wocast", nn - 1, 14 + k2)]
                else:
                    w = [p2done]
                dma("sync", Bf[:, WOST4 + b * 1024:WOST4 + (b + 1) * 1024].rearrange("p (c m) -> p c m", m=512),
                    wo_d[k2 * 256:(k2 + 1) * 256, nn * 512:(nn + 1) * 512].rearrange("(c p) m -> p c m", p=128),
                    waits=w, ms=("wod", nn, k2))
                wc = [("wod", nn, k2)]
                whf = wo_half_free(nn, half)
                if whf is not None:
                    wc.append(whf)
                if nn == 0 and k2 < 2:
                    wc.append(p2done)
                op("scalar",
                   lambda e, b=b, k2=k2: e.copy(Bb[:, WOBF4 + k2 * 1024:WOBF4 + (k2 + 1) * 1024],
                                                Bf[:, WOST4 + b * 1024:WOST4 + (b + 1) * 1024]),
                   waits=wc, ms=("wocast", nn, k2))
            for gi, ms_ in enumerate(groupsM):
                for kk in range(KD):
                    for ii, m in enumerate(ms_):
                        wmm = [("wocast", nn, kk // 2)]
                        if kk == 0:
                            wmm += bwait(ii) + [p2done]
                        if kk == KD - 1 and ii == len(ms_) - 1:
                            mskey = ("womm", nn, gi)
                        elif kk == 15 and ii == len(ms_) - 1:
                            mskey = ("wommH0", nn, gi)
                        else:
                            mskey = None
                        op("tensor",
                           lambda e, m=m, ii=ii, kk=kk: e.matmul(
                               bank(ii),
                               A[:, kk * QW + m * 128:kk * QW + (m + 1) * 128],
                               Bb[:, WOBF4 + kk * 512:WOBF4 + (kk + 1) * 512],
                               start=(kk == 0), stop=(kk == KD - 1)),
                           waits=wmm if (ii == 0 or kk == 0) else (),
                           ms=mskey)
                for ii, m in enumerate(ms_):
                    zb = m % 4
                    wx = [p2done]
                    if ("x", zb) in xlast:
                        wx.append(xlast[("x", zb)])
                    dma("sync", Bf[:, XRS4 + zb * 512:XRS4 + (zb + 1) * 512],
                        xres_d[m * 128:(m + 1) * 128, nn * 512:(nn + 1) * 512],
                        waits=wx, ms=("xrsd", nn, m))
                    w = [("womm", nn, gi), ("xrsd", nn, m)]
                    if ("z", zb) in zlast:
                        w.append(zlast[("z", zb)])
                    mskey = ("zev", nn, m)
                    op("vector",
                       lambda e, ii=ii, zb=zb: e.tensor_add(
                           Bf[:, ZST4 + zb * 512:ZST4 + (zb + 1) * 512], bank(ii),
                           Bf[:, XRS4 + zb * 512:XRS4 + (zb + 1) * 512]),
                       waits=w, ms=mskey)
                    bankfree[ii] = mskey
                    xlast[("x", zb)] = mskey
                    op("vector",
                       lambda e, m=m, nn=nn, zb=zb: e.bn_stats(
                           stats[:, ST_BN + (m * 8 + nn) * 6:ST_BN + (m * 8 + nn + 1) * 6],
                           Bf[:, ZST4 + zb * 512:ZST4 + (zb + 1) * 512]),
                       waits=[("zev", nn, m)], ms=("bnst", m, nn))
                    dma("gpsimd", z_hbm[m][:, nn * 512:(nn + 1) * 512],
                        Bf[:, ZST4 + zb * 512:ZST4 + (zb + 1) * 512],
                        waits=[("zev", nn, m), ("bnst", m, nn)], ms=("zdma", nn, m))
                    zlast[("z", zb)] = ("zdma", nn, m)

        zall_list = [("zdma", 7, i) for i in range(NM)]
        E.set_fence([("womm", 7, NG - 1), ("bnst", groupsM[-1][-1], 7),
                     ("wocast", 7, 15)] + dma_tail("sync") + dma_tail("gpsimd"))

        TSLb5 = 0
        X2Fb5 = TSLb5 + 4 * 512 * 4
        X2Bb5 = X2Fb5 + 4 * 512 * 4
        TSL5, X2F5 = TSLb5 // 4, X2Fb5 // 4
        X2B5 = X2Bb5 // 2
        def p5_stats(m):
            op("vector",
               lambda e, m=m: e.bn_aggr(stats[:, ST_MV + m * 2:ST_MV + (m + 1) * 2],
                                        stats[:, ST_BN + m * 48:ST_BN + (m + 1) * 48]),
               waits=[("bnst", m, n2) for n2 in range(8)], ms=("mv", m))
            op("scalar",
               lambda e, m=m: e.activation(stats[:, ST_RSTD + m:ST_RSTD + m + 1],
                                           stats[:, ST_MV + m * 2 + 1:ST_MV + m * 2 + 2],
                                           AF.Ln, bias=stats[:, ST_EPS:ST_EPS + 1], scale=1.0),
               waits=[("mv", m), ("epsv",)], ms=("lnv", m))
            op("scalar",
               lambda e, m=m: e.activation(stats[:, ST_RSTD + m:ST_RSTD + m + 1],
                                           stats[:, ST_RSTD + m:ST_RSTD + m + 1],
                                           AF.Exp, bias=0.0, scale=-0.5),
               waits=[("lnv", m)], ms=("rstd", m))
            op("vector",
               lambda e, m=m: e.tensor_mul(stats[:, ST_NMR + m:ST_NMR + m + 1],
                                           stats[:, ST_MV + m * 2:ST_MV + m * 2 + 1],
                                           stats[:, ST_RSTD + m:ST_RSTD + m + 1]),
               waits=[("rstd", m)], ms=("nmr", m))

        def p5_front(m, nn):
            bx = nn % 4
            w = list(zall_list) + [("zdma", 7, m)]
            if nn >= 4:
                w.append(("x2n", m, nn - 4))
            elif m > 0:
                w.append(("x2n", m - 1, nn + 4))
            dma("sync", Bf[:, TSL5 + bx * 512:TSL5 + (bx + 1) * 512],
                z_hbm[m][:, nn * 512:(nn + 1) * 512], waits=w, ms=("zrd", m, nn))
            tb = nn % 4
            wn = [("zrd", m, nn), ("nmr", m)]
            if nn >= 4:
                wn.append(("x2dma", m, nn - 4))
            elif m > 0:
                wn.append(("x2dma", m - 1, nn + 4))
            op("vector",
               lambda e, m=m, bx=bx, tb=tb: e.tensor_scalar(
                   Bf[:, X2F5 + tb * 512:X2F5 + (tb + 1) * 512],
                   Bf[:, TSL5 + bx * 512:TSL5 + (bx + 1) * 512],
                   stats[:, ST_RSTD + m:ST_RSTD + m + 1],
                   stats[:, ST_NMR + m:ST_NMR + m + 1],
                   ALU.mult, ALU.subtract),
               waits=wn, ms=("x2n", m, nn))
            dma("gpsimd", x2_hbm[m][:, nn * 512:(nn + 1) * 512],
                Bf[:, X2F5 + tb * 512:X2F5 + (tb + 1) * 512],
                waits=[("x2n", m, nn)], ms=("x2dma", m, nn))
            wxc = [("x2n", m, nn)]
            if nn >= 4:
                wxc.append(("x2t", m, (nn - 4) * 4 + 3))
            elif m > 0:
                wxc.append(("x2t", m - 1, (nn + 4) * 4 + 3))
            op("scalar",
               lambda e, tb=tb: e.copy(
                   Bb[:, X2B5 + tb * 512:X2B5 + (tb + 1) * 512],
                   Bf[:, X2F5 + tb * 512:X2F5 + (tb + 1) * 512]),
               waits=wxc, ms=("x2c", m, nn))

        def p5_back(m, nn):
            tb = nn % 4
            wrsp = [("x2c", m, nn)]
            if m > 0 and nn == 0:
                wrsp.append(("rs", m - 1))
            op("vector",
               lambda e, nn=nn, tb=tb: e.reduce_sum(
                   stats[:, ST_RSP + nn:ST_RSP + nn + 1],
                   Bb[:, X2B5 + tb * 512:X2B5 + (tb + 1) * 512], axis=AX.X),
               waits=wrsp, ms=("rsp", m, nn))
            pbk = len(grp_hist) % 2
            wtr = [("x2c", m, nn), p2done]
            if len(grp_hist) >= 2:
                wtr.append(grp_hist[-2])
            for t4 in range(4):
                kkc = nn * 4 + t4
                op("tensor",
                   lambda e, pbk=pbk, tb=tb, t4=t4: e.transpose(
                       trP[pbk][:, t4 * 128:(t4 + 1) * 128],
                       Bb[:, X2B5 + tb * 512 + t4 * 128:X2B5 + tb * 512 + (t4 + 1) * 128],
                       identb[:, :]),
                   waits=wtr if t4 == 0 else (), ms=("x2t", m, kkc))
            gkey = ("x2g", m, nn)
            op("vector",
               lambda e, pbk=pbk, m=m, nn=nn: e.tensor_copy(
                   A[:, 0:32 * NM * 128].rearrange("p (c t) -> p c t", t=NM * 128)
                   [:, nn * 4:nn * 4 + 4, m * 128:(m + 1) * 128],
                   trP[pbk][:, 0:512].rearrange("p (c t) -> p c t", t=128)),
               waits=[("x2t", m, nn * 4 + 3)], ms=gkey)
            grp_hist.append(gkey)
            if nn == 7:
                op("vector",
                   lambda e, m=m: e.reduce_sum(stats[:, ST_RS + m:ST_RS + m + 1],
                                               stats[:, ST_RSP:ST_RSP + 8], axis=AX.X),
                   waits=[("rsp", m, n2) for n2 in range(8)], ms=("rs", m))

        chunksP5 = [(m, nn) for m in range(NM) for nn in range(8)]
        for j5 in range(len(chunksP5) + 1):
            if j5 < len(chunksP5):
                m5, nn5 = chunksP5[j5]
                if nn5 == 0:
                    p5_stats(m5)
                p5_front(m5, nn5)
            if j5 >= 1:
                p5_back(*chunksP5[j5 - 1])
        op("vector",
           lambda e: e.tensor_scalar_mul(stats[:, ST_NB:ST_NB + NM], stats[:, ST_RS:ST_RS + NM],
                                         stats[:, ST_NEGFS:ST_NEGFS + 1]),
           waits=[("rs", m2) for m2 in range(NM)] + [("negfs",)], ms=("nb",))

        x2tall = ("x2g", NM - 1, 7)
        bankfree = [None] * 8
        E.set_fence([("x2t", NM - 1, 31), ("nb",), ("x2c", NM - 1, 7)]
                    + dma_tail("sync") + dma_tail("gpsimd"))

        FWSTb6 = 0
        FWBFb6 = FWSTb6 + 2 * 1024
        X2RSb6 = FWBFb6 + 16384 * 2
        Z2STb6 = X2RSb6 + 4 * 512 * 4
        assert Z2STb6 + 4 * 512 * 4 <= 64512
        FWST6 = FWSTb6
        FWBF6 = FWBFb6 // 2
        X2RS6, Z2ST6 = X2RSb6 // 4, Z2STb6 // 4
        zlast = {}
        xlast = {}

        for nn in range(8):
            for k2 in range(16):
                b = k2 % 2
                half = k2 // 8
                if k2 >= 2:
                    w = [("fwcast", nn, k2 - 2)]
                elif nn > 0:
                    w = [("fwcast", nn - 1, 14 + k2)]
                else:
                    w = [("nb",), x2tall]
                dma("sync", Bu[:, FWST6 + b * 1024:FWST6 + (b + 1) * 1024].rearrange("p (c m) -> p c m", m=512),
                    fw_d[k2 * 256:(k2 + 1) * 256, nn * 512:(nn + 1) * 512].rearrange("(c p) m -> p c m", p=128),
                    waits=w, ms=("fwd", nn, k2))
                wc = [("fwd", nn, k2)]
                if nn > 0:
                    wc.append(("fmmH0", nn - 1, NG - 1) if half == 0
                              else ("fmm", nn - 1, NG - 1))
                if nn == 0 and k2 < 2:
                    wc += [("nb",), x2tall]
                op("scalar",
                   lambda e, b=b, k2=k2: e.copy(Bb[:, FWBF6 + k2 * 1024:FWBF6 + (k2 + 1) * 1024],
                                                Bu[:, FWST6 + b * 1024:FWST6 + (b + 1) * 1024]),
                   waits=wc, ms=("fwcast", nn, k2))
            for gi, ms_ in enumerate(groupsM):
                for kk in range(KD):
                    for ii, m in enumerate(ms_):
                        wmm = [("fwcast", nn, kk // 2)]
                        if kk == 0:
                            wmm += bwait(ii) + [x2tall]
                        if kk == KD - 1 and ii == len(ms_) - 1:
                            mskey = ("fmm", nn, gi)
                        elif kk == 15 and ii == len(ms_) - 1:
                            mskey = ("fmmH0", nn, gi)
                        else:
                            mskey = None
                        op("tensor",
                           lambda e, m=m, ii=ii, kk=kk: e.matmul(
                               bank(ii),
                               A[:, kk * NM * 128 + m * 128:kk * NM * 128 + (m + 1) * 128],
                               Bb[:, FWBF6 + kk * 512:FWBF6 + (kk + 1) * 512],
                               start=(kk == 0), stop=(kk == KD - 1)),
                           waits=wmm if (ii == 0 or kk == 0) else (),
                           ms=mskey)
                for ii, m in enumerate(ms_):
                    zb = m % 4
                    wx = [("nb",), x2tall, ("x2dma", m, nn)]
                    if ("x", zb) in xlast:
                        wx.append(xlast[("x", zb)])
                    dma("sync", Bf[:, X2RS6 + zb * 512:X2RS6 + (zb + 1) * 512],
                        x2_hbm[m][:, nn * 512:(nn + 1) * 512],
                        waits=wx, ms=("x2sl", nn, m))
                    op("vector",
                       lambda e, m=m, zb=zb: e.tensor_scalar_add(
                           Bf[:, X2RS6 + zb * 512:X2RS6 + (zb + 1) * 512],
                           Bf[:, X2RS6 + zb * 512:X2RS6 + (zb + 1) * 512],
                           stats[:, ST_NB + m:ST_NB + m + 1]),
                       waits=[("x2sl", nn, m), ("nb",)], ms=("xadj", nn, m))
                    w = [("fmm", nn, gi), ("xadj", nn, m), ("x2sl", nn, m), ("fs2",)]
                    if ("z", zb) in zlast:
                        w.append(zlast[("z", zb)])
                    mskey = ("z2ev", nn, m)
                    op("vector",
                       lambda e, ii=ii, zb=zb: e.scalar_tensor_tensor(
                           Bf[:, Z2ST6 + zb * 512:Z2ST6 + (zb + 1) * 512], bank(ii),
                           stats[:, ST_FS2:ST_FS2 + 1],
                           Bf[:, X2RS6 + zb * 512:X2RS6 + (zb + 1) * 512],
                           ALU.mult, ALU.add),
                       waits=w, ms=mskey)
                    bankfree[ii] = mskey
                    xlast[("x", zb)] = mskey
                    wb = [("z2ev", nn, m)]
                    if nn == 0:
                        wb.append(("mv", NM - 1))
                    op("vector",
                       lambda e, m=m, nn=nn, zb=zb: e.bn_stats(
                           stats[:, ST_BN + (m * 8 + nn) * 6:ST_BN + (m * 8 + nn + 1) * 6],
                           Bf[:, Z2ST6 + zb * 512:Z2ST6 + (zb + 1) * 512]),
                       waits=wb, ms=("bnst2", m, nn))
                    dma("gpsimd", z2_hbm[m][:, nn * 512:(nn + 1) * 512],
                        Bf[:, Z2ST6 + zb * 512:Z2ST6 + (zb + 1) * 512],
                        waits=[("bnst2", m, nn)], ms=("z2dma", nn, m))
                    zlast[("z", zb)] = ("z2dma", nn, m)

        z2all_list = [("z2dma", 7, i) for i in range(NM)]
        E.set_fence([("fmm", 7, NG - 1), ("bnst2", groupsM[-1][-1], 7),
                     ("fwcast", 7, 15)] + dma_tail("sync") + dma_tail("gpsimd"))

        T7b = 0
        O7b = T7b + 4 * 512 * 4
        T7, O7 = T7b // 4, O7b // 4
        assert O7b + 4 * 512 * 4 <= 64512
        for m in range(NM):
            op("vector",
               lambda e, m=m: e.bn_aggr(stats[:, ST_MV + m * 2:ST_MV + (m + 1) * 2],
                                        stats[:, ST_BN + m * 48:ST_BN + (m + 1) * 48]),
               waits=[("bnst2", m, n2) for n2 in range(8)] + [("rstd", NM - 1), ("nmr", NM - 1)],
               ms=("mv2", m))
            op("scalar",
               lambda e, m=m: e.activation(stats[:, ST_RSTD + m:ST_RSTD + m + 1],
                                           stats[:, ST_MV + m * 2 + 1:ST_MV + m * 2 + 2],
                                           AF.Ln, bias=stats[:, ST_EPS:ST_EPS + 1], scale=1.0),
               waits=[("mv2", m), ("epsv",)], ms=("lnv2", m))
            op("scalar",
               lambda e, m=m: e.activation(stats[:, ST_RSTD + m:ST_RSTD + m + 1],
                                           stats[:, ST_RSTD + m:ST_RSTD + m + 1],
                                           AF.Exp, bias=0.0, scale=-0.5),
               waits=[("lnv2", m)], ms=("rstd2", m))
            op("vector",
               lambda e, m=m: e.tensor_mul(stats[:, ST_NMR + m:ST_NMR + m + 1],
                                           stats[:, ST_MV + m * 2:ST_MV + m * 2 + 1],
                                           stats[:, ST_RSTD + m:ST_RSTD + m + 1]),
               waits=[("rstd2", m)], ms=("nmr2", m))
            for nn in range(8):
                bx = nn % 4
                w = list(z2all_list) + [("z2dma", 7, m)]
                if nn >= 4:
                    w.append(("t2n", m, nn - 4))
                elif m > 0:
                    w.append(("t2n", m - 1, nn + 4))
                dma("sync", Bf[:, T7 + bx * 512:T7 + (bx + 1) * 512],
                    z2_hbm[m][:, nn * 512:(nn + 1) * 512], waits=w, ms=("z2rd", m, nn))
                tb = nn % 4
                wn = [("z2rd", m, nn), ("nmr2", m)]
                if nn >= 4:
                    wn.append(("outdma", m, nn - 4))
                elif m > 0:
                    wn.append(("outdma", m - 1, nn + 4))
                op("vector",
                   lambda e, m=m, bx=bx, tb=tb: e.tensor_scalar(
                       Bf[:, O7 + tb * 512:O7 + (tb + 1) * 512],
                       Bf[:, T7 + bx * 512:T7 + (bx + 1) * 512],
                       stats[:, ST_RSTD + m:ST_RSTD + m + 1],
                       stats[:, ST_NMR + m:ST_NMR + m + 1],
                       ALU.mult, ALU.subtract),
                   waits=wn, ms=("t2n", m, nn))
                dma("gpsimd", out_d[m * 128:(m + 1) * 128, nn * 512:(nn + 1) * 512],
                    Bf[:, O7 + tb * 512:O7 + (tb + 1) * 512],
                    waits=[("t2n", m, nn)], ms=("outdma", m, nn))

        for i in range(QK["sync"]):
            E.final_wait("sync", f"s{i}")
        for i in range(QK["gpsimd"]):
            E.final_wait("gpsimd", f"g{i}")

    E0.begin_count()
    walk(0, E0)
    E1.begin_count()
    walk(1, E1)

    sems = {}
    with ExitStack() as semstack:
        for name in (["t", "v", "s", "p", "rr"] + [f"s{i}" for i in range(12)] + [f"g{i}" for i in range(8)]):
            sems[name] = semstack.enter_context(nc.semaphore(f"sem_{name}"))

        def section(engname):
            def body(eng):
                if engname == "sync":
                    eng.dma_start(out=rolesb[:, :], in_=role_d).then_inc(sems["rr"], 16)
                eng.wait_ge(sems["rr"], 16)
                with eng.register(f"role_{engname}") as r:
                    eng.reg_load(r, rolesb[0:1, 0:1])
                    for i in nc.Switch(engines=[eng], index=[eng.snap(r)], n=2):
                        EE = E0 if i == 0 else E1
                        EE.begin_emit(engname, eng, sems)
                        walk(i, EE)
            return body

        with nc.Block() as block:
            block.sync(section("sync"))
            block.gpsimd(section("gpsimd"))
            block.tensor(section("tensor"))
            block.vector(section("vector"))
            block.scalar(section("scalar"))
    es.close()
    return nc



def _rot_perm():
    p = np.arange(D)
    for h in range(NH):
        base = h * HD
        p[base:base + 64] = np.concatenate([np.arange(0, 64, 2), np.arange(1, 64, 2)]) + base
    return p


def _tables(idx, width):
    inv_freq = (1.0 / (10000.0 ** (np.arange(0, 64, 2, dtype=np.float32) / np.float32(64.0)))).astype(np.float32)
    t = idx.astype(np.float32)
    sinus = np.einsum("i,j->ij", t, inv_freq).astype(np.float32)
    sn = np.sin(sinus).astype(np.float32).T
    cs = np.cos(sinus).astype(np.float32).T
    out = np.zeros((128, width), np.float32)
    n = len(idx)
    out[0:32, 0:n] = cs
    out[32:64, 0:n] = cs
    out[64:96, 0:n] = sn
    out[96:128, 0:n] = sn
    return out.astype(BFNP)


def _in_maps(x, wq, wk, wv, wo, fw, fs):
    ident = np.eye(128, dtype=np.float32).astype(BFNP)
    mval = np.float32(NEGV / 16.0)
    tri = np.where(np.arange(128)[:, None] >= np.arange(128)[None, :],
                   np.float32(0.0), mval).astype(BFNP)
    NMXT = ACUT * 128
    maps = []
    for c in range(8):
        b, role = c // 2, c % 2
        if role == 0:
            myidx = np.arange(0, ACUT * 128)
        else:
            myidx = np.arange(ACUT * 128, S)
        xres = np.zeros((NMXT, D), np.float32)
        xres[0:len(myidx)] = x[b][myidx]
        maps.append({
            "xT": np.ascontiguousarray(x[b].T),
            "xres": xres,
            "wq": wq, "wk": wk, "wv": wv, "wo": wo, "fw": fw, "fs": fs,
            "rolein": np.array([[role]], dtype=np.uint32),
            "tabq": _tables(myidx, NMXT), "tabk": _tables(np.arange(S), S),
            "maska": tri, "ident": ident,
        })
    return maps


def kernel(L0_attn_hidden_states, L0_attn_Wq, L0_attn_Wk, L0_attn_Wv, L0_attn_Wo,
           L0_ffn_kernel, L0_ffn_scale):
    x = np.asarray(L0_attn_hidden_states)
    perm = _rot_perm()
    wq = np.ascontiguousarray(np.asarray(L0_attn_Wq)[:, perm])
    wk = np.ascontiguousarray(np.asarray(L0_attn_Wk)[:, perm])
    wv = np.ascontiguousarray(np.asarray(L0_attn_Wv))
    wo = np.ascontiguousarray(np.asarray(L0_attn_Wo))
    fw = np.ascontiguousarray(np.asarray(L0_ffn_kernel).view(np.uint8))
    fs = np.asarray(L0_ffn_scale, dtype=np.float32).reshape(1, 1)

    nc = build_nc()
    in_maps = _in_maps(x, wq, wk, wv, wo, fw, fs)
    res = run_bass_kernel_spmd(nc, in_maps, core_ids=list(range(8)))
    out = np.empty((4, S, D), np.float32)
    for c in range(8):
        b, role = c // 2, c % 2
        if role == 0:
            out[b, 0:ACUT * 128, :] = res.results[c]["out"][0:ACUT * 128]
        else:
            out[b, ACUT * 128:S, :] = res.results[c]["out"][0:S - ACUT * 128]
    return out


# version 16
# speedup vs baseline: 1.0412x; 1.0038x over previous
from contextlib import ExitStack

import numpy as np
import ml_dtypes

import concourse.bass as bass
import concourse.mybir as mybir
from concourse.bass_utils import run_bass_kernel_spmd

F32 = mybir.dt.float32
BF16 = mybir.dt.bfloat16
U8 = mybir.dt.uint8
U32 = mybir.dt.uint32
AF = mybir.ActivationFunctionType
ALU = mybir.AluOpType
AX = mybir.AxisListType

D = 4096
S = 2048
NH = 16
HD = 256
KD = D // 128
ACUT = 10
EPS = 1e-5
NEGV = -1e9
SCL = 1.0 / 16.0

BFNP = np.dtype(ml_dtypes.bfloat16)


def _split512(n):
    out = []
    while n > 0:
        c = min(n, 512)
        out.append(c)
        n -= c
    return out


class Sched:
    def __init__(self):
        self.mode = "count"
        self.cnt = {}
        self.ms = {}
        self.eng = None
        self.engname = None
        self.sems = None
        self.waited = None
        self.fence = {}

    def begin_count(self):
        self.mode = "count"
        self.cnt = {}
        self.ms = {}
        self.fence = {}

    def begin_emit(self, engname, eng, sems):
        self.mode = "emit"
        self.engname = engname
        self.eng = eng
        self.sems = sems
        self.waited = {}
        self.fence = {}

    def set_fence(self, keys):
        for en in ("sync", "gpsimd", "tensor", "vector", "scalar"):
            self.fence[en] = list(keys)

    def op(self, engname, fn, waits=(), inc=None, ms=None):
        pend = self.fence.pop(engname, None)
        if pend:
            waits = list(waits) + pend
        if self.mode == "count":
            for k in waits:
                assert k in self.ms, f"wait on undefined milestone {k}"
            if inc is not None:
                s, n = inc
                self.cnt[s] = self.cnt.get(s, 0) + n
                if ms is not None:
                    for k in (ms if isinstance(ms, list) else [ms]):
                        assert k not in self.ms, f"dup milestone {k}"
                        self.ms[k] = (s, self.cnt[s])
            else:
                assert ms is None, ms
            return
        if engname != self.engname:
            return
        for k in waits:
            s, v = self.ms[k]
            if v > self.waited.get(s, 0):
                self.eng.wait_ge(self.sems[s], v)
                self.waited[s] = v
        instr = fn(self.eng)
        if inc is not None:
            s, n = inc
            instr.then_inc(self.sems[s], n)

    def final_wait(self, engname, semname):
        if self.mode == "count" or engname != self.engname:
            return
        v = self.cnt.get(semname, 0)
        if v > self.waited.get(semname, 0):
            self.eng.wait_ge(self.sems[semname], v)
            self.waited[semname] = v


def build_nc():
    nc = bass.Bass("TRN2", target_bir_lowering=False)

    NQR = [ACUT, NH - ACUT]
    NKVR = [ACUT, 16]
    QB0R = [0, ACUT]
    NMX = max(NQR)

    xT_d = nc.dram_tensor("xT", [D, S], F32, kind="ExternalInput").ap()
    xres_d = nc.dram_tensor("xres", [NMX * 128, D], F32, kind="ExternalInput").ap()
    wq_d = nc.dram_tensor("wq", [D, D], F32, kind="ExternalInput").ap()
    wk_d = nc.dram_tensor("wk", [D, D], F32, kind="ExternalInput").ap()
    wv_d = nc.dram_tensor("wv", [D, D], F32, kind="ExternalInput").ap()
    wo_d = nc.dram_tensor("wo", [D, D], F32, kind="ExternalInput").ap()
    fw_d = nc.dram_tensor("fw", [D, D], U8, kind="ExternalInput").ap()
    fs_d = nc.dram_tensor("fs", [1, 1], F32, kind="ExternalInput").ap()
    role_d = nc.dram_tensor("rolein", [1, 1], U32, kind="ExternalInput").ap()
    tabq_d = nc.dram_tensor("tabq", [128, NMX * 128], BF16, kind="ExternalInput").ap()
    tabk_d = nc.dram_tensor("tabk", [128, S], BF16, kind="ExternalInput").ap()
    maska_d = nc.dram_tensor("maska", [128, 128], BF16, kind="ExternalInput").ap()
    ident_d = nc.dram_tensor("ident", [128, 128], BF16, kind="ExternalInput").ap()
    out_d = nc.dram_tensor("out", [NMX * 128, D], F32, kind="ExternalOutput").ap()

    qspill = nc.dram_tensor("qspill", [NH, HD, NMX * 128], BF16, kind="Internal").ap()
    kspill = nc.dram_tensor("kspill", [NH, HD, S], BF16, kind="Internal").ap()
    vspill = nc.dram_tensor("vspill", [NH, 128, 16 * HD], BF16, kind="Internal").ap()
    z_hbm = nc.dram_tensor("z_hbm", [NMX, 128, D], F32, kind="Internal").ap()
    x2_hbm = nc.dram_tensor("x2_hbm", [NMX, 128, D], F32, kind="Internal").ap()
    z2_hbm = nc.dram_tensor("z2_hbm", [NMX, 128, D], F32, kind="Internal").ap()

    es = ExitStack()
    A = es.enter_context(nc.sbuf_tensor("arenaA", [128, 65536], BF16))
    Bu = es.enter_context(nc.sbuf_tensor("arenaB", [128, 64512], U8))
    Bf = Bu.bitcast(F32)
    Bb = Bu.bitcast(BF16)
    identb = es.enter_context(nc.sbuf_tensor("identb", [128, 128], BF16))
    maska = es.enter_context(nc.sbuf_tensor("maska_s", [128, 128], BF16))
    tabqc = es.enter_context(nc.sbuf_tensor("tabqc", [64, NMX * 128], BF16))
    tabqs = es.enter_context(nc.sbuf_tensor("tabqs", [64, NMX * 128], BF16))
    tabkc = es.enter_context(nc.sbuf_tensor("tabkc", [64, S], BF16))
    tabks = es.enter_context(nc.sbuf_tensor("tabks", [64, S], BF16))
    stats = es.enter_context(nc.sbuf_tensor("stats", [128, 576], F32))
    ones1 = es.enter_context(nc.sbuf_tensor("ones1", [1, 128], F32))
    rolesb = es.enter_context(nc.sbuf_tensor("rolesb", [1, 1], U32))

    ST_BN = 0
    ST_MV = 480
    ST_RSTD = 500
    ST_NMR = 512
    ST_RS = 524
    ST_NB = 536
    ST_FS = 548
    ST_FSB = 549
    ST_FS2 = 550
    ST_NEGFS = 551
    ST_EPS = 552
    ST_RSP = 560

    PL = es.enter_context(nc.psum_tensor("PL", [128, 2048], F32))
    pb = [es.enter_context(nc.psum_tensor(f"pb{i}", [128, 512], F32)) for i in range(4)]
    trP = [p.bitcast(BF16) for p in pb[:2]]
    avP = [pb[2], pb[3]]

    def bank(i):
        if i < 4:
            return PL[:, i * 512:(i + 1) * 512]
        return pb[i - 4][:, :]

    E0, E1 = Sched(), Sched()

    def walk(role, E):
        NQ = NQR[role]
        NKV = NKVR[role]
        QB0 = QB0R[role]
        NM = NQ
        QW = NQ * 128
        KWA = NKV * 128
        T2 = NH * NQ

        QK = {"sync": 12, "gpsimd": 8, "scalar": 8}
        QPFX = {"sync": "s", "gpsimd": "g", "scalar": "c"}
        qc = {"sync": 0, "gpsimd": 0, "scalar": 0}

        def dma(q, out, in_, waits=(), ms=None):
            j = qc[q]
            qc[q] = j + 1
            sem = f"{QPFX[q]}{j % QK[q]}"
            w = list(waits)
            if j >= QK[q]:
                w.append(("_dma", q, j - QK[q]))
            mss = [("_dma", q, j)] + ([ms] if ms is not None else [])
            E.op(q, lambda e, o=out, i=in_: e.dma_start(out=o, in_=i),
                 w, inc=(sem, 16), ms=mss)

        def op(eng, fn, waits=(), ms=None):
            semname = {"tensor": "t", "vector": "v", "scalar": "s", "gpsimd": "p"}[eng]
            E.op(eng, fn, waits, inc=(semname, 1) if ms is not None else None, ms=ms)

        def dma_tail(q):
            j1 = qc[q]
            return [("_dma", q, j) for j in range(max(0, j1 - QK[q]), j1)]

        bankfree = [None] * 8

        def bwait(i):
            return [bankfree[i]] if bankfree[i] is not None else []

        dma("sync", identb[:, :], ident_d, ms=("d_ident",))
        dma("sync", maska[:, :], maska_d, ms=("d_maska",))
        dma("sync", tabqc[:, :], tabq_d[0:64, :], ms=("tabq",))
        dma("sync", tabqs[:, :], tabq_d[64:128, :], ms=("tabqs",))
        dma("sync", tabkc[:, :], tabk_d[0:64, :], ms=("tabk",))
        dma("sync", tabks[:, :], tabk_d[64:128, :], ms=("tabks",))
        dma("sync", stats[0:1, ST_FS:ST_FS + 1], fs_d, ms=("d_fs",))
        op("vector", lambda e: e.memset(ones1[:, :], 1.0), ms=("ones",))
        op("vector", lambda e: e.memset(stats[:, ST_EPS:ST_EPS + 1], float(EPS)), ms=("epsv",))
        op("tensor", lambda e: e.matmul(pb[0][:, 0:1], ones1[0:1, :],
                                        stats[0:1, ST_FS:ST_FS + 1], start=True, stop=True),
           waits=[("ones",), ("d_fs",)], ms=("fsmm",))
        op("vector", lambda e: e.tensor_copy(stats[:, ST_FSB:ST_FSB + 1], pb[0][:, 0:1]),
           waits=[("fsmm",)], ms=("fsb",))
        op("vector", lambda e: e.tensor_scalar_mul(stats[:, ST_FS2:ST_FS2 + 1],
                                                   stats[:, ST_FSB:ST_FSB + 1], 2.0),
           waits=[("fsb",)], ms=("fs2",))
        op("vector", lambda e: e.tensor_scalar_mul(stats[:, ST_NEGFS:ST_NEGFS + 1],
                                                   stats[:, ST_FSB:ST_FSB + 1], -1.0),
           waits=[("fsb",)], ms=("negfs",))
        bankfree[4] = ("fsb",)

        XH = KWA // 2
        XSTb = 0
        WQSTb = XSTb + 3 * XH * 4
        WKSTb = WQSTb + 2 * 512 * 4
        WVSTb = WKSTb + 2 * 512 * 4
        WQBFb = WVSTb + 2 * 1024 * 4
        WKBFb = WQBFb + 4 * 512 * 2
        WVBFb = WKBFb + 4 * 512 * 2
        QSTGb = WVBFb + 4 * 1024 * 2
        KSTGb = QSTGb + 2 * QW * 2
        VSTGb = KSTGb + 2 * KWA * 2
        assert VSTGb + 4096 * 2 <= 64512, VSTGb
        XST = XSTb // 4
        WQST, WKST, WVST = WQSTb // 4, WKSTb // 4, WVSTb // 4
        WQBF, WKBF, WVBF = WQBFb // 2, WKBFb // 2, WVBFb // 2
        QSTG, KSTG, VSTG = QSTGb // 2, KSTGb // 2, VSTGb // 2

        qsplit = _split512(QW)
        ksplit = _split512(KWA)
        nqc, nkc = len(qsplit), len(ksplit)
        vpasses = []
        c0 = 0
        while c0 < NKV:
            vpasses.append((c0, min(c0 + 8, NKV)))
            c0 += 8
        NVP = len(vpasses)

        for kk in range(KD):
            for hf in range(2):
                idx = 2 * kk + hf
                s = idx % 3
                if idx >= 3:
                    pk, ph = divmod(idx - 3, 2)
                    w = [("xcast", pk, ph)]
                else:
                    w = []
                dma("gpsimd", Bf[:, XST + s * XH:XST + (s + 1) * XH],
                    xT_d[kk * 128:(kk + 1) * 128, hf * XH:(hf + 1) * XH],
                    waits=w, ms=("xdma", kk, hf))
                op("scalar",
                   lambda e, kk=kk, hf=hf, s=s: e.copy(
                       A[:, kk * KWA + hf * XH:kk * KWA + (hf + 1) * XH],
                       Bf[:, XST + s * XH:XST + (s + 1) * XH]),
                   waits=[("xdma", kk, hf)], ms=("xcast", kk, hf))

        for p in range(8):
            h0 = 2 * p
            for hh in range(2):
                h = h0 + hh
                for k2 in range(16):
                    b = k2 % 2
                    if k2 >= 2:
                        w = [("qcast", p, hh, k2 - 2)]
                    elif hh == 1:
                        w = [("qcast", p, 0, 14 + k2)]
                    elif p > 0:
                        w = [("qcast", p - 1, 1, 14 + k2)]
                    else:
                        w = []
                    dma("sync", Bf[:, WQST + b * 512:WQST + (b + 1) * 512].rearrange("p (c m) -> p c m", m=256),
                        wq_d[k2 * 256:(k2 + 1) * 256, h * HD:(h + 1) * HD].rearrange("(c p) m -> p c m", p=128),
                        waits=w, ms=("qwd", p, hh, k2))
                    b4 = k2 % 4
                    wc = [("qwd", p, hh, k2)]
                    if k2 >= 4:
                        wc.append(("qmm", p, hh, k2 - 4))
                    elif hh == 1:
                        wc.append(("qmm", p, 0, 12 + k2))
                    elif p > 0:
                        wc.append(("qmm", p - 1, 1, 12 + k2))
                    op("vector",
                       lambda e, b=b, b4=b4: e.tensor_copy(Bb[:, WQBF + b4 * 512:WQBF + (b4 + 1) * 512],
                                                           Bf[:, WQST + b * 512:WQST + (b + 1) * 512]),
                       waits=wc, ms=("qcast", p, hh, k2))
                    for sub in range(2):
                        kk = 2 * k2 + sub
                        coff = 0
                        for n, csz in enumerate(qsplit):
                            for m in range(2):
                                wmm = [("qcast", p, hh, k2), ("xcast", kk, 0), ("xcast", kk, 1)]
                                if kk == 0:
                                    wmm += bwait(m * nqc + n)
                                op("tensor",
                                   lambda e, b4=b4, sub=sub, m=m, n=n, kk=kk, coff=coff, csz=csz: e.matmul(
                                       bank(m * nqc + n)[:, 0:csz],
                                       Bb[:, WQBF + b4 * 512 + sub * 256 + m * 128:
                                          WQBF + b4 * 512 + sub * 256 + (m + 1) * 128],
                                       A[:, kk * KWA + QB0 * 128 + coff:kk * KWA + QB0 * 128 + coff + csz],
                                       start=(kk == 0), stop=(kk == KD - 1)),
                                   waits=wmm if (kk == 0 or (m == 0 and n == 0)) else (),
                                   ms=("qmm", p, hh, k2) if (m == 1 and n == nqc - 1 and sub == 1) else None)
                            coff += csz
                for m in range(2):
                    coff = 0
                    for n, csz in enumerate(qsplit):
                        w = [("qmm", p, hh, 15)]
                        if hh == 1:
                            w.append(("qspill", p, 0))
                        elif p > 0:
                            w.append(("qspill", p - 1, 1))
                        mskey = ("qevb", p, hh, m * nqc + n)
                        op("vector",
                           lambda e, m=m, coff=coff, csz=csz, n=n: e.tensor_scalar_mul(
                               Bb[:, QSTG + m * QW + coff:QSTG + m * QW + coff + csz],
                               bank(m * nqc + n)[:, 0:csz], SCL),
                           waits=w, ms=mskey)
                        bankfree[m * nqc + n] = mskey
                        coff += csz
                dma("gpsimd",
                    qspill[h][:, 0:QW].rearrange("(c p) t -> p c t", p=128),
                    Bb[:, QSTG:QSTG + 2 * QW].rearrange("p (c t) -> p c t", t=QW),
                    waits=[("qevb", p, hh, m * nqc + n) for m in range(2) for n in range(nqc)],
                    ms=("qspill", p, hh))

            for hh in range(2):
                h = h0 + hh
                for k2 in range(16):
                    b = k2 % 2
                    if k2 >= 2:
                        w = [("kcast", p, hh, k2 - 2)]
                    elif hh == 1:
                        w = [("kcast", p, 0, 14 + k2)]
                    elif p > 0:
                        w = [("kcast", p - 1, 1, 14 + k2)]
                    else:
                        w = []
                    dma("sync", Bf[:, WKST + b * 512:WKST + (b + 1) * 512].rearrange("p (c m) -> p c m", m=256),
                        wk_d[k2 * 256:(k2 + 1) * 256, h * HD:(h + 1) * HD].rearrange("(c p) m -> p c m", p=128),
                        waits=w, ms=("kwd", p, hh, k2))
                    b4 = k2 % 4
                    wc = [("kwd", p, hh, k2)]
                    if k2 >= 4:
                        wc.append(("kmm", p, hh, k2 - 4))
                    elif hh == 1:
                        wc.append(("kmm", p, 0, 12 + k2))
                    elif p > 0:
                        wc.append(("kmm", p - 1, 1, 12 + k2))
                    op("vector",
                       lambda e, b=b, b4=b4: e.tensor_copy(Bb[:, WKBF + b4 * 512:WKBF + (b4 + 1) * 512],
                                                           Bf[:, WKST + b * 512:WKST + (b + 1) * 512]),
                       waits=wc, ms=("kcast", p, hh, k2))
                    for sub in range(2):
                        kk = 2 * k2 + sub
                        coff = 0
                        for n, csz in enumerate(ksplit):
                            for m in range(2):
                                wmm = [("kcast", p, hh, k2), ("xcast", kk, 0), ("xcast", kk, 1)]
                                if kk == 0:
                                    wmm += bwait(m * nkc + n)
                                op("tensor",
                                   lambda e, b4=b4, sub=sub, m=m, n=n, kk=kk, coff=coff, csz=csz: e.matmul(
                                       bank(m * nkc + n)[:, 0:csz],
                                       Bb[:, WKBF + b4 * 512 + sub * 256 + m * 128:
                                          WKBF + b4 * 512 + sub * 256 + (m + 1) * 128],
                                       A[:, kk * KWA + coff:kk * KWA + coff + csz],
                                       start=(kk == 0), stop=(kk == KD - 1)),
                                   waits=wmm if (kk == 0 or (m == 0 and n == 0)) else (),
                                   ms=("kmm", p, hh, k2) if (m == 1 and n == nkc - 1 and sub == 1) else None)
                            coff += csz
                for m in range(2):
                    coff = 0
                    for n, csz in enumerate(ksplit):
                        w = [("kmm", p, hh, 15)]
                        if hh == 1:
                            w.append(("kspill", p, 0))
                        elif p > 0:
                            w.append(("kspill", p - 1, 1))
                        mskey = ("kevb", p, hh, m * nkc + n)
                        op("vector",
                           lambda e, m=m, coff=coff, csz=csz, n=n: e.tensor_copy(
                               Bb[:, KSTG + m * KWA + coff:KSTG + m * KWA + coff + csz],
                               bank(m * nkc + n)[:, 0:csz]),
                           waits=w, ms=mskey)
                        bankfree[m * nkc + n] = mskey
                        coff += csz
                dma("gpsimd",
                    kspill[h][:, 0:KWA].rearrange("(c p) t -> p c t", p=128),
                    Bb[:, KSTG:KSTG + 2 * KWA].rearrange("p (c t) -> p c t", t=KWA),
                    waits=[("kevb", p, hh, m * nkc + n) for m in range(2) for n in range(nkc)],
                    ms=("kspill", p, hh))

            for ps, (ca, cb) in enumerate(vpasses):
                npc = cb - ca
                for k2 in range(16):
                    b = k2 % 2
                    if k2 >= 2:
                        w = [("vcast", p, ps, k2 - 2)]
                    elif ps > 0:
                        w = [("vcast", p, ps - 1, 14 + k2)]
                    elif p > 0:
                        w = [("vcast", p - 1, NVP - 1, 14 + k2)]
                    else:
                        w = []
                    dma("scalar", Bf[:, WVST + b * 1024:WVST + (b + 1) * 1024].rearrange("p (c m) -> p c m", m=512),
                        wv_d[k2 * 256:(k2 + 1) * 256, h0 * HD:h0 * HD + 512].rearrange("(c p) m -> p c m", p=128),
                        waits=w, ms=("vwd", p, ps, k2))
                    b4 = k2 % 4
                    wc = [("vwd", p, ps, k2)]
                    if k2 >= 4:
                        wc.append(("vmm", p, ps, k2 - 4))
                    elif ps > 0:
                        wc.append(("vmm", p, ps - 1, 12 + k2))
                    elif p > 0:
                        wc.append(("vmm", p - 1, NVP - 1, 12 + k2))
                    op("vector",
                       lambda e, b=b, b4=b4: e.tensor_copy(Bb[:, WVBF + b4 * 1024:WVBF + (b4 + 1) * 1024],
                                                           Bf[:, WVST + b * 1024:WVST + (b + 1) * 1024]),
                       waits=wc, ms=("vcast", p, ps, k2))
                    for sub in range(2):
                        kk = 2 * k2 + sub
                        for bidx in range(npc):
                            c = ca + bidx
                            wmm = [("vcast", p, ps, k2), ("xcast", kk, 0), ("xcast", kk, 1)]
                            if kk == 0:
                                wmm += bwait(bidx)
                            op("tensor",
                               lambda e, b4=b4, sub=sub, c=c, bidx=bidx, kk=kk: e.matmul(
                                   bank(bidx),
                                   A[:, kk * KWA + c * 128:kk * KWA + (c + 1) * 128],
                                   Bb[:, WVBF + b4 * 1024 + sub * 512:WVBF + b4 * 1024 + (sub + 1) * 512],
                                   start=(kk == 0), stop=(kk == KD - 1)),
                               waits=wmm if (kk == 0 or bidx == 0) else (),
                               ms=("vmm", p, ps, k2) if (bidx == npc - 1 and sub == 1) else None)
                for bidx in range(npc):
                    w = [("vmm", p, ps, 15)]
                    if ps > 0:
                        w += [("vspill", p, ps - 1, 0), ("vspill", p, ps - 1, 1)]
                    elif p > 0:
                        w += [("vspill", p - 1, NVP - 1, 0), ("vspill", p - 1, NVP - 1, 1)]
                    mskey = ("vevb", p, ps, bidx)
                    op("vector",
                       lambda e, bidx=bidx: e.tensor_copy(
                           Bb[:, VSTG + bidx * 512:VSTG + (bidx + 1) * 512], bank(bidx)),
                       waits=w, ms=mskey)
                    bankfree[bidx] = mskey
                for hh in range(2):
                    h = h0 + hh
                    dma("gpsimd",
                        vspill[h].rearrange("p (c d) -> p c d", d=HD)[:, ca:cb, :],
                        Bb[:, VSTG:VSTG + npc * 512].rearrange("p (c two) -> p c two", two=512)[:, :, hh * HD:(hh + 1) * HD],
                        waits=[("vevb", p, ps, i) for i in range(npc)],
                        ms=("vspill", p, ps, hh))

        p1done = ("vspill", 7, NVP - 1, 1)
        p1all = [("vspill", 7, NVP - 1, 0), ("vspill", 7, NVP - 1, 1)]
        npc_last = vpasses[-1][1] - vpasses[-1][0]
        E.set_fence([("vmm", 7, NVP - 1, 15), ("vevb", 7, NVP - 1, npc_last - 1),
                     ("xcast", 31, 0), ("xcast", 31, 1)] + dma_tail("sync") + dma_tail("gpsimd") + dma_tail("scalar"))

        SETSZ = 2 * QW + 2 * KWA + NKV * 256
        SETA0 = 32 * QW
        ANW = (QB0 + NQ) * 128
        ANB = SETA0 + 2 * SETSZ
        assert ANB + 6 * ANW <= 65536, (ANB, ANW)
        SET3b = 0
        SWQb = SET3b + SETSZ * 2
        SWKb = SWQb + 2 * QW * 2
        SCQb = SWKb + 2 * KWA * 2
        SCKb = SCQb + 2 * QW * 2
        AUBb = SCKb + 2 * KWA * 2
        SVBb = AUBb + 3 * ANW * 2
        assert SVBb + 3 * 16 * 4 <= 64512, SVBb
        SET3 = SET3b // 2
        SWQ, SWK, SCQ, SCK = SWQb // 2, SWKb // 2, SCQb // 2, SCKb // 2
        AUB0 = AUBb // 2
        SVB0 = SVBb // 4

        grp_hist = []

        def qkvset(h):
            si = h % 3
            if si < 2:
                return A, SETA0 + si * SETSZ
            return Bb, SET3

        def emit_head_loads(h):
            AR, qb = qkvset(h)
            kb = qb + 2 * QW
            vb = kb + 2 * KWA
            par = h % 2
            swq = SWQ + par * QW
            swk = SWK + par * KWA
            scq = SCQ + par * QW
            sck = SCK + par * KWA
            wl = [("qspill", h // 2, h % 2), ("kspill", h // 2, h % 2)] + \
                 [("vspill", h // 2, psx, h % 2) for psx in range(NVP)] + p1all
            if h >= 3:
                wl.append(("avmm", (h - 3) * NQ + NQ - 1))
            dma("gpsimd", AR[:, qb:qb + 2 * QW].rearrange("p (c t) -> p c t", t=QW),
                qspill[h][:, 0:QW].rearrange("(c p) t -> p c t", p=128),
                waits=wl, ms=("ldq", h))
            dma("gpsimd", AR[:, kb:kb + 2 * KWA].rearrange("p (c t) -> p c t", t=KWA),
                kspill[h][:, 0:KWA].rearrange("(c p) t -> p c t", p=128),
                waits=wl, ms=("ldk", h))
            dma("gpsimd", AR[:, vb:vb + NKV * 256],
                vspill[h][:, 0:NKV * 256], waits=wl, ms=("ldv", h))
            wr = [("ldq", h)] + ([("ropeq", h - 2)] if h >= 2 else [])
            dma("gpsimd", Bb[0:32, swq:swq + QW], AR[32:64, qb:qb + QW], waits=wr, ms=("swq1", h))
            dma("gpsimd", Bb[32:64, swq:swq + QW], AR[0:32, qb:qb + QW], waits=wr, ms=("swq2", h))
            wr = [("ldk", h)] + ([("ropek", h - 2)] if h >= 2 else [])
            dma("gpsimd", Bb[0:32, swk:swk + KWA], AR[32:64, kb:kb + KWA], waits=wr, ms=("swk1", h))
            dma("gpsimd", Bb[32:64, swk:swk + KWA], AR[0:32, kb:kb + KWA], waits=wr, ms=("swk2", h))
            wv_ = [("ldq", h), ("tabq",), ("tabqs",)] + ([("ropeq", h - 2)] if h >= 2 else [])
            op("vector", lambda e, AR=AR, qb=qb, scq=scq: e.tensor_mul(
                   Bb[0:64, scq:scq + QW], AR[0:64, qb:qb + QW], tabqc[:, 0:QW]),
               waits=wv_, ms=("rq1", h))
            op("vector", lambda e, swq=swq: e.tensor_mul(
                   Bb[0:64, swq:swq + QW], Bb[0:64, swq:swq + QW], tabqs[:, 0:QW]),
               waits=[("swq1", h), ("swq2", h)], ms=("rq2", h))
            op("vector", lambda e, AR=AR, qb=qb, scq=scq, swq=swq: e.tensor_add(
                   AR[0:64, qb:qb + QW], Bb[0:64, scq:scq + QW], Bb[0:64, swq:swq + QW]),
               waits=[("rq1", h), ("rq2", h)], ms=("ropeq", h))
            wv_ = [("ldk", h), ("tabk",), ("tabks",)] + ([("ropek", h - 2)] if h >= 2 else [])
            op("vector", lambda e, AR=AR, kb=kb, sck=sck: e.tensor_mul(
                   Bb[0:64, sck:sck + KWA], AR[0:64, kb:kb + KWA], tabkc[:, 0:KWA]),
               waits=wv_, ms=("rk1", h))
            op("vector", lambda e, swk=swk: e.tensor_mul(
                   Bb[0:64, swk:swk + KWA], Bb[0:64, swk:swk + KWA], tabks[:, 0:KWA]),
               waits=[("swk1", h), ("swk2", h)], ms=("rk2", h))
            op("vector", lambda e, AR=AR, kb=kb, sck=sck, swk=swk: e.tensor_add(
                   AR[0:64, kb:kb + KWA], Bb[0:64, sck:sck + KWA], Bb[0:64, swk:swk + KWA]),
               waits=[("rk1", h), ("rk2", h)], ms=("ropek", h))

        pl_bank_last = {i: bankfree[i] for i in range(4)}
        tk_base = {}
        pl_flip = [0]

        def pl_banks(t):
            jl = t % NQ
            W = (QB0 + jl + 1) * 128
            b0 = tk_base[t]
            return list(range(b0 // 512, (b0 + W + 511) // 512))

        def emit_lg(t):
            h, jl = divmod(t, NQ)
            klen = QB0 + jl + 1
            W = klen * 128
            if role == 0 and klen <= 8:
                b0 = pl_flip[0] * 1024
                pl_flip[0] ^= 1
            else:
                b0 = 0
            tk_base[t] = b0
            AR, qb = qkvset(h)
            kb = qb + 2 * QW
            wlog = [("ropeq", h), ("ropek", h), ("d_maska",), ("d_ident",)]
            if t == 0:
                wlog.append(p1done)
            for bi in pl_banks(t):
                if pl_bank_last[bi] is not None:
                    wlog.append(pl_bank_last[bi])
            first = True
            coff = 0
            chunks = _split512(W)
            for ci, csz in enumerate(chunks):
                isdiag = (ci == len(chunks) - 1)
                for m in range(2):
                    stop = (m == 1 and not isdiag)
                    op("tensor",
                       lambda e, m=m, coff=coff, csz=csz, qb=qb, kb=kb, jl=jl,
                       stop=stop, AR=AR, b0=b0: e.matmul(
                           PL[:, b0 + coff:b0 + coff + csz],
                           AR[:, qb + m * QW + jl * 128:qb + m * QW + (jl + 1) * 128],
                           AR[:, kb + m * KWA + coff:kb + m * KWA + coff + csz],
                           start=(m == 0), stop=stop),
                       waits=wlog if first else (),
                       ms=("lgh", t) if (stop and ci == len(chunks) - 2) else None)
                    first = False
                if isdiag:
                    op("tensor",
                       lambda e, W=W, b0=b0: e.matmul(
                           PL[:, b0 + W - 128:b0 + W], identb[:, :], maska[:, :],
                           start=False, stop=True),
                       ms=("lgp", t))
                coff += csz

        def emit_exp(t):
            h, jl = divmod(t, NQ)
            gp = t % 3
            W = (QB0 + jl + 1) * 128
            b0 = tk_base[t]
            SVX = SVB0 + 16 * gp
            AUB = AUB0 + ANW * gp
            wexp = [("lgp", t)]
            if t >= 3:
                wexp += [("nrm", t - 3), ("rcp", t - 3)]
            op("scalar", lambda e, W=W, SVX=SVX, AUB=AUB, b0=b0: e.activation(
                   Bb[:, AUB:AUB + W], PL[:, b0:b0 + W], AF.Exp,
                   bias=0.0, scale=1.0,
                   accum_out=Bf[:, SVX + 5:SVX + 6]),
               waits=wexp, ms=("exp", t))
            for bi in pl_banks(t):
                pl_bank_last[bi] = ("exp", t)

        def emit_smx_back(t):
            h, jl = divmod(t, NQ)
            gp = t % 3
            W = (QB0 + jl + 1) * 128
            SVX = SVB0 + 16 * gp
            AUB = AUB0 + ANW * gp
            AN = ANB + ANW * gp
            op("vector", lambda e, SVX=SVX: e.reciprocal(
                   Bf[:, SVX + 6:SVX + 7], Bf[:, SVX + 5:SVX + 6]),
               waits=[("exp", t)], ms=("rcp", t))
            wn = [("rcp", t)]
            if t >= 3:
                wn.append(("trl", t - 3))
            op("vector", lambda e, W=W, AUB=AUB, AN=AN, SVX=SVX: e.tensor_scalar_mul(
                   A[:, AN:AN + W], Bb[:, AUB:AUB + W], Bf[:, SVX + 6:SVX + 7]),
               waits=wn, ms=("nrm", t))

        def emit_tr(t):
            h, jl = divmod(t, NQ)
            klen = QB0 + jl + 1
            gp = t % 3
            AN = ANB + ANW * gp
            AT = ANB + 3 * ANW + ANW * gp
            ngrp = (klen + 3) // 4
            for g in range(ngrp):
                kks = list(range(4 * g, min(4 * g + 4, klen)))
                tb = len(grp_hist) % 2
                wtr = [("nrm", t)]
                if len(grp_hist) >= 2:
                    wtr.append(grp_hist[-2])
                elif bankfree[4 + tb] is not None:
                    wtr.append(bankfree[4 + tb])
                for i, kk in enumerate(kks):
                    op("tensor",
                       lambda e, tb=tb, kk=kk, i=i, AN=AN: e.transpose(
                           trP[tb][:, i * 128:(i + 1) * 128],
                           A[:, AN + kk * 128:AN + (kk + 1) * 128], identb[:, :]),
                       waits=wtr if i == 0 else (),
                       ms=("trl", t) if kk == klen - 1 else ("trx", t, kk))
                wge = [("trl", t) if kks[-1] == klen - 1 else ("trx", t, kks[-1])]
                if t >= 3:
                    wge.append(("avmm", t - 3))
                gkey = ("grpev", t, g)
                op("vector",
                   lambda e, tb=tb, g=g, n=len(kks), AT=AT: e.tensor_copy(
                       A[:, AT + g * 512:AT + g * 512 + n * 128],
                       trP[tb][:, 0:n * 128]),
                   waits=wge, ms=gkey)
                grp_hist.append(gkey)

        def emit_av(t):
            h, jl = divmod(t, NQ)
            klen = QB0 + jl + 1
            gp = t % 3
            AR, qb = qkvset(h)
            vb = qb + 2 * QW + 2 * KWA
            AT = ANB + 3 * ANW + ANW * gp
            for kk in range(klen):
                wav = [("grpev", t, kk // 4)]
                if kk == 0:
                    wav.append(("avev", t - 1) if t > 0 else p1done)
                    wav.append(("ldv", h))
                    if t == 0:
                        wav += bwait(6) + bwait(7)
                for dh in range(2):
                    op("tensor",
                       lambda e, dh=dh, kk=kk, klen=klen, vb=vb, AT=AT, AR=AR: e.matmul(
                           avP[dh][:, 0:128],
                           AR[:, vb + kk * 256 + dh * 128:vb + kk * 256 + (dh + 1) * 128],
                           A[:, AT + kk * 128:AT + (kk + 1) * 128],
                           start=(kk == 0), stop=(kk == klen - 1)),
                       waits=wav if (dh == 0 or kk == 0) else (),
                       ms=("avmm", t) if (kk == klen - 1 and dh == 1) else None)

        def emit_avev(t):
            h, jl = divmod(t, NQ)
            for dh in range(2):
                op("scalar",
                   lambda e, dh=dh, h=h, jl=jl: e.copy(
                       A[:, (2 * h + dh) * QW + jl * 128:(2 * h + dh) * QW + (jl + 1) * 128],
                       avP[dh][:, 0:128]),
                   waits=[("avmm", t)], ms=("avev", t) if dh == 1 else None)

        emit_head_loads(0)
        emit_head_loads(1)
        emit_head_loads(2)
        for u in range(T2 + 2):
            if u - 2 >= 0:
                emit_av(u - 2)
                emit_avev(u - 2)
            if u < T2:
                h, jl = divmod(u, NQ)
                if jl == 3 and 3 <= h + 2 < NH:
                    emit_head_loads(h + 2)
                emit_lg(u)
            if 0 <= u - 1 < T2:
                emit_smx_back(u - 1)
            if u < T2:
                emit_exp(u)
            if 0 <= u - 1 < T2:
                emit_tr(u - 1)

        p2done = ("avev", T2 - 1)
        bankfree = [None] * 8
        E.set_fence([("avmm", T2 - 1), p2done, grp_hist[-1],
                     ("ropeq", NH - 1), ("ropek", NH - 1)]
                    + dma_tail("sync") + dma_tail("gpsimd") + dma_tail("scalar"))

        WOSTb4 = 0
        WOBFb4 = WOSTb4 + 2 * 1024 * 4
        ZSTb4 = WOBFb4 + 16384 * 2
        XRSb4 = ZSTb4 + 4 * 512 * 4
        assert XRSb4 + 4 * 512 * 4 <= 64512
        WOST4, WOBF4 = WOSTb4 // 4, WOBFb4 // 2
        ZST4, XRS4 = ZSTb4 // 4, XRSb4 // 4
        groupsM = [list(range(0, min(NM, 8)))] + ([list(range(8, NM))] if NM > 8 else [])
        NG = len(groupsM)
        zlast = {}
        xlast = {}

        def wo_half_free(nn, half):
            if nn == 0:
                return None
            return ("wommH0", nn - 1, NG - 1) if half == 0 else ("womm", nn - 1, NG - 1)

        for nn in range(8):
            for k2 in range(16):
                b = k2 % 2
                half = k2 // 8
                if k2 >= 2:
                    w = [("wocast", nn, k2 - 2)]
                elif nn > 0:
                    w = [("wocast", nn - 1, 14 + k2)]
                else:
                    w = [p2done]
                dma("sync", Bf[:, WOST4 + b * 1024:WOST4 + (b + 1) * 1024].rearrange("p (c m) -> p c m", m=512),
                    wo_d[k2 * 256:(k2 + 1) * 256, nn * 512:(nn + 1) * 512].rearrange("(c p) m -> p c m", p=128),
                    waits=w, ms=("wod", nn, k2))
                wc = [("wod", nn, k2)]
                whf = wo_half_free(nn, half)
                if whf is not None:
                    wc.append(whf)
                if nn == 0 and k2 < 2:
                    wc.append(p2done)
                op("scalar",
                   lambda e, b=b, k2=k2: e.copy(Bb[:, WOBF4 + k2 * 1024:WOBF4 + (k2 + 1) * 1024],
                                                Bf[:, WOST4 + b * 1024:WOST4 + (b + 1) * 1024]),
                   waits=wc, ms=("wocast", nn, k2))
            for gi, ms_ in enumerate(groupsM):
                for kk in range(KD):
                    for ii, m in enumerate(ms_):
                        wmm = [("wocast", nn, kk // 2)]
                        if kk == 0:
                            wmm += bwait(ii) + [p2done]
                        if kk == KD - 1 and ii == len(ms_) - 1:
                            mskey = ("womm", nn, gi)
                        elif kk == 15 and ii == len(ms_) - 1:
                            mskey = ("wommH0", nn, gi)
                        else:
                            mskey = None
                        op("tensor",
                           lambda e, m=m, ii=ii, kk=kk: e.matmul(
                               bank(ii),
                               A[:, kk * QW + m * 128:kk * QW + (m + 1) * 128],
                               Bb[:, WOBF4 + kk * 512:WOBF4 + (kk + 1) * 512],
                               start=(kk == 0), stop=(kk == KD - 1)),
                           waits=wmm if (ii == 0 or kk == 0) else (),
                           ms=mskey)
                for ii, m in enumerate(ms_):
                    zb = m % 4
                    wx = [p2done]
                    if ("x", zb) in xlast:
                        wx.append(xlast[("x", zb)])
                    dma("sync", Bf[:, XRS4 + zb * 512:XRS4 + (zb + 1) * 512],
                        xres_d[m * 128:(m + 1) * 128, nn * 512:(nn + 1) * 512],
                        waits=wx, ms=("xrsd", nn, m))
                    w = [("womm", nn, gi), ("xrsd", nn, m)]
                    if ("z", zb) in zlast:
                        w.append(zlast[("z", zb)])
                    mskey = ("zev", nn, m)
                    op("vector",
                       lambda e, ii=ii, zb=zb: e.tensor_add(
                           Bf[:, ZST4 + zb * 512:ZST4 + (zb + 1) * 512], bank(ii),
                           Bf[:, XRS4 + zb * 512:XRS4 + (zb + 1) * 512]),
                       waits=w, ms=mskey)
                    bankfree[ii] = mskey
                    xlast[("x", zb)] = mskey
                    op("vector",
                       lambda e, m=m, nn=nn, zb=zb: e.bn_stats(
                           stats[:, ST_BN + (m * 8 + nn) * 6:ST_BN + (m * 8 + nn + 1) * 6],
                           Bf[:, ZST4 + zb * 512:ZST4 + (zb + 1) * 512]),
                       waits=[("zev", nn, m)], ms=("bnst", m, nn))
                    dma("gpsimd", z_hbm[m][:, nn * 512:(nn + 1) * 512],
                        Bf[:, ZST4 + zb * 512:ZST4 + (zb + 1) * 512],
                        waits=[("zev", nn, m), ("bnst", m, nn)], ms=("zdma", nn, m))
                    zlast[("z", zb)] = ("zdma", nn, m)

        zall_list = [("zdma", 7, i) for i in range(NM)]
        E.set_fence([("womm", 7, NG - 1), ("bnst", groupsM[-1][-1], 7),
                     ("wocast", 7, 15)] + dma_tail("sync") + dma_tail("gpsimd") + dma_tail("scalar"))

        TSLb5 = 0
        X2Fb5 = TSLb5 + 4 * 512 * 4
        X2Bb5 = X2Fb5 + 4 * 512 * 4
        TSL5, X2F5 = TSLb5 // 4, X2Fb5 // 4
        X2B5 = X2Bb5 // 2
        def p5_stats(m):
            op("vector",
               lambda e, m=m: e.bn_aggr(stats[:, ST_MV + m * 2:ST_MV + (m + 1) * 2],
                                        stats[:, ST_BN + m * 48:ST_BN + (m + 1) * 48]),
               waits=[("bnst", m, n2) for n2 in range(8)], ms=("mv", m))
            op("scalar",
               lambda e, m=m: e.activation(stats[:, ST_RSTD + m:ST_RSTD + m + 1],
                                           stats[:, ST_MV + m * 2 + 1:ST_MV + m * 2 + 2],
                                           AF.Ln, bias=stats[:, ST_EPS:ST_EPS + 1], scale=1.0),
               waits=[("mv", m), ("epsv",)], ms=("lnv", m))
            op("scalar",
               lambda e, m=m: e.activation(stats[:, ST_RSTD + m:ST_RSTD + m + 1],
                                           stats[:, ST_RSTD + m:ST_RSTD + m + 1],
                                           AF.Exp, bias=0.0, scale=-0.5),
               waits=[("lnv", m)], ms=("rstd", m))
            op("vector",
               lambda e, m=m: e.tensor_mul(stats[:, ST_NMR + m:ST_NMR + m + 1],
                                           stats[:, ST_MV + m * 2:ST_MV + m * 2 + 1],
                                           stats[:, ST_RSTD + m:ST_RSTD + m + 1]),
               waits=[("rstd", m)], ms=("nmr", m))

        def p5_front(m, nn):
            bx = nn % 4
            w = list(zall_list) + [("zdma", 7, m)]
            if nn >= 4:
                w.append(("x2n", m, nn - 4))
            elif m > 0:
                w.append(("x2n", m - 1, nn + 4))
            dma("sync", Bf[:, TSL5 + bx * 512:TSL5 + (bx + 1) * 512],
                z_hbm[m][:, nn * 512:(nn + 1) * 512], waits=w, ms=("zrd", m, nn))
            tb = nn % 4
            wn = [("zrd", m, nn), ("nmr", m)]
            if nn >= 4:
                wn.append(("x2dma", m, nn - 4))
            elif m > 0:
                wn.append(("x2dma", m - 1, nn + 4))
            op("vector",
               lambda e, m=m, bx=bx, tb=tb: e.tensor_scalar(
                   Bf[:, X2F5 + tb * 512:X2F5 + (tb + 1) * 512],
                   Bf[:, TSL5 + bx * 512:TSL5 + (bx + 1) * 512],
                   stats[:, ST_RSTD + m:ST_RSTD + m + 1],
                   stats[:, ST_NMR + m:ST_NMR + m + 1],
                   ALU.mult, ALU.subtract),
               waits=wn, ms=("x2n", m, nn))
            dma("gpsimd", x2_hbm[m][:, nn * 512:(nn + 1) * 512],
                Bf[:, X2F5 + tb * 512:X2F5 + (tb + 1) * 512],
                waits=[("x2n", m, nn)], ms=("x2dma", m, nn))
            wxc = [("x2n", m, nn)]
            if nn >= 4:
                wxc.append(("x2t", m, (nn - 4) * 4 + 3))
            elif m > 0:
                wxc.append(("x2t", m - 1, (nn + 4) * 4 + 3))
            op("scalar",
               lambda e, tb=tb: e.copy(
                   Bb[:, X2B5 + tb * 512:X2B5 + (tb + 1) * 512],
                   Bf[:, X2F5 + tb * 512:X2F5 + (tb + 1) * 512]),
               waits=wxc, ms=("x2c", m, nn))

        def p5_back(m, nn):
            tb = nn % 4
            wrsp = [("x2c", m, nn)]
            if m > 0 and nn == 0:
                wrsp.append(("rs", m - 1))
            op("vector",
               lambda e, nn=nn, tb=tb: e.reduce_sum(
                   stats[:, ST_RSP + nn:ST_RSP + nn + 1],
                   Bb[:, X2B5 + tb * 512:X2B5 + (tb + 1) * 512], axis=AX.X),
               waits=wrsp, ms=("rsp", m, nn))
            pbk = len(grp_hist) % 2
            wtr = [("x2c", m, nn), p2done]
            if len(grp_hist) >= 2:
                wtr.append(grp_hist[-2])
            for t4 in range(4):
                kkc = nn * 4 + t4
                op("tensor",
                   lambda e, pbk=pbk, tb=tb, t4=t4: e.transpose(
                       trP[pbk][:, t4 * 128:(t4 + 1) * 128],
                       Bb[:, X2B5 + tb * 512 + t4 * 128:X2B5 + tb * 512 + (t4 + 1) * 128],
                       identb[:, :]),
                   waits=wtr if t4 == 0 else (), ms=("x2t", m, kkc))
            gkey = ("x2g", m, nn)
            op("vector",
               lambda e, pbk=pbk, m=m, nn=nn: e.tensor_copy(
                   A[:, 0:32 * NM * 128].rearrange("p (c t) -> p c t", t=NM * 128)
                   [:, nn * 4:nn * 4 + 4, m * 128:(m + 1) * 128],
                   trP[pbk][:, 0:512].rearrange("p (c t) -> p c t", t=128)),
               waits=[("x2t", m, nn * 4 + 3)], ms=gkey)
            grp_hist.append(gkey)
            if nn == 7:
                op("vector",
                   lambda e, m=m: e.reduce_sum(stats[:, ST_RS + m:ST_RS + m + 1],
                                               stats[:, ST_RSP:ST_RSP + 8], axis=AX.X),
                   waits=[("rsp", m, n2) for n2 in range(8)], ms=("rs", m))

        chunksP5 = [(m, nn) for m in range(NM) for nn in range(8)]
        for j5 in range(len(chunksP5) + 1):
            if j5 < len(chunksP5):
                m5, nn5 = chunksP5[j5]
                if nn5 == 0:
                    p5_stats(m5)
                p5_front(m5, nn5)
            if j5 >= 1:
                p5_back(*chunksP5[j5 - 1])
        op("vector",
           lambda e: e.tensor_scalar_mul(stats[:, ST_NB:ST_NB + NM], stats[:, ST_RS:ST_RS + NM],
                                         stats[:, ST_NEGFS:ST_NEGFS + 1]),
           waits=[("rs", m2) for m2 in range(NM)] + [("negfs",)], ms=("nb",))

        x2tall = ("x2g", NM - 1, 7)
        bankfree = [None] * 8
        E.set_fence([("x2t", NM - 1, 31), ("nb",), ("x2c", NM - 1, 7)]
                    + dma_tail("sync") + dma_tail("gpsimd") + dma_tail("scalar"))

        FWSTb6 = 0
        FWBFb6 = FWSTb6 + 2 * 1024
        X2RSb6 = FWBFb6 + 16384 * 2
        Z2STb6 = X2RSb6 + 4 * 512 * 4
        assert Z2STb6 + 4 * 512 * 4 <= 64512
        FWST6 = FWSTb6
        FWBF6 = FWBFb6 // 2
        X2RS6, Z2ST6 = X2RSb6 // 4, Z2STb6 // 4
        zlast = {}
        xlast = {}

        for nn in range(8):
            for k2 in range(16):
                b = k2 % 2
                half = k2 // 8
                if k2 >= 2:
                    w = [("fwcast", nn, k2 - 2)]
                elif nn > 0:
                    w = [("fwcast", nn - 1, 14 + k2)]
                else:
                    w = [("nb",), x2tall]
                dma("sync", Bu[:, FWST6 + b * 1024:FWST6 + (b + 1) * 1024].rearrange("p (c m) -> p c m", m=512),
                    fw_d[k2 * 256:(k2 + 1) * 256, nn * 512:(nn + 1) * 512].rearrange("(c p) m -> p c m", p=128),
                    waits=w, ms=("fwd", nn, k2))
                wc = [("fwd", nn, k2)]
                if nn > 0:
                    wc.append(("fmmH0", nn - 1, NG - 1) if half == 0
                              else ("fmm", nn - 1, NG - 1))
                if nn == 0 and k2 < 2:
                    wc += [("nb",), x2tall]
                op("scalar",
                   lambda e, b=b, k2=k2: e.copy(Bb[:, FWBF6 + k2 * 1024:FWBF6 + (k2 + 1) * 1024],
                                                Bu[:, FWST6 + b * 1024:FWST6 + (b + 1) * 1024]),
                   waits=wc, ms=("fwcast", nn, k2))
            for gi, ms_ in enumerate(groupsM):
                for kk in range(KD):
                    for ii, m in enumerate(ms_):
                        wmm = [("fwcast", nn, kk // 2)]
                        if kk == 0:
                            wmm += bwait(ii) + [x2tall]
                        if kk == KD - 1 and ii == len(ms_) - 1:
                            mskey = ("fmm", nn, gi)
                        elif kk == 15 and ii == len(ms_) - 1:
                            mskey = ("fmmH0", nn, gi)
                        else:
                            mskey = None
                        op("tensor",
                           lambda e, m=m, ii=ii, kk=kk: e.matmul(
                               bank(ii),
                               A[:, kk * NM * 128 + m * 128:kk * NM * 128 + (m + 1) * 128],
                               Bb[:, FWBF6 + kk * 512:FWBF6 + (kk + 1) * 512],
                               start=(kk == 0), stop=(kk == KD - 1)),
                           waits=wmm if (ii == 0 or kk == 0) else (),
                           ms=mskey)
                for ii, m in enumerate(ms_):
                    zb = m % 4
                    wx = [("nb",), x2tall, ("x2dma", m, nn)]
                    if ("x", zb) in xlast:
                        wx.append(xlast[("x", zb)])
                    dma("sync", Bf[:, X2RS6 + zb * 512:X2RS6 + (zb + 1) * 512],
                        x2_hbm[m][:, nn * 512:(nn + 1) * 512],
                        waits=wx, ms=("x2sl", nn, m))
                    op("vector",
                       lambda e, m=m, zb=zb: e.tensor_scalar_add(
                           Bf[:, X2RS6 + zb * 512:X2RS6 + (zb + 1) * 512],
                           Bf[:, X2RS6 + zb * 512:X2RS6 + (zb + 1) * 512],
                           stats[:, ST_NB + m:ST_NB + m + 1]),
                       waits=[("x2sl", nn, m), ("nb",)], ms=("xadj", nn, m))
                    w = [("fmm", nn, gi), ("xadj", nn, m), ("x2sl", nn, m), ("fs2",)]
                    if ("z", zb) in zlast:
                        w.append(zlast[("z", zb)])
                    mskey = ("z2ev", nn, m)
                    op("vector",
                       lambda e, ii=ii, zb=zb: e.scalar_tensor_tensor(
                           Bf[:, Z2ST6 + zb * 512:Z2ST6 + (zb + 1) * 512], bank(ii),
                           stats[:, ST_FS2:ST_FS2 + 1],
                           Bf[:, X2RS6 + zb * 512:X2RS6 + (zb + 1) * 512],
                           ALU.mult, ALU.add),
                       waits=w, ms=mskey)
                    bankfree[ii] = mskey
                    xlast[("x", zb)] = mskey
                    wb = [("z2ev", nn, m)]
                    if nn == 0:
                        wb.append(("mv", NM - 1))
                    op("vector",
                       lambda e, m=m, nn=nn, zb=zb: e.bn_stats(
                           stats[:, ST_BN + (m * 8 + nn) * 6:ST_BN + (m * 8 + nn + 1) * 6],
                           Bf[:, Z2ST6 + zb * 512:Z2ST6 + (zb + 1) * 512]),
                       waits=wb, ms=("bnst2", m, nn))
                    dma("gpsimd", z2_hbm[m][:, nn * 512:(nn + 1) * 512],
                        Bf[:, Z2ST6 + zb * 512:Z2ST6 + (zb + 1) * 512],
                        waits=[("bnst2", m, nn)], ms=("z2dma", nn, m))
                    zlast[("z", zb)] = ("z2dma", nn, m)

        z2all_list = [("z2dma", 7, i) for i in range(NM)]
        E.set_fence([("fmm", 7, NG - 1), ("bnst2", groupsM[-1][-1], 7),
                     ("fwcast", 7, 15)] + dma_tail("sync") + dma_tail("gpsimd") + dma_tail("scalar"))

        T7b = 0
        O7b = T7b + 4 * 512 * 4
        T7, O7 = T7b // 4, O7b // 4
        assert O7b + 4 * 512 * 4 <= 64512
        for m in range(NM):
            op("vector",
               lambda e, m=m: e.bn_aggr(stats[:, ST_MV + m * 2:ST_MV + (m + 1) * 2],
                                        stats[:, ST_BN + m * 48:ST_BN + (m + 1) * 48]),
               waits=[("bnst2", m, n2) for n2 in range(8)] + [("rstd", NM - 1), ("nmr", NM - 1)],
               ms=("mv2", m))
            op("scalar",
               lambda e, m=m: e.activation(stats[:, ST_RSTD + m:ST_RSTD + m + 1],
                                           stats[:, ST_MV + m * 2 + 1:ST_MV + m * 2 + 2],
                                           AF.Ln, bias=stats[:, ST_EPS:ST_EPS + 1], scale=1.0),
               waits=[("mv2", m), ("epsv",)], ms=("lnv2", m))
            op("scalar",
               lambda e, m=m: e.activation(stats[:, ST_RSTD + m:ST_RSTD + m + 1],
                                           stats[:, ST_RSTD + m:ST_RSTD + m + 1],
                                           AF.Exp, bias=0.0, scale=-0.5),
               waits=[("lnv2", m)], ms=("rstd2", m))
            op("vector",
               lambda e, m=m: e.tensor_mul(stats[:, ST_NMR + m:ST_NMR + m + 1],
                                           stats[:, ST_MV + m * 2:ST_MV + m * 2 + 1],
                                           stats[:, ST_RSTD + m:ST_RSTD + m + 1]),
               waits=[("rstd2", m)], ms=("nmr2", m))
            for nn in range(8):
                bx = nn % 4
                w = list(z2all_list) + [("z2dma", 7, m)]
                if nn >= 4:
                    w.append(("t2n", m, nn - 4))
                elif m > 0:
                    w.append(("t2n", m - 1, nn + 4))
                dma("sync", Bf[:, T7 + bx * 512:T7 + (bx + 1) * 512],
                    z2_hbm[m][:, nn * 512:(nn + 1) * 512], waits=w, ms=("z2rd", m, nn))
                tb = nn % 4
                wn = [("z2rd", m, nn), ("nmr2", m)]
                if nn >= 4:
                    wn.append(("outdma", m, nn - 4))
                elif m > 0:
                    wn.append(("outdma", m - 1, nn + 4))
                op("vector",
                   lambda e, m=m, bx=bx, tb=tb: e.tensor_scalar(
                       Bf[:, O7 + tb * 512:O7 + (tb + 1) * 512],
                       Bf[:, T7 + bx * 512:T7 + (bx + 1) * 512],
                       stats[:, ST_RSTD + m:ST_RSTD + m + 1],
                       stats[:, ST_NMR + m:ST_NMR + m + 1],
                       ALU.mult, ALU.subtract),
                   waits=wn, ms=("t2n", m, nn))
                dma("gpsimd", out_d[m * 128:(m + 1) * 128, nn * 512:(nn + 1) * 512],
                    Bf[:, O7 + tb * 512:O7 + (tb + 1) * 512],
                    waits=[("t2n", m, nn)], ms=("outdma", m, nn))

        for i in range(QK["sync"]):
            E.final_wait("sync", f"s{i}")
        for i in range(QK["gpsimd"]):
            E.final_wait("gpsimd", f"g{i}")
        for i in range(QK["scalar"]):
            E.final_wait("scalar", f"c{i}")

    E0.begin_count()
    walk(0, E0)
    E1.begin_count()
    walk(1, E1)

    sems = {}
    with ExitStack() as semstack:
        for name in (["t", "v", "s", "p", "rr"] + [f"s{i}" for i in range(12)]
                     + [f"g{i}" for i in range(8)] + [f"c{i}" for i in range(8)]):
            sems[name] = semstack.enter_context(nc.semaphore(f"sem_{name}"))

        def section(engname):
            def body(eng):
                if engname == "sync":
                    eng.dma_start(out=rolesb[:, :], in_=role_d).then_inc(sems["rr"], 16)
                eng.wait_ge(sems["rr"], 16)
                with eng.register(f"role_{engname}") as r:
                    eng.reg_load(r, rolesb[0:1, 0:1])
                    for i in nc.Switch(engines=[eng], index=[eng.snap(r)], n=2):
                        EE = E0 if i == 0 else E1
                        EE.begin_emit(engname, eng, sems)
                        walk(i, EE)
            return body

        with nc.Block() as block:
            block.sync(section("sync"))
            block.gpsimd(section("gpsimd"))
            block.tensor(section("tensor"))
            block.vector(section("vector"))
            block.scalar(section("scalar"))
    es.close()
    return nc



def _rot_perm():
    p = np.arange(D)
    for h in range(NH):
        base = h * HD
        p[base:base + 64] = np.concatenate([np.arange(0, 64, 2), np.arange(1, 64, 2)]) + base
    return p


def _tables(idx, width):
    inv_freq = (1.0 / (10000.0 ** (np.arange(0, 64, 2, dtype=np.float32) / np.float32(64.0)))).astype(np.float32)
    t = idx.astype(np.float32)
    sinus = np.einsum("i,j->ij", t, inv_freq).astype(np.float32)
    sn = np.sin(sinus).astype(np.float32).T
    cs = np.cos(sinus).astype(np.float32).T
    out = np.zeros((128, width), np.float32)
    n = len(idx)
    out[0:32, 0:n] = cs
    out[32:64, 0:n] = cs
    out[64:96, 0:n] = sn
    out[96:128, 0:n] = sn
    return out.astype(BFNP)


def _in_maps(x, wq, wk, wv, wo, fw, fs):
    ident = np.eye(128, dtype=np.float32).astype(BFNP)
    mval = np.float32(NEGV / 16.0)
    tri = np.where(np.arange(128)[:, None] >= np.arange(128)[None, :],
                   np.float32(0.0), mval).astype(BFNP)
    NMXT = ACUT * 128
    maps = []
    for c in range(8):
        b, role = c // 2, c % 2
        if role == 0:
            myidx = np.arange(0, ACUT * 128)
        else:
            myidx = np.arange(ACUT * 128, S)
        xres = np.zeros((NMXT, D), np.float32)
        xres[0:len(myidx)] = x[b][myidx]
        maps.append({
            "xT": np.ascontiguousarray(x[b].T),
            "xres": xres,
            "wq": wq, "wk": wk, "wv": wv, "wo": wo, "fw": fw, "fs": fs,
            "rolein": np.array([[role]], dtype=np.uint32),
            "tabq": _tables(myidx, NMXT), "tabk": _tables(np.arange(S), S),
            "maska": tri, "ident": ident,
        })
    return maps


def kernel(L0_attn_hidden_states, L0_attn_Wq, L0_attn_Wk, L0_attn_Wv, L0_attn_Wo,
           L0_ffn_kernel, L0_ffn_scale):
    x = np.asarray(L0_attn_hidden_states)
    perm = _rot_perm()
    wq = np.ascontiguousarray(np.asarray(L0_attn_Wq)[:, perm])
    wk = np.ascontiguousarray(np.asarray(L0_attn_Wk)[:, perm])
    wv = np.ascontiguousarray(np.asarray(L0_attn_Wv))
    wo = np.ascontiguousarray(np.asarray(L0_attn_Wo))
    fw = np.ascontiguousarray(np.asarray(L0_ffn_kernel).view(np.uint8))
    fs = np.asarray(L0_ffn_scale, dtype=np.float32).reshape(1, 1)

    nc = build_nc()
    in_maps = _in_maps(x, wq, wk, wv, wo, fw, fs)
    res = run_bass_kernel_spmd(nc, in_maps, core_ids=list(range(8)))
    out = np.empty((4, S, D), np.float32)
    for c in range(8):
        b, role = c // 2, c % 2
        if role == 0:
            out[b, 0:ACUT * 128, :] = res.results[c]["out"][0:ACUT * 128]
        else:
            out[b, ACUT * 128:S, :] = res.results[c]["out"][0:S - ACUT * 128]
    return out


# version 17 (final)
# speedup vs baseline: 1.0733x; 1.0309x over previous
from contextlib import ExitStack

import numpy as np
import ml_dtypes

import concourse.bass as bass
import concourse.mybir as mybir
from concourse.bass_utils import run_bass_kernel_spmd

F32 = mybir.dt.float32
BF16 = mybir.dt.bfloat16
U8 = mybir.dt.uint8
U32 = mybir.dt.uint32
AF = mybir.ActivationFunctionType
ALU = mybir.AluOpType
AX = mybir.AxisListType

D = 4096
S = 2048
NH = 16
HD = 256
KD = D // 128
ACUT = 10
EPS = 1e-5
NEGV = -1e9
SCL = 1.0 / 16.0

BFNP = np.dtype(ml_dtypes.bfloat16)


def _split512(n):
    out = []
    while n > 0:
        c = min(n, 512)
        out.append(c)
        n -= c
    return out


class Sched:
    def __init__(self):
        self.mode = "count"
        self.cnt = {}
        self.ms = {}
        self.eng = None
        self.engname = None
        self.sems = None
        self.waited = None
        self.fence = {}

    def begin_count(self):
        self.mode = "count"
        self.cnt = {}
        self.ms = {}
        self.fence = {}

    def begin_emit(self, engname, eng, sems):
        self.mode = "emit"
        self.engname = engname
        self.eng = eng
        self.sems = sems
        self.waited = {}
        self.fence = {}

    def set_fence(self, keys):
        for en in ("sync", "gpsimd", "tensor", "vector", "scalar"):
            self.fence[en] = list(keys)

    def op(self, engname, fn, waits=(), inc=None, ms=None):
        pend = self.fence.pop(engname, None)
        if pend:
            waits = list(waits) + pend
        if self.mode == "count":
            for k in waits:
                assert k in self.ms, f"wait on undefined milestone {k}"
            if inc is not None:
                s, n = inc
                self.cnt[s] = self.cnt.get(s, 0) + n
                if ms is not None:
                    for k in (ms if isinstance(ms, list) else [ms]):
                        assert k not in self.ms, f"dup milestone {k}"
                        self.ms[k] = (s, self.cnt[s])
            else:
                assert ms is None, ms
            return
        if engname != self.engname:
            return
        for k in waits:
            s, v = self.ms[k]
            if v > self.waited.get(s, 0):
                self.eng.wait_ge(self.sems[s], v)
                self.waited[s] = v
        instr = fn(self.eng)
        if inc is not None:
            s, n = inc
            instr.then_inc(self.sems[s], n)

    def final_wait(self, engname, semname):
        if self.mode == "count" or engname != self.engname:
            return
        v = self.cnt.get(semname, 0)
        if v > self.waited.get(semname, 0):
            self.eng.wait_ge(self.sems[semname], v)
            self.waited[semname] = v


def build_nc():
    nc = bass.Bass("TRN2", target_bir_lowering=False)

    NQR = [ACUT, NH - ACUT]
    NKVR = [ACUT, 16]
    QB0R = [0, ACUT]
    NMX = max(NQR)

    xT_d = nc.dram_tensor("xT", [D, S], F32, kind="ExternalInput").ap()
    xres_d = nc.dram_tensor("xres", [NMX * 128, D], F32, kind="ExternalInput").ap()
    wq_d = nc.dram_tensor("wq", [D, D], F32, kind="ExternalInput").ap()
    wk_d = nc.dram_tensor("wk", [D, D], F32, kind="ExternalInput").ap()
    wv_d = nc.dram_tensor("wv", [D, D], F32, kind="ExternalInput").ap()
    wo_d = nc.dram_tensor("wo", [D, D], F32, kind="ExternalInput").ap()
    fw_d = nc.dram_tensor("fw", [D, D], U8, kind="ExternalInput").ap()
    fs_d = nc.dram_tensor("fs", [1, 1], F32, kind="ExternalInput").ap()
    role_d = nc.dram_tensor("rolein", [1, 1], U32, kind="ExternalInput").ap()
    tabq_d = nc.dram_tensor("tabq", [128, NMX * 128], BF16, kind="ExternalInput").ap()
    tabk_d = nc.dram_tensor("tabk", [128, S], BF16, kind="ExternalInput").ap()
    maska_d = nc.dram_tensor("maska", [128, 128], BF16, kind="ExternalInput").ap()
    ident_d = nc.dram_tensor("ident", [128, 128], BF16, kind="ExternalInput").ap()
    out_d = nc.dram_tensor("out", [NMX * 128, D], F32, kind="ExternalOutput").ap()

    qspill = nc.dram_tensor("qspill", [NH, HD, NMX * 128], BF16, kind="Internal").ap()
    kspill = nc.dram_tensor("kspill", [NH, HD, S], BF16, kind="Internal").ap()
    vspill = nc.dram_tensor("vspill", [NH, 128, 16 * HD], BF16, kind="Internal").ap()
    z_hbm = nc.dram_tensor("z_hbm", [NMX, 128, D], F32, kind="Internal").ap()
    x2_hbm = nc.dram_tensor("x2_hbm", [NMX, 128, D], F32, kind="Internal").ap()
    z2_hbm = nc.dram_tensor("z2_hbm", [NMX, 128, D], F32, kind="Internal").ap()

    es = ExitStack()
    A = es.enter_context(nc.sbuf_tensor("arenaA", [128, 65536], BF16))
    Bu = es.enter_context(nc.sbuf_tensor("arenaB", [128, 64512], U8))
    Bf = Bu.bitcast(F32)
    Bb = Bu.bitcast(BF16)
    identb = es.enter_context(nc.sbuf_tensor("identb", [128, 128], BF16))
    maska = es.enter_context(nc.sbuf_tensor("maska_s", [128, 128], BF16))
    tabqc = es.enter_context(nc.sbuf_tensor("tabqc", [64, NMX * 128], BF16))
    tabqs = es.enter_context(nc.sbuf_tensor("tabqs", [64, NMX * 128], BF16))
    tabkc = es.enter_context(nc.sbuf_tensor("tabkc", [64, S], BF16))
    tabks = es.enter_context(nc.sbuf_tensor("tabks", [64, S], BF16))
    stats = es.enter_context(nc.sbuf_tensor("stats", [128, 576], F32))
    ones1 = es.enter_context(nc.sbuf_tensor("ones1", [1, 128], F32))
    rolesb = es.enter_context(nc.sbuf_tensor("rolesb", [1, 1], U32))

    ST_BN = 0
    ST_MV = 480
    ST_RSTD = 500
    ST_NMR = 512
    ST_RS = 524
    ST_NB = 536
    ST_FS = 548
    ST_FSB = 549
    ST_FS2 = 550
    ST_NEGFS = 551
    ST_EPS = 552
    ST_RSP = 560

    PL = es.enter_context(nc.psum_tensor("PL", [128, 2048], F32))
    pb = [es.enter_context(nc.psum_tensor(f"pb{i}", [128, 512], F32)) for i in range(4)]
    trP = [p.bitcast(BF16) for p in pb[:2]]
    avP = [pb[2], pb[3]]

    def bank(i):
        if i < 4:
            return PL[:, i * 512:(i + 1) * 512]
        return pb[i - 4][:, :]

    E0, E1 = Sched(), Sched()

    def walk(role, E):
        NQ = NQR[role]
        NKV = NKVR[role]
        QB0 = QB0R[role]
        NM = NQ
        QW = NQ * 128
        KWA = NKV * 128
        T2 = NH * NQ

        QK = {"sync": 12, "gpsimd": 8, "scalar": 8}
        QPFX = {"sync": "s", "gpsimd": "g", "scalar": "c"}
        qc = {"sync": 0, "gpsimd": 0, "scalar": 0}

        def dma(q, out, in_, waits=(), ms=None):
            j = qc[q]
            qc[q] = j + 1
            sem = f"{QPFX[q]}{j % QK[q]}"
            w = list(waits)
            if j >= QK[q]:
                w.append(("_dma", q, j - QK[q]))
            mss = [("_dma", q, j)] + ([ms] if ms is not None else [])
            E.op(q, lambda e, o=out, i=in_: e.dma_start(out=o, in_=i),
                 w, inc=(sem, 16), ms=mss)

        def op(eng, fn, waits=(), ms=None):
            semname = {"tensor": "t", "vector": "v", "scalar": "s", "gpsimd": "p"}[eng]
            E.op(eng, fn, waits, inc=(semname, 1) if ms is not None else None, ms=ms)

        def dma_tail(q):
            j1 = qc[q]
            return [("_dma", q, j) for j in range(max(0, j1 - QK[q]), j1)]

        bankfree = [None] * 8

        def bwait(i):
            return [bankfree[i]] if bankfree[i] is not None else []

        dma("sync", identb[:, :], ident_d, ms=("d_ident",))
        dma("sync", maska[:, :], maska_d, ms=("d_maska",))
        dma("sync", tabqc[:, :], tabq_d[0:64, :], ms=("tabq",))
        dma("sync", tabqs[:, :], tabq_d[64:128, :], ms=("tabqs",))
        dma("sync", tabkc[:, :], tabk_d[0:64, :], ms=("tabk",))
        dma("sync", tabks[:, :], tabk_d[64:128, :], ms=("tabks",))
        dma("sync", stats[0:1, ST_FS:ST_FS + 1], fs_d, ms=("d_fs",))
        op("vector", lambda e: e.memset(ones1[:, :], 1.0), ms=("ones",))
        op("vector", lambda e: e.memset(stats[:, ST_EPS:ST_EPS + 1], float(EPS)), ms=("epsv",))
        op("tensor", lambda e: e.matmul(pb[0][:, 0:1], ones1[0:1, :],
                                        stats[0:1, ST_FS:ST_FS + 1], start=True, stop=True),
           waits=[("ones",), ("d_fs",)], ms=("fsmm",))
        op("vector", lambda e: e.tensor_copy(stats[:, ST_FSB:ST_FSB + 1], pb[0][:, 0:1]),
           waits=[("fsmm",)], ms=("fsb",))
        op("vector", lambda e: e.tensor_scalar_mul(stats[:, ST_FS2:ST_FS2 + 1],
                                                   stats[:, ST_FSB:ST_FSB + 1], 2.0),
           waits=[("fsb",)], ms=("fs2",))
        op("vector", lambda e: e.tensor_scalar_mul(stats[:, ST_NEGFS:ST_NEGFS + 1],
                                                   stats[:, ST_FSB:ST_FSB + 1], -1.0),
           waits=[("fsb",)], ms=("negfs",))
        bankfree[4] = ("fsb",)

        XH = KWA // 2
        XSTb = 0
        WQSTb = XSTb + 3 * XH * 4
        WKSTb = WQSTb + 2 * 512 * 4
        WVSTb = WKSTb + 2 * 512 * 4
        WQBFb = WVSTb + 2 * 1024 * 4
        WKBFb = WQBFb + 4 * 512 * 2
        WVBFb = WKBFb + 4 * 512 * 2
        QSTGb = WVBFb + 4 * 1024 * 2
        KSTGb = QSTGb + 2 * QW * 2
        VSTGb = KSTGb + 2 * KWA * 2
        assert VSTGb + 4096 * 2 <= 64512, VSTGb
        XST = XSTb // 4
        WQST, WKST, WVST = WQSTb // 4, WKSTb // 4, WVSTb // 4
        WQBF, WKBF, WVBF = WQBFb // 2, WKBFb // 2, WVBFb // 2
        QSTG, KSTG, VSTG = QSTGb // 2, KSTGb // 2, VSTGb // 2

        qsplit = _split512(QW)
        ksplit = _split512(KWA)
        nqc, nkc = len(qsplit), len(ksplit)
        vpasses = []
        c0 = 0
        while c0 < NKV:
            vpasses.append((c0, min(c0 + 8, NKV)))
            c0 += 8
        NVP = len(vpasses)

        for kk in range(KD):
            for hf in range(2):
                idx = 2 * kk + hf
                s = idx % 3
                if idx >= 3:
                    pk, ph = divmod(idx - 3, 2)
                    w = [("xcast", pk, ph)]
                else:
                    w = []
                dma("gpsimd", Bf[:, XST + s * XH:XST + (s + 1) * XH],
                    xT_d[kk * 128:(kk + 1) * 128, hf * XH:(hf + 1) * XH],
                    waits=w, ms=("xdma", kk, hf))
                op("scalar",
                   lambda e, kk=kk, hf=hf, s=s: e.copy(
                       A[:, kk * KWA + hf * XH:kk * KWA + (hf + 1) * XH],
                       Bf[:, XST + s * XH:XST + (s + 1) * XH]),
                   waits=[("xdma", kk, hf)], ms=("xcast", kk, hf))

        for p in range(8):
            h0 = 2 * p
            for hh in range(2):
                h = h0 + hh
                for k2 in range(16):
                    b = k2 % 2
                    if k2 >= 2:
                        w = [("qcast", p, hh, k2 - 2)]
                    elif hh == 1:
                        w = [("qcast", p, 0, 14 + k2)]
                    elif p > 0:
                        w = [("qcast", p - 1, 1, 14 + k2)]
                    else:
                        w = []
                    dma("sync", Bf[:, WQST + b * 512:WQST + (b + 1) * 512].rearrange("p (c m) -> p c m", m=256),
                        wq_d[k2 * 256:(k2 + 1) * 256, h * HD:(h + 1) * HD].rearrange("(c p) m -> p c m", p=128),
                        waits=w, ms=("qwd", p, hh, k2))
                    b4 = k2 % 4
                    wc = [("qwd", p, hh, k2)]
                    if k2 >= 4:
                        wc.append(("qmm", p, hh, k2 - 4))
                    elif hh == 1:
                        wc.append(("qmm", p, 0, 12 + k2))
                    elif p > 0:
                        wc.append(("qmm", p - 1, 1, 12 + k2))
                    op("vector",
                       lambda e, b=b, b4=b4: e.tensor_copy(Bb[:, WQBF + b4 * 512:WQBF + (b4 + 1) * 512],
                                                           Bf[:, WQST + b * 512:WQST + (b + 1) * 512]),
                       waits=wc, ms=("qcast", p, hh, k2))
                    for sub in range(2):
                        kk = 2 * k2 + sub
                        coff = 0
                        for n, csz in enumerate(qsplit):
                            for m in range(2):
                                wmm = [("qcast", p, hh, k2), ("xcast", kk, 0), ("xcast", kk, 1)]
                                if kk == 0:
                                    wmm += bwait(m * nqc + n)
                                op("tensor",
                                   lambda e, b4=b4, sub=sub, m=m, n=n, kk=kk, coff=coff, csz=csz: e.matmul(
                                       bank(m * nqc + n)[:, 0:csz],
                                       Bb[:, WQBF + b4 * 512 + sub * 256 + m * 128:
                                          WQBF + b4 * 512 + sub * 256 + (m + 1) * 128],
                                       A[:, kk * KWA + QB0 * 128 + coff:kk * KWA + QB0 * 128 + coff + csz],
                                       start=(kk == 0), stop=(kk == KD - 1)),
                                   waits=wmm if (kk == 0 or (m == 0 and n == 0)) else (),
                                   ms=("qmm", p, hh, k2) if (m == 1 and n == nqc - 1 and sub == 1) else None)
                            coff += csz
                for m in range(2):
                    coff = 0
                    for n, csz in enumerate(qsplit):
                        w = [("qmm", p, hh, 15)]
                        if hh == 1:
                            w.append(("qspill", p, 0))
                        elif p > 0:
                            w.append(("qspill", p - 1, 1))
                        mskey = ("qevb", p, hh, m * nqc + n)
                        op("vector",
                           lambda e, m=m, coff=coff, csz=csz, n=n: e.tensor_scalar_mul(
                               Bb[:, QSTG + m * QW + coff:QSTG + m * QW + coff + csz],
                               bank(m * nqc + n)[:, 0:csz], SCL),
                           waits=w, ms=mskey)
                        bankfree[m * nqc + n] = mskey
                        coff += csz
                dma("gpsimd",
                    qspill[h][:, 0:QW].rearrange("(c p) t -> p c t", p=128),
                    Bb[:, QSTG:QSTG + 2 * QW].rearrange("p (c t) -> p c t", t=QW),
                    waits=[("qevb", p, hh, m * nqc + n) for m in range(2) for n in range(nqc)],
                    ms=("qspill", p, hh))

            for hh in range(2):
                h = h0 + hh
                for k2 in range(16):
                    b = k2 % 2
                    if k2 >= 2:
                        w = [("kcast", p, hh, k2 - 2)]
                    elif hh == 1:
                        w = [("kcast", p, 0, 14 + k2)]
                    elif p > 0:
                        w = [("kcast", p - 1, 1, 14 + k2)]
                    else:
                        w = []
                    dma("sync", Bf[:, WKST + b * 512:WKST + (b + 1) * 512].rearrange("p (c m) -> p c m", m=256),
                        wk_d[k2 * 256:(k2 + 1) * 256, h * HD:(h + 1) * HD].rearrange("(c p) m -> p c m", p=128),
                        waits=w, ms=("kwd", p, hh, k2))
                    b4 = k2 % 4
                    wc = [("kwd", p, hh, k2)]
                    if k2 >= 4:
                        wc.append(("kmm", p, hh, k2 - 4))
                    elif hh == 1:
                        wc.append(("kmm", p, 0, 12 + k2))
                    elif p > 0:
                        wc.append(("kmm", p - 1, 1, 12 + k2))
                    op("vector",
                       lambda e, b=b, b4=b4: e.tensor_copy(Bb[:, WKBF + b4 * 512:WKBF + (b4 + 1) * 512],
                                                           Bf[:, WKST + b * 512:WKST + (b + 1) * 512]),
                       waits=wc, ms=("kcast", p, hh, k2))
                    for sub in range(2):
                        kk = 2 * k2 + sub
                        coff = 0
                        for n, csz in enumerate(ksplit):
                            for m in range(2):
                                wmm = [("kcast", p, hh, k2), ("xcast", kk, 0), ("xcast", kk, 1)]
                                if kk == 0:
                                    wmm += bwait(m * nkc + n)
                                op("tensor",
                                   lambda e, b4=b4, sub=sub, m=m, n=n, kk=kk, coff=coff, csz=csz: e.matmul(
                                       bank(m * nkc + n)[:, 0:csz],
                                       Bb[:, WKBF + b4 * 512 + sub * 256 + m * 128:
                                          WKBF + b4 * 512 + sub * 256 + (m + 1) * 128],
                                       A[:, kk * KWA + coff:kk * KWA + coff + csz],
                                       start=(kk == 0), stop=(kk == KD - 1)),
                                   waits=wmm if (kk == 0 or (m == 0 and n == 0)) else (),
                                   ms=("kmm", p, hh, k2) if (m == 1 and n == nkc - 1 and sub == 1) else None)
                            coff += csz
                for m in range(2):
                    coff = 0
                    for n, csz in enumerate(ksplit):
                        w = [("kmm", p, hh, 15)]
                        if hh == 1:
                            w.append(("kspill", p, 0))
                        elif p > 0:
                            w.append(("kspill", p - 1, 1))
                        mskey = ("kevb", p, hh, m * nkc + n)
                        op("vector",
                           lambda e, m=m, coff=coff, csz=csz, n=n: e.tensor_copy(
                               Bb[:, KSTG + m * KWA + coff:KSTG + m * KWA + coff + csz],
                               bank(m * nkc + n)[:, 0:csz]),
                           waits=w, ms=mskey)
                        bankfree[m * nkc + n] = mskey
                        coff += csz
                dma("gpsimd",
                    kspill[h][:, 0:KWA].rearrange("(c p) t -> p c t", p=128),
                    Bb[:, KSTG:KSTG + 2 * KWA].rearrange("p (c t) -> p c t", t=KWA),
                    waits=[("kevb", p, hh, m * nkc + n) for m in range(2) for n in range(nkc)],
                    ms=("kspill", p, hh))

            for ps, (ca, cb) in enumerate(vpasses):
                npc = cb - ca
                for k2 in range(16):
                    b = k2 % 2
                    if k2 >= 2:
                        w = [("vcast", p, ps, k2 - 2)]
                    elif ps > 0:
                        w = [("vcast", p, ps - 1, 14 + k2)]
                    elif p > 0:
                        w = [("vcast", p - 1, NVP - 1, 14 + k2)]
                    else:
                        w = []
                    dma("scalar", Bf[:, WVST + b * 1024:WVST + (b + 1) * 1024].rearrange("p (c m) -> p c m", m=512),
                        wv_d[k2 * 256:(k2 + 1) * 256, h0 * HD:h0 * HD + 512].rearrange("(c p) m -> p c m", p=128),
                        waits=w, ms=("vwd", p, ps, k2))
                    b4 = k2 % 4
                    wc = [("vwd", p, ps, k2)]
                    if k2 >= 4:
                        wc.append(("vmm", p, ps, k2 - 4))
                    elif ps > 0:
                        wc.append(("vmm", p, ps - 1, 12 + k2))
                    elif p > 0:
                        wc.append(("vmm", p - 1, NVP - 1, 12 + k2))
                    op("vector",
                       lambda e, b=b, b4=b4: e.tensor_copy(Bb[:, WVBF + b4 * 1024:WVBF + (b4 + 1) * 1024],
                                                           Bf[:, WVST + b * 1024:WVST + (b + 1) * 1024]),
                       waits=wc, ms=("vcast", p, ps, k2))
                    for sub in range(2):
                        kk = 2 * k2 + sub
                        for bidx in range(npc):
                            c = ca + bidx
                            wmm = [("vcast", p, ps, k2), ("xcast", kk, 0), ("xcast", kk, 1)]
                            if kk == 0:
                                wmm += bwait(bidx)
                            op("tensor",
                               lambda e, b4=b4, sub=sub, c=c, bidx=bidx, kk=kk: e.matmul(
                                   bank(bidx),
                                   A[:, kk * KWA + c * 128:kk * KWA + (c + 1) * 128],
                                   Bb[:, WVBF + b4 * 1024 + sub * 512:WVBF + b4 * 1024 + (sub + 1) * 512],
                                   start=(kk == 0), stop=(kk == KD - 1)),
                               waits=wmm if (kk == 0 or bidx == 0) else (),
                               ms=("vmm", p, ps, k2) if (bidx == npc - 1 and sub == 1) else None)
                for bidx in range(npc):
                    w = [("vmm", p, ps, 15)]
                    if ps > 0:
                        w += [("vspill", p, ps - 1, 0), ("vspill", p, ps - 1, 1)]
                    elif p > 0:
                        w += [("vspill", p - 1, NVP - 1, 0), ("vspill", p - 1, NVP - 1, 1)]
                    mskey = ("vevb", p, ps, bidx)
                    op("vector",
                       lambda e, bidx=bidx: e.tensor_copy(
                           Bb[:, VSTG + bidx * 512:VSTG + (bidx + 1) * 512], bank(bidx)),
                       waits=w, ms=mskey)
                    bankfree[bidx] = mskey
                for hh in range(2):
                    h = h0 + hh
                    dma("gpsimd",
                        vspill[h].rearrange("p (c d) -> p c d", d=HD)[:, ca:cb, :],
                        Bb[:, VSTG:VSTG + npc * 512].rearrange("p (c two) -> p c two", two=512)[:, :, hh * HD:(hh + 1) * HD],
                        waits=[("vevb", p, ps, i) for i in range(npc)],
                        ms=("vspill", p, ps, hh))

        p1done = ("vspill", 7, NVP - 1, 1)
        p1all = [("vspill", 7, NVP - 1, 0), ("vspill", 7, NVP - 1, 1)]
        npc_last = vpasses[-1][1] - vpasses[-1][0]
        E.set_fence([("vmm", 7, NVP - 1, 15), ("vevb", 7, NVP - 1, npc_last - 1),
                     ("xcast", 31, 0), ("xcast", 31, 1)] + dma_tail("sync") + dma_tail("gpsimd") + dma_tail("scalar"))

        QKSZ = 2 * QW + 2 * KWA
        VSZ = NKV * 256
        ANW = (QB0 + NQ) * 128
        QKB0 = 65536 - 4 * QKSZ - 3 * ANW
        ATB0 = QKB0 + 4 * QKSZ
        assert QKB0 >= 32 * QW, (QKB0, QW)
        SWQb = 0
        SWKb = SWQb + 2 * QW * 2
        SCQb = SWKb + 2 * KWA * 2
        SCKb = SCQb + 2 * QW * 2
        AUBb = SCKb + 2 * KWA * 2
        VBb = AUBb + 3 * ANW * 2
        SVBb = VBb + 3 * VSZ * 2
        assert SVBb + 3 * 16 * 4 <= 64512, SVBb
        SWQ, SWK, SCQ, SCK = SWQb // 2, SWKb // 2, SCQb // 2, SCKb // 2
        AUB0 = AUBb // 2
        VB0 = VBb // 2
        SVB0 = SVBb // 4

        grp_hist = []

        def qkloc(h):
            qb = QKB0 + (h % 4) * QKSZ
            return qb, qb + 2 * QW

        def vloc(h):
            return VB0 + (h % 3) * VSZ

        def emit_qk_loads(h):
            qb, kb = qkloc(h)
            par = h % 2
            swq = SWQ + par * QW
            swk = SWK + par * KWA
            wl = [("qspill", h // 2, h % 2), ("kspill", h // 2, h % 2)] + p1all
            if h >= 4:
                wl.append(("lgp", (h - 4) * NQ + NQ - 1))
            dma("gpsimd", A[:, qb:qb + 2 * QW].rearrange("p (c t) -> p c t", t=QW),
                qspill[h][:, 0:QW].rearrange("(c p) t -> p c t", p=128),
                waits=wl, ms=("ldq", h))
            dma("sync", A[:, kb:kb + 2 * KWA].rearrange("p (c t) -> p c t", t=KWA),
                kspill[h][:, 0:KWA].rearrange("(c p) t -> p c t", p=128),
                waits=wl, ms=("ldk", h))
            wr = [("ldq", h)] + ([("ropeq", h - 2)] if h >= 2 else [])
            dma("gpsimd", Bb[0:32, swq:swq + QW], A[32:64, qb:qb + QW], waits=wr, ms=("swq1", h))
            dma("gpsimd", Bb[32:64, swq:swq + QW], A[0:32, qb:qb + QW], waits=wr, ms=("swq2", h))
            wr = [("ldk", h)] + ([("ropek", h - 2)] if h >= 2 else [])
            dma("gpsimd", Bb[0:32, swk:swk + KWA], A[32:64, kb:kb + KWA], waits=wr, ms=("swk1", h))
            dma("gpsimd", Bb[32:64, swk:swk + KWA], A[0:32, kb:kb + KWA], waits=wr, ms=("swk2", h))

        def emit_v_load(h):
            vb = vloc(h)
            wl = [("vspill", h // 2, psx, h % 2) for psx in range(NVP)] + p1all
            if h >= 3:
                wl.append(("avmm", (h - 3) * NQ + NQ - 1))
            dma("scalar", Bb[:, vb:vb + VSZ], vspill[h][:, 0:NKV * 256],
                waits=wl, ms=("ldv", h))

        def emit_rope(h):
            qb, kb = qkloc(h)
            par = h % 2
            swq = SWQ + par * QW
            swk = SWK + par * KWA
            scq = SCQ + par * QW
            sck = SCK + par * KWA
            wv_ = [("ldq", h), ("tabq",), ("tabqs",)] + ([("ropeq", h - 2)] if h >= 2 else [])
            op("vector", lambda e, qb=qb, scq=scq: e.tensor_mul(
                   Bb[0:64, scq:scq + QW], A[0:64, qb:qb + QW], tabqc[:, 0:QW]),
               waits=wv_, ms=("rq1", h))
            op("vector", lambda e, swq=swq: e.tensor_mul(
                   Bb[0:64, swq:swq + QW], Bb[0:64, swq:swq + QW], tabqs[:, 0:QW]),
               waits=[("swq1", h), ("swq2", h)], ms=("rq2", h))
            op("vector", lambda e, qb=qb, scq=scq, swq=swq: e.tensor_add(
                   A[0:64, qb:qb + QW], Bb[0:64, scq:scq + QW], Bb[0:64, swq:swq + QW]),
               waits=[("rq1", h), ("rq2", h)], ms=("ropeq", h))
            wv_ = [("ldk", h), ("tabk",), ("tabks",)] + ([("ropek", h - 2)] if h >= 2 else [])
            op("vector", lambda e, kb=kb, sck=sck: e.tensor_mul(
                   Bb[0:64, sck:sck + KWA], A[0:64, kb:kb + KWA], tabkc[:, 0:KWA]),
               waits=wv_, ms=("rk1", h))
            op("vector", lambda e, swk=swk: e.tensor_mul(
                   Bb[0:64, swk:swk + KWA], Bb[0:64, swk:swk + KWA], tabks[:, 0:KWA]),
               waits=[("swk1", h), ("swk2", h)], ms=("rk2", h))
            op("vector", lambda e, kb=kb, sck=sck, swk=swk: e.tensor_add(
                   A[0:64, kb:kb + KWA], Bb[0:64, sck:sck + KWA], Bb[0:64, swk:swk + KWA]),
               waits=[("rk1", h), ("rk2", h)], ms=("ropek", h))

        pl_bank_last = {i: bankfree[i] for i in range(4)}
        tk_base = {}
        pl_flip = [0]

        def pl_banks(t):
            jl = t % NQ
            W = (QB0 + jl + 1) * 128
            b0 = tk_base[t]
            return list(range(b0 // 512, (b0 + W + 511) // 512))

        def emit_lg(t):
            h, jl = divmod(t, NQ)
            klen = QB0 + jl + 1
            W = klen * 128
            if role == 0 and klen <= 8:
                b0 = pl_flip[0] * 1024
                pl_flip[0] ^= 1
            else:
                b0 = 0
            tk_base[t] = b0
            qb, kb = qkloc(h)
            wlog = [("ropeq", h), ("ropek", h), ("d_maska",), ("d_ident",)]
            if t == 0:
                wlog.append(p1done)
            for bi in pl_banks(t):
                if pl_bank_last[bi] is not None:
                    wlog.append(pl_bank_last[bi])
            first = True
            coff = 0
            chunks = _split512(W)
            for ci, csz in enumerate(chunks):
                isdiag = (ci == len(chunks) - 1)
                for m in range(2):
                    stop = (m == 1 and not isdiag)
                    op("tensor",
                       lambda e, m=m, coff=coff, csz=csz, qb=qb, kb=kb, jl=jl,
                       stop=stop, b0=b0: e.matmul(
                           PL[:, b0 + coff:b0 + coff + csz],
                           A[:, qb + m * QW + jl * 128:qb + m * QW + (jl + 1) * 128],
                           A[:, kb + m * KWA + coff:kb + m * KWA + coff + csz],
                           start=(m == 0), stop=stop),
                       waits=wlog if first else (),
                       ms=("lgh", t) if (stop and ci == len(chunks) - 2) else None)
                    first = False
                if isdiag:
                    op("tensor",
                       lambda e, W=W, b0=b0: e.matmul(
                           PL[:, b0 + W - 128:b0 + W], identb[:, :], maska[:, :],
                           start=False, stop=True),
                       ms=("lgp", t))
                coff += csz

        def emit_exp(t):
            h, jl = divmod(t, NQ)
            gp = t % 3
            W = (QB0 + jl + 1) * 128
            b0 = tk_base[t]
            SVX = SVB0 + 16 * gp
            AUB = AUB0 + ANW * gp
            wre = [("trl", t - 3), ("rcp", t - 3)] if t >= 3 else []
            if W > 1024:
                op("scalar", lambda e, SVX=SVX, AUB=AUB, b0=b0: e.activation(
                       Bb[:, AUB:AUB + 1024], PL[:, b0:b0 + 1024], AF.Exp,
                       bias=0.0, scale=1.0,
                       accum_out=Bf[:, SVX + 7:SVX + 8]),
                   waits=[("lgh", t)] + wre, ms=("expa", t))
                op("scalar", lambda e, W=W, SVX=SVX, AUB=AUB, b0=b0: e.activation(
                       Bb[:, AUB + 1024:AUB + W], PL[:, b0 + 1024:b0 + W], AF.Exp,
                       bias=0.0, scale=1.0,
                       accum_out=Bf[:, SVX + 8:SVX + 9]),
                   waits=[("lgp", t)], ms=("expb", t))
                op("vector", lambda e, SVX=SVX: e.tensor_add(
                       Bf[:, SVX + 5:SVX + 6], Bf[:, SVX + 7:SVX + 8],
                       Bf[:, SVX + 8:SVX + 9]),
                   waits=[("expa", t), ("expb", t)], ms=("exp", t))
                for bi in pl_banks(t):
                    pl_bank_last[bi] = ("expa", t) if ((bi + 1) * 512 <= b0 + 1024) else ("expb", t)
            else:
                op("scalar", lambda e, W=W, SVX=SVX, AUB=AUB, b0=b0: e.activation(
                       Bb[:, AUB:AUB + W], PL[:, b0:b0 + W], AF.Exp,
                       bias=0.0, scale=1.0,
                       accum_out=Bf[:, SVX + 5:SVX + 6]),
                   waits=[("lgp", t)] + wre, ms=("exp", t))
                for bi in pl_banks(t):
                    pl_bank_last[bi] = ("exp", t)

        def emit_smx_back(t):
            gp = t % 3
            jl = t % NQ
            W = (QB0 + jl + 1) * 128
            SVX = SVB0 + 16 * gp
            AUB = AUB0 + ANW * gp
            op("vector", lambda e, SVX=SVX: e.reciprocal(
                   Bf[:, SVX + 6:SVX + 7], Bf[:, SVX + 5:SVX + 6]),
               waits=[("exp", t)], ms=("rcp", t))
            op("vector", lambda e, W=W, AUB=AUB, SVX=SVX: e.tensor_scalar_mul(
                   Bb[:, AUB:AUB + W], Bb[:, AUB:AUB + W], Bf[:, SVX + 6:SVX + 7]),
               waits=[("rcp", t)], ms=("nrm", t))

        def emit_tr(t):
            h, jl = divmod(t, NQ)
            klen = QB0 + jl + 1
            gp = t % 3
            AUB = AUB0 + ANW * gp
            AT = ATB0 + ANW * gp
            ngrp = (klen + 3) // 4
            for g in range(ngrp):
                kks = list(range(4 * g, min(4 * g + 4, klen)))
                tb = len(grp_hist) % 2
                wtr = [("nrm", t)]
                if len(grp_hist) >= 2:
                    wtr.append(grp_hist[-2])
                elif bankfree[4 + tb] is not None:
                    wtr.append(bankfree[4 + tb])
                for i, kk in enumerate(kks):
                    op("tensor",
                       lambda e, tb=tb, kk=kk, i=i, AUB=AUB: e.transpose(
                           trP[tb][:, i * 128:(i + 1) * 128],
                           Bb[:, AUB + kk * 128:AUB + (kk + 1) * 128], identb[:, :]),
                       waits=wtr if i == 0 else (),
                       ms=("trl", t) if kk == klen - 1 else ("trx", t, kk))
                wge = [("trl", t) if kks[-1] == klen - 1 else ("trx", t, kks[-1])]
                if t >= 3:
                    wge.append(("avmm", t - 3))
                gkey = ("grpev", t, g)
                op("vector",
                   lambda e, tb=tb, g=g, n=len(kks), AT=AT: e.tensor_copy(
                       A[:, AT + g * 512:AT + g * 512 + n * 128],
                       trP[tb][:, 0:n * 128]),
                   waits=wge, ms=gkey)
                grp_hist.append(gkey)

        def emit_av(t):
            h, jl = divmod(t, NQ)
            klen = QB0 + jl + 1
            gp = t % 3
            vb = vloc(h)
            AT = ATB0 + ANW * gp
            for kk in range(klen):
                wav = [("grpev", t, kk // 4)]
                if kk == 0:
                    wav.append(("avev", t - 1) if t > 0 else p1done)
                    wav.append(("ldv", h))
                    if t == 0:
                        wav += bwait(6) + bwait(7)
                for dh in range(2):
                    op("tensor",
                       lambda e, dh=dh, kk=kk, klen=klen, vb=vb, AT=AT: e.matmul(
                           avP[dh][:, 0:128],
                           Bb[:, vb + kk * 256 + dh * 128:vb + kk * 256 + (dh + 1) * 128],
                           A[:, AT + kk * 128:AT + (kk + 1) * 128],
                           start=(kk == 0), stop=(kk == klen - 1)),
                       waits=wav if (dh == 0 or kk == 0) else (),
                       ms=("avmm", t) if (kk == klen - 1 and dh == 1) else None)

        def emit_avev(t):
            h, jl = divmod(t, NQ)
            for dh in range(2):
                op("scalar",
                   lambda e, dh=dh, h=h, jl=jl: e.copy(
                       A[:, (2 * h + dh) * QW + jl * 128:(2 * h + dh) * QW + (jl + 1) * 128],
                       avP[dh][:, 0:128]),
                   waits=[("avmm", t)], ms=("avev", t) if dh == 1 else None)

        emit_qk_loads(0)
        emit_qk_loads(1)
        emit_rope(0)
        emit_rope(1)
        emit_qk_loads(2)
        emit_qk_loads(3)
        emit_v_load(0)
        emit_v_load(1)
        emit_v_load(2)
        for u in range(T2 + 2):
            if u - 2 >= 0:
                emit_av(u - 2)
                emit_avev(u - 2)
            if u < T2:
                h, jl = divmod(u, NQ)
                if jl == 0 and 1 <= h <= NH - 4:
                    emit_qk_loads(h + 3)
                if jl == 1 and 1 <= h <= NH - 3:
                    emit_v_load(h + 2)
                if jl == NQ - 3 and h + 2 < NH:
                    emit_rope(h + 2)
                emit_lg(u)
            if 0 <= u - 1 < T2:
                emit_smx_back(u - 1)
            if u < T2:
                emit_exp(u)
            if 0 <= u - 1 < T2:
                emit_tr(u - 1)
        p2done = ("avev", T2 - 1)
        bankfree = [None] * 8
        E.set_fence([("avmm", T2 - 1), p2done, grp_hist[-1],
                     ("ropeq", NH - 1), ("ropek", NH - 1)]
                    + dma_tail("sync") + dma_tail("gpsimd") + dma_tail("scalar"))

        WOSTb4 = 0
        WOBFb4 = WOSTb4 + 2 * 1024 * 4
        ZSTb4 = WOBFb4 + 16384 * 2
        XRSb4 = ZSTb4 + 4 * 512 * 4
        assert XRSb4 + 4 * 512 * 4 <= 64512
        WOST4, WOBF4 = WOSTb4 // 4, WOBFb4 // 2
        ZST4, XRS4 = ZSTb4 // 4, XRSb4 // 4
        groupsM = [list(range(0, min(NM, 8)))] + ([list(range(8, NM))] if NM > 8 else [])
        NG = len(groupsM)
        zlast = {}
        xlast = {}

        def wo_half_free(nn, half):
            if nn == 0:
                return None
            return ("wommH0", nn - 1, NG - 1) if half == 0 else ("womm", nn - 1, NG - 1)

        for nn in range(8):
            for k2 in range(16):
                b = k2 % 2
                half = k2 // 8
                if k2 >= 2:
                    w = [("wocast", nn, k2 - 2)]
                elif nn > 0:
                    w = [("wocast", nn - 1, 14 + k2)]
                else:
                    w = [p2done]
                dma("sync", Bf[:, WOST4 + b * 1024:WOST4 + (b + 1) * 1024].rearrange("p (c m) -> p c m", m=512),
                    wo_d[k2 * 256:(k2 + 1) * 256, nn * 512:(nn + 1) * 512].rearrange("(c p) m -> p c m", p=128),
                    waits=w, ms=("wod", nn, k2))
                wc = [("wod", nn, k2)]
                whf = wo_half_free(nn, half)
                if whf is not None:
                    wc.append(whf)
                if nn == 0 and k2 < 2:
                    wc.append(p2done)
                op("scalar",
                   lambda e, b=b, k2=k2: e.copy(Bb[:, WOBF4 + k2 * 1024:WOBF4 + (k2 + 1) * 1024],
                                                Bf[:, WOST4 + b * 1024:WOST4 + (b + 1) * 1024]),
                   waits=wc, ms=("wocast", nn, k2))
            for gi, ms_ in enumerate(groupsM):
                for kk in range(KD):
                    for ii, m in enumerate(ms_):
                        wmm = [("wocast", nn, kk // 2)]
                        if kk == 0:
                            wmm += bwait(ii) + [p2done]
                        if kk == KD - 1 and ii == len(ms_) - 1:
                            mskey = ("womm", nn, gi)
                        elif kk == 15 and ii == len(ms_) - 1:
                            mskey = ("wommH0", nn, gi)
                        else:
                            mskey = None
                        op("tensor",
                           lambda e, m=m, ii=ii, kk=kk: e.matmul(
                               bank(ii),
                               A[:, kk * QW + m * 128:kk * QW + (m + 1) * 128],
                               Bb[:, WOBF4 + kk * 512:WOBF4 + (kk + 1) * 512],
                               start=(kk == 0), stop=(kk == KD - 1)),
                           waits=wmm if (ii == 0 or kk == 0) else (),
                           ms=mskey)
                for ii, m in enumerate(ms_):
                    zb = m % 4
                    wx = [p2done]
                    if ("x", zb) in xlast:
                        wx.append(xlast[("x", zb)])
                    dma("sync", Bf[:, XRS4 + zb * 512:XRS4 + (zb + 1) * 512],
                        xres_d[m * 128:(m + 1) * 128, nn * 512:(nn + 1) * 512],
                        waits=wx, ms=("xrsd", nn, m))
                    w = [("womm", nn, gi), ("xrsd", nn, m)]
                    if ("z", zb) in zlast:
                        w.append(zlast[("z", zb)])
                    mskey = ("zev", nn, m)
                    op("vector",
                       lambda e, ii=ii, zb=zb: e.tensor_add(
                           Bf[:, ZST4 + zb * 512:ZST4 + (zb + 1) * 512], bank(ii),
                           Bf[:, XRS4 + zb * 512:XRS4 + (zb + 1) * 512]),
                       waits=w, ms=mskey)
                    bankfree[ii] = mskey
                    xlast[("x", zb)] = mskey
                    op("vector",
                       lambda e, m=m, nn=nn, zb=zb: e.bn_stats(
                           stats[:, ST_BN + (m * 8 + nn) * 6:ST_BN + (m * 8 + nn + 1) * 6],
                           Bf[:, ZST4 + zb * 512:ZST4 + (zb + 1) * 512]),
                       waits=[("zev", nn, m)], ms=("bnst", m, nn))
                    dma("gpsimd", z_hbm[m][:, nn * 512:(nn + 1) * 512],
                        Bf[:, ZST4 + zb * 512:ZST4 + (zb + 1) * 512],
                        waits=[("zev", nn, m), ("bnst", m, nn)], ms=("zdma", nn, m))
                    zlast[("z", zb)] = ("zdma", nn, m)

        zall_list = [("zdma", 7, i) for i in range(NM)]
        E.set_fence([("womm", 7, NG - 1), ("bnst", groupsM[-1][-1], 7),
                     ("wocast", 7, 15)] + dma_tail("sync") + dma_tail("gpsimd") + dma_tail("scalar"))

        TSLb5 = 0
        X2Fb5 = TSLb5 + 4 * 512 * 4
        X2Bb5 = X2Fb5 + 4 * 512 * 4
        TSL5, X2F5 = TSLb5 // 4, X2Fb5 // 4
        X2B5 = X2Bb5 // 2
        def p5_stats(m):
            op("vector",
               lambda e, m=m: e.bn_aggr(stats[:, ST_MV + m * 2:ST_MV + (m + 1) * 2],
                                        stats[:, ST_BN + m * 48:ST_BN + (m + 1) * 48]),
               waits=[("bnst", m, n2) for n2 in range(8)], ms=("mv", m))
            op("scalar",
               lambda e, m=m: e.activation(stats[:, ST_RSTD + m:ST_RSTD + m + 1],
                                           stats[:, ST_MV + m * 2 + 1:ST_MV + m * 2 + 2],
                                           AF.Ln, bias=stats[:, ST_EPS:ST_EPS + 1], scale=1.0),
               waits=[("mv", m), ("epsv",)], ms=("lnv", m))
            op("scalar",
               lambda e, m=m: e.activation(stats[:, ST_RSTD + m:ST_RSTD + m + 1],
                                           stats[:, ST_RSTD + m:ST_RSTD + m + 1],
                                           AF.Exp, bias=0.0, scale=-0.5),
               waits=[("lnv", m)], ms=("rstd", m))
            op("vector",
               lambda e, m=m: e.tensor_mul(stats[:, ST_NMR + m:ST_NMR + m + 1],
                                           stats[:, ST_MV + m * 2:ST_MV + m * 2 + 1],
                                           stats[:, ST_RSTD + m:ST_RSTD + m + 1]),
               waits=[("rstd", m)], ms=("nmr", m))

        def p5_front(m, nn):
            bx = nn % 4
            w = list(zall_list) + [("zdma", 7, m)]
            if nn >= 4:
                w.append(("x2n", m, nn - 4))
            elif m > 0:
                w.append(("x2n", m - 1, nn + 4))
            dma("sync", Bf[:, TSL5 + bx * 512:TSL5 + (bx + 1) * 512],
                z_hbm[m][:, nn * 512:(nn + 1) * 512], waits=w, ms=("zrd", m, nn))
            tb = nn % 4
            wn = [("zrd", m, nn), ("nmr", m)]
            if nn >= 4:
                wn.append(("x2dma", m, nn - 4))
            elif m > 0:
                wn.append(("x2dma", m - 1, nn + 4))
            op("vector",
               lambda e, m=m, bx=bx, tb=tb: e.tensor_scalar(
                   Bf[:, X2F5 + tb * 512:X2F5 + (tb + 1) * 512],
                   Bf[:, TSL5 + bx * 512:TSL5 + (bx + 1) * 512],
                   stats[:, ST_RSTD + m:ST_RSTD + m + 1],
                   stats[:, ST_NMR + m:ST_NMR + m + 1],
                   ALU.mult, ALU.subtract),
               waits=wn, ms=("x2n", m, nn))
            dma("gpsimd", x2_hbm[m][:, nn * 512:(nn + 1) * 512],
                Bf[:, X2F5 + tb * 512:X2F5 + (tb + 1) * 512],
                waits=[("x2n", m, nn)], ms=("x2dma", m, nn))
            wxc = [("x2n", m, nn)]
            if nn >= 4:
                wxc.append(("x2t", m, (nn - 4) * 4 + 3))
            elif m > 0:
                wxc.append(("x2t", m - 1, (nn + 4) * 4 + 3))
            op("scalar",
               lambda e, tb=tb: e.copy(
                   Bb[:, X2B5 + tb * 512:X2B5 + (tb + 1) * 512],
                   Bf[:, X2F5 + tb * 512:X2F5 + (tb + 1) * 512]),
               waits=wxc, ms=("x2c", m, nn))

        def p5_back(m, nn):
            tb = nn % 4
            wrsp = [("x2c", m, nn)]
            if m > 0 and nn == 0:
                wrsp.append(("rs", m - 1))
            op("vector",
               lambda e, nn=nn, tb=tb: e.reduce_sum(
                   stats[:, ST_RSP + nn:ST_RSP + nn + 1],
                   Bb[:, X2B5 + tb * 512:X2B5 + (tb + 1) * 512], axis=AX.X),
               waits=wrsp, ms=("rsp", m, nn))
            pbk = len(grp_hist) % 2
            wtr = [("x2c", m, nn), p2done]
            if len(grp_hist) >= 2:
                wtr.append(grp_hist[-2])
            for t4 in range(4):
                kkc = nn * 4 + t4
                op("tensor",
                   lambda e, pbk=pbk, tb=tb, t4=t4: e.transpose(
                       trP[pbk][:, t4 * 128:(t4 + 1) * 128],
                       Bb[:, X2B5 + tb * 512 + t4 * 128:X2B5 + tb * 512 + (t4 + 1) * 128],
                       identb[:, :]),
                   waits=wtr if t4 == 0 else (), ms=("x2t", m, kkc))
            gkey = ("x2g", m, nn)
            op("vector",
               lambda e, pbk=pbk, m=m, nn=nn: e.tensor_copy(
                   A[:, 0:32 * NM * 128].rearrange("p (c t) -> p c t", t=NM * 128)
                   [:, nn * 4:nn * 4 + 4, m * 128:(m + 1) * 128],
                   trP[pbk][:, 0:512].rearrange("p (c t) -> p c t", t=128)),
               waits=[("x2t", m, nn * 4 + 3)], ms=gkey)
            grp_hist.append(gkey)
            if nn == 7:
                op("vector",
                   lambda e, m=m: e.reduce_sum(stats[:, ST_RS + m:ST_RS + m + 1],
                                               stats[:, ST_RSP:ST_RSP + 8], axis=AX.X),
                   waits=[("rsp", m, n2) for n2 in range(8)], ms=("rs", m))

        chunksP5 = [(m, nn) for m in range(NM) for nn in range(8)]
        for j5 in range(len(chunksP5) + 1):
            if j5 < len(chunksP5):
                m5, nn5 = chunksP5[j5]
                if nn5 == 0:
                    p5_stats(m5)
                p5_front(m5, nn5)
            if j5 >= 1:
                p5_back(*chunksP5[j5 - 1])
        op("vector",
           lambda e: e.tensor_scalar_mul(stats[:, ST_NB:ST_NB + NM], stats[:, ST_RS:ST_RS + NM],
                                         stats[:, ST_NEGFS:ST_NEGFS + 1]),
           waits=[("rs", m2) for m2 in range(NM)] + [("negfs",)], ms=("nb",))

        x2tall = ("x2g", NM - 1, 7)
        bankfree = [None] * 8
        E.set_fence([("x2t", NM - 1, 31), ("nb",), ("x2c", NM - 1, 7)]
                    + dma_tail("sync") + dma_tail("gpsimd") + dma_tail("scalar"))

        FWSTb6 = 0
        FWBFb6 = FWSTb6 + 2 * 1024
        X2RSb6 = FWBFb6 + 16384 * 2
        Z2STb6 = X2RSb6 + 4 * 512 * 4
        assert Z2STb6 + 4 * 512 * 4 <= 64512
        FWST6 = FWSTb6
        FWBF6 = FWBFb6 // 2
        X2RS6, Z2ST6 = X2RSb6 // 4, Z2STb6 // 4
        zlast = {}
        xlast = {}

        for nn in range(8):
            for k2 in range(16):
                b = k2 % 2
                half = k2 // 8
                if k2 >= 2:
                    w = [("fwcast", nn, k2 - 2)]
                elif nn > 0:
                    w = [("fwcast", nn - 1, 14 + k2)]
                else:
                    w = [("nb",), x2tall]
                dma("sync", Bu[:, FWST6 + b * 1024:FWST6 + (b + 1) * 1024].rearrange("p (c m) -> p c m", m=512),
                    fw_d[k2 * 256:(k2 + 1) * 256, nn * 512:(nn + 1) * 512].rearrange("(c p) m -> p c m", p=128),
                    waits=w, ms=("fwd", nn, k2))
                wc = [("fwd", nn, k2)]
                if nn > 0:
                    wc.append(("fmmH0", nn - 1, NG - 1) if half == 0
                              else ("fmm", nn - 1, NG - 1))
                if nn == 0 and k2 < 2:
                    wc += [("nb",), x2tall]
                op("scalar",
                   lambda e, b=b, k2=k2: e.copy(Bb[:, FWBF6 + k2 * 1024:FWBF6 + (k2 + 1) * 1024],
                                                Bu[:, FWST6 + b * 1024:FWST6 + (b + 1) * 1024]),
                   waits=wc, ms=("fwcast", nn, k2))
            for gi, ms_ in enumerate(groupsM):
                for kk in range(KD):
                    for ii, m in enumerate(ms_):
                        wmm = [("fwcast", nn, kk // 2)]
                        if kk == 0:
                            wmm += bwait(ii) + [x2tall]
                        if kk == KD - 1 and ii == len(ms_) - 1:
                            mskey = ("fmm", nn, gi)
                        elif kk == 15 and ii == len(ms_) - 1:
                            mskey = ("fmmH0", nn, gi)
                        else:
                            mskey = None
                        op("tensor",
                           lambda e, m=m, ii=ii, kk=kk: e.matmul(
                               bank(ii),
                               A[:, kk * NM * 128 + m * 128:kk * NM * 128 + (m + 1) * 128],
                               Bb[:, FWBF6 + kk * 512:FWBF6 + (kk + 1) * 512],
                               start=(kk == 0), stop=(kk == KD - 1)),
                           waits=wmm if (ii == 0 or kk == 0) else (),
                           ms=mskey)
                for ii, m in enumerate(ms_):
                    zb = m % 4
                    wx = [("nb",), x2tall, ("x2dma", m, nn)]
                    if ("x", zb) in xlast:
                        wx.append(xlast[("x", zb)])
                    dma("sync", Bf[:, X2RS6 + zb * 512:X2RS6 + (zb + 1) * 512],
                        x2_hbm[m][:, nn * 512:(nn + 1) * 512],
                        waits=wx, ms=("x2sl", nn, m))
                    op("vector",
                       lambda e, m=m, zb=zb: e.tensor_scalar_add(
                           Bf[:, X2RS6 + zb * 512:X2RS6 + (zb + 1) * 512],
                           Bf[:, X2RS6 + zb * 512:X2RS6 + (zb + 1) * 512],
                           stats[:, ST_NB + m:ST_NB + m + 1]),
                       waits=[("x2sl", nn, m), ("nb",)], ms=("xadj", nn, m))
                    w = [("fmm", nn, gi), ("xadj", nn, m), ("x2sl", nn, m), ("fs2",)]
                    if ("z", zb) in zlast:
                        w.append(zlast[("z", zb)])
                    mskey = ("z2ev", nn, m)
                    op("vector",
                       lambda e, ii=ii, zb=zb: e.scalar_tensor_tensor(
                           Bf[:, Z2ST6 + zb * 512:Z2ST6 + (zb + 1) * 512], bank(ii),
                           stats[:, ST_FS2:ST_FS2 + 1],
                           Bf[:, X2RS6 + zb * 512:X2RS6 + (zb + 1) * 512],
                           ALU.mult, ALU.add),
                       waits=w, ms=mskey)
                    bankfree[ii] = mskey
                    xlast[("x", zb)] = mskey
                    wb = [("z2ev", nn, m)]
                    if nn == 0:
                        wb.append(("mv", NM - 1))
                    op("vector",
                       lambda e, m=m, nn=nn, zb=zb: e.bn_stats(
                           stats[:, ST_BN + (m * 8 + nn) * 6:ST_BN + (m * 8 + nn + 1) * 6],
                           Bf[:, Z2ST6 + zb * 512:Z2ST6 + (zb + 1) * 512]),
                       waits=wb, ms=("bnst2", m, nn))
                    dma("gpsimd", z2_hbm[m][:, nn * 512:(nn + 1) * 512],
                        Bf[:, Z2ST6 + zb * 512:Z2ST6 + (zb + 1) * 512],
                        waits=[("bnst2", m, nn)], ms=("z2dma", nn, m))
                    zlast[("z", zb)] = ("z2dma", nn, m)

        z2all_list = [("z2dma", 7, i) for i in range(NM)]
        E.set_fence([("fmm", 7, NG - 1), ("bnst2", groupsM[-1][-1], 7),
                     ("fwcast", 7, 15)] + dma_tail("sync") + dma_tail("gpsimd") + dma_tail("scalar"))

        T7b = 0
        O7b = T7b + 4 * 512 * 4
        T7, O7 = T7b // 4, O7b // 4
        assert O7b + 4 * 512 * 4 <= 64512
        for m in range(NM):
            op("vector",
               lambda e, m=m: e.bn_aggr(stats[:, ST_MV + m * 2:ST_MV + (m + 1) * 2],
                                        stats[:, ST_BN + m * 48:ST_BN + (m + 1) * 48]),
               waits=[("bnst2", m, n2) for n2 in range(8)] + [("rstd", NM - 1), ("nmr", NM - 1)],
               ms=("mv2", m))
            op("scalar",
               lambda e, m=m: e.activation(stats[:, ST_RSTD + m:ST_RSTD + m + 1],
                                           stats[:, ST_MV + m * 2 + 1:ST_MV + m * 2 + 2],
                                           AF.Ln, bias=stats[:, ST_EPS:ST_EPS + 1], scale=1.0),
               waits=[("mv2", m), ("epsv",)], ms=("lnv2", m))
            op("scalar",
               lambda e, m=m: e.activation(stats[:, ST_RSTD + m:ST_RSTD + m + 1],
                                           stats[:, ST_RSTD + m:ST_RSTD + m + 1],
                                           AF.Exp, bias=0.0, scale=-0.5),
               waits=[("lnv2", m)], ms=("rstd2", m))
            op("vector",
               lambda e, m=m: e.tensor_mul(stats[:, ST_NMR + m:ST_NMR + m + 1],
                                           stats[:, ST_MV + m * 2:ST_MV + m * 2 + 1],
                                           stats[:, ST_RSTD + m:ST_RSTD + m + 1]),
               waits=[("rstd2", m)], ms=("nmr2", m))
            for nn in range(8):
                bx = nn % 4
                w = list(z2all_list) + [("z2dma", 7, m)]
                if nn >= 4:
                    w.append(("t2n", m, nn - 4))
                elif m > 0:
                    w.append(("t2n", m - 1, nn + 4))
                dma("sync", Bf[:, T7 + bx * 512:T7 + (bx + 1) * 512],
                    z2_hbm[m][:, nn * 512:(nn + 1) * 512], waits=w, ms=("z2rd", m, nn))
                tb = nn % 4
                wn = [("z2rd", m, nn), ("nmr2", m)]
                if nn >= 4:
                    wn.append(("outdma", m, nn - 4))
                elif m > 0:
                    wn.append(("outdma", m - 1, nn + 4))
                op("vector",
                   lambda e, m=m, bx=bx, tb=tb: e.tensor_scalar(
                       Bf[:, O7 + tb * 512:O7 + (tb + 1) * 512],
                       Bf[:, T7 + bx * 512:T7 + (bx + 1) * 512],
                       stats[:, ST_RSTD + m:ST_RSTD + m + 1],
                       stats[:, ST_NMR + m:ST_NMR + m + 1],
                       ALU.mult, ALU.subtract),
                   waits=wn, ms=("t2n", m, nn))
                dma("gpsimd", out_d[m * 128:(m + 1) * 128, nn * 512:(nn + 1) * 512],
                    Bf[:, O7 + tb * 512:O7 + (tb + 1) * 512],
                    waits=[("t2n", m, nn)], ms=("outdma", m, nn))

        for i in range(QK["sync"]):
            E.final_wait("sync", f"s{i}")
        for i in range(QK["gpsimd"]):
            E.final_wait("gpsimd", f"g{i}")
        for i in range(QK["scalar"]):
            E.final_wait("scalar", f"c{i}")

    E0.begin_count()
    walk(0, E0)
    E1.begin_count()
    walk(1, E1)

    sems = {}
    with ExitStack() as semstack:
        for name in (["t", "v", "s", "p", "rr"] + [f"s{i}" for i in range(12)]
                     + [f"g{i}" for i in range(8)] + [f"c{i}" for i in range(8)]):
            sems[name] = semstack.enter_context(nc.semaphore(f"sem_{name}"))

        def section(engname):
            def body(eng):
                if engname == "sync":
                    eng.dma_start(out=rolesb[:, :], in_=role_d).then_inc(sems["rr"], 16)
                eng.wait_ge(sems["rr"], 16)
                with eng.register(f"role_{engname}") as r:
                    eng.reg_load(r, rolesb[0:1, 0:1])
                    for i in nc.Switch(engines=[eng], index=[eng.snap(r)], n=2):
                        EE = E0 if i == 0 else E1
                        EE.begin_emit(engname, eng, sems)
                        walk(i, EE)
            return body

        with nc.Block() as block:
            block.sync(section("sync"))
            block.gpsimd(section("gpsimd"))
            block.tensor(section("tensor"))
            block.vector(section("vector"))
            block.scalar(section("scalar"))
    es.close()
    return nc



def _rot_perm():
    p = np.arange(D)
    for h in range(NH):
        base = h * HD
        p[base:base + 64] = np.concatenate([np.arange(0, 64, 2), np.arange(1, 64, 2)]) + base
    return p


def _tables(idx, width):
    inv_freq = (1.0 / (10000.0 ** (np.arange(0, 64, 2, dtype=np.float32) / np.float32(64.0)))).astype(np.float32)
    t = idx.astype(np.float32)
    sinus = np.einsum("i,j->ij", t, inv_freq).astype(np.float32)
    sn = np.sin(sinus).astype(np.float32).T
    cs = np.cos(sinus).astype(np.float32).T
    out = np.zeros((128, width), np.float32)
    n = len(idx)
    out[0:32, 0:n] = cs
    out[32:64, 0:n] = cs
    out[64:96, 0:n] = sn
    out[96:128, 0:n] = sn
    return out.astype(BFNP)


def _in_maps(x, wq, wk, wv, wo, fw, fs):
    ident = np.eye(128, dtype=np.float32).astype(BFNP)
    mval = np.float32(NEGV / 16.0)
    tri = np.where(np.arange(128)[:, None] >= np.arange(128)[None, :],
                   np.float32(0.0), mval).astype(BFNP)
    NMXT = ACUT * 128
    maps = []
    for c in range(8):
        b, role = c // 2, c % 2
        if role == 0:
            myidx = np.arange(0, ACUT * 128)
        else:
            myidx = np.arange(ACUT * 128, S)
        xres = np.zeros((NMXT, D), np.float32)
        xres[0:len(myidx)] = x[b][myidx]
        maps.append({
            "xT": np.ascontiguousarray(x[b].T),
            "xres": xres,
            "wq": wq, "wk": wk, "wv": wv, "wo": wo, "fw": fw, "fs": fs,
            "rolein": np.array([[role]], dtype=np.uint32),
            "tabq": _tables(myidx, NMXT), "tabk": _tables(np.arange(S), S),
            "maska": tri, "ident": ident,
        })
    return maps


def kernel(L0_attn_hidden_states, L0_attn_Wq, L0_attn_Wk, L0_attn_Wv, L0_attn_Wo,
           L0_ffn_kernel, L0_ffn_scale):
    x = np.asarray(L0_attn_hidden_states)
    perm = _rot_perm()
    wq = np.ascontiguousarray(np.asarray(L0_attn_Wq)[:, perm])
    wk = np.ascontiguousarray(np.asarray(L0_attn_Wk)[:, perm])
    wv = np.ascontiguousarray(np.asarray(L0_attn_Wv))
    wo = np.ascontiguousarray(np.asarray(L0_attn_Wo))
    fw = np.ascontiguousarray(np.asarray(L0_ffn_kernel).view(np.uint8))
    fs = np.asarray(L0_ffn_scale, dtype=np.float32).reshape(1, 1)

    nc = build_nc()
    in_maps = _in_maps(x, wq, wk, wv, wo, fw, fs)
    res = run_bass_kernel_spmd(nc, in_maps, core_ids=list(range(8)))
    out = np.empty((4, S, D), np.float32)
    for c in range(8):
        b, role = c // 2, c % 2
        if role == 0:
            out[b, 0:ACUT * 128, :] = res.results[c]["out"][0:ACUT * 128]
        else:
            out[b, ACUT * 128:S, :] = res.results[c]["out"][0:S - ACUT * 128]
    return out
